# version 22
# baseline (speedup 1.0000x reference)
import numpy as np
import concourse.bass as bass
import concourse.mybir as mybir
from concourse.ap import AP
from concourse.bass_utils import run_bass_kernel_spmd

F32 = mybir.dt.float32
BF16 = mybir.dt.bfloat16
AF = mybir.ActivationFunctionType
ALU = mybir.AluOpType

NCORES = 8
TFULL = 16384
C = 1024
NCH = 8
T = TFULL // NCORES
TT = 512
NTILES = T // TT
H = 16
EPS_LN = 1e-5
EPS_AT = 1e-6

SGN = [1.0, 1.0, -1.0, -1.0, -1.0, -1.0, 1.0, 1.0]
SIG = lambda j: (j + 4) % 8
XBFN_SLOT = {6: 0, 7: 1, 0: 2, 1: 3}


class Sem:
    def __init__(self, nc, name):
        self.h = nc.alloc_semaphore(name)
        self.val = 0


class Builder:
    ENGS = ("sync", "gpsimd", "vector", "scalar", "tensor")

    def __init__(self):
        self.ops = {e: [] for e in self.ENGS}

    def op(self, eng, fn, inc=None, n=1):
        if inc is not None:
            inc.val += n
            def run(e, fn=fn, sem=inc.h, n=n):
                fn(e).then_inc(sem, n)
            self.ops[eng].append(run)
            return (inc, inc.val)
        self.ops[eng].append(lambda e, fn=fn: fn(e))
        return None

    def wait(self, eng, sem, val=None):
        v = sem.val if val is None else val
        if v <= 0:
            return
        self.ops[eng].append(lambda e, s=sem.h, v=v: e.wait_ge(s, v))

    def wait_t(self, eng, ticket):
        if ticket is not None:
            self.wait(eng, ticket[0], ticket[1])


def bcast3(tile_ap, reps, n):
    return AP(tile_ap.tensor, tile_ap.offset,
              [[tile_ap.ap[0][0], tile_ap.ap[0][1]], [0, reps], [1, n]])


def r3(ap2d, c, n):
    return ap2d.rearrange("p (c n) -> p c n", c=c, n=n)


def build_program(stage=3, dump=()):
    nc = bass.Bass("TRN2", target_bir_lowering=False, detect_race_conditions=False)

    x_d = nc.dram_tensor("x", [128, NCH * T], F32, kind="ExternalInput").ap()
    out_d = nc.dram_tensor("out", [128, NCH * T], F32, kind="ExternalOutput").ap()
    w1_d = nc.dram_tensor("w1", [128, 8192], F32, kind="ExternalInput").ap()
    w2_d = nc.dram_tensor("w2", [128, 4096], F32, kind="ExternalInput").ap()
    w3_d = nc.dram_tensor("w3", [128, 1024], F32, kind="ExternalInput").ap()
    w4_d = nc.dram_tensor("w4", [128, 2], F32, kind="ExternalInput").ap()
    gw1_d = nc.dram_tensor("gw1", [128, 4096], F32, kind="ExternalInput").ap()
    gw2_d = nc.dram_tensor("gw2", [128, 1024], F32, kind="ExternalInput").ap()
    gw3_d = nc.dram_tensor("gw3", [128, 2], F32, kind="ExternalInput").ap()
    wqkv_d = nc.dram_tensor("wqkv", [128, 24576], F32, kind="ExternalInput").ap()
    wproj_d = nc.dram_tensor("wproj", [128, 8192], F32, kind="ExternalInput").ap()
    biasc_d = nc.dram_tensor("biasc", [128, 16], F32, kind="ExternalInput").ap()
    brows_d = nc.dram_tensor("brows", [1, 2560], F32, kind="ExternalInput").ap()
    scal_d = nc.dram_tensor("scal", [1, 4], F32, kind="ExternalInput").ap()
    eall_d = nc.dram_tensor("eall", [16, 1024], F32, kind="ExternalInput").ap()
    ccg_in = nc.dram_tensor("ccg_in", [128, 8], F32, kind="Internal")
    ccg_out = nc.dram_tensor("ccg_out", [128, 8], F32, kind="Internal")
    cck_in = nc.dram_tensor("cck_in", [128, 520], F32, kind="Internal")
    cck_out = nc.dram_tensor("cck_out", [128, 520], F32, kind="Internal")
    ccs_in = nc.dram_tensor("ccs_in", [128, 8], F32, kind="Internal")
    ccs_out = nc.dram_tensor("ccs_out", [128, 8], F32, kind="Internal")

    base = nc.sbuf_base
    cur = [base]
    offs = {}

    def alloc(name, shape, dtype, at=None):
        nbytes = int(np.prod(shape[1:])) * (4 if dtype == F32 else 2)
        if at is None:
            o = (cur[0] + 63) & ~63
            cur[0] = o + nbytes
        else:
            o = at
        t = nc.alloc_sbuf_tensor_at(name, shape, dtype, offset=o)
        offs[name] = o
        return t

    XR = alloc("XR", [128, NCH * T], F32)
    TROW = alloc("TROW", [1, 3072], F32)
    SPROW = alloc("SPROW", [1, 2048], F32)
    CELLS = alloc("CELLS", [128, 16], F32)
    BIASC = alloc("BIASC", [128, 16], F32)
    ONESB = alloc("ONESB", [1, 512], BF16)
    ONESF = alloc("ONESF", [1, 128], F32)
    ONE1B = alloc("ONE1B", [1, 32], BF16)
    ONECB = alloc("ONECB", [128, 1], BF16)
    PERS_END = cur[0]

    cur[0] = PERS_END
    W1SB = alloc("W1SB", [128, 8192], BF16)
    W2SB = alloc("W2SB", [128, 4096], BF16)
    W3SB = alloc("W3SB", [128, 1024], BF16)
    W4SB = alloc("W4SB", [128, 2], BF16)
    BROWS = alloc("BROWS", [1, 2560], BF16)
    CBROW = alloc("CBROW", [1, 512], BF16)
    GCP = alloc("GCP", [128, 32], F32)
    GCN = alloc("GCN", [128, 8], F32)
    GCNB = alloc("GCNB", [128, 8], BF16)
    GH1 = alloc("GH1", [128, 4], BF16)
    GH2 = alloc("GH2", [128, 2], BF16)
    MROW = alloc("MROW", [1, 2048], F32)
    RROW = alloc("RROW", [1, 2048], F32)
    XBF = alloc("XBF", [128, 4096], BF16)
    XBFN = alloc("XBFN", [128, 2048], BF16)
    GPAIR = [alloc("G_a", [128, 4096], BF16), alloc("G_b", [128, 4096], BF16)]
    HPAIR = [alloc("H_a", [128, 4096], BF16), alloc("H_b", [128, 4096], BF16)]
    SCR = []
    for pi, par in enumerate(("a", "b")):
        s0 = (cur[0] + 63) & ~63
        d = {
            "H1G": alloc("H1G_" + par, [128, 4096], BF16, at=s0),
            "H2G": alloc("H2G_" + par, [128, 2048], BF16, at=s0 + 8192),
            "H3G": alloc("H3G_" + par, [128, 1024], BF16, at=s0 + 12288),
            "CBC": alloc("CBC_" + par, [128, 512], BF16, at=offs["XBFN"] + pi * 1024),
        }
        cur[0] = s0 + 14336
        SCR.append(d)
    GW1SB = alloc("GW1SB", [128, 4096], BF16, at=offs["H1G_a"])
    GW2SB = alloc("GW2SB", [128, 1024], BF16, at=offs["H2G_a"])
    GW3SB = alloc("GW3SB", [128, 2], BF16, at=offs["H3G_a"])
    QFIN = alloc("QFIN", [128, 4096], F32, at=offs["H1G_a"])
    TMP1 = alloc("TMP1", [128, 4096], BF16, at=offs["XBF"])
    FFN_END = cur[0]

    cur[0] = PERS_END
    WQKV = alloc("WQKV", [128, 24576], BF16)
    WPROJ = alloc("WPROJ", [128, 8192], BF16, at=offs["WQKV"])
    QPHI = alloc("QPHI", [128, NCH * T], BF16)
    XNBF = alloc("XNBF", [128, 4096], BF16)
    XNBF2 = alloc("XNBF2", [128, 4096], BF16)
    KTM = alloc("KTM", [128, 1024], BF16)
    VEXT = alloc("VEXT", [128, 1056], BF16)
    PHA = alloc("PHA", [128, 512], BF16)
    PHB = alloc("PHB", [128, 512], BF16)
    PHA2 = alloc("PHA2", [128, 512], BF16)
    PHB2 = alloc("PHB2", [128, 512], BF16)
    OUTPRE = alloc("OUTPRE", [128, 4096], BF16)
    KVSB = alloc("KVSB", [128, 520], F32)
    KVBF = alloc("KVBF", [128, 520], BF16)
    NRS = alloc("NRS", [16, 512], F32)
    REC16 = alloc("REC16", [16, 512], F32)
    EALL = alloc("EALL", [16, 1024], F32, at=offs["KTM"])
    KS8 = alloc("KS8", [128, 8], F32)
    KSF = alloc("KSF", [128, 128], BF16)
    ATTN_END = cur[0]

    cap = nc.sbuf_top
    assert FFN_END <= cap, (FFN_END, cap)
    assert ATTN_END <= cap, (ATTN_END, cap)

    pb = [nc.alloc_psum_tensor("pb%d" % i, [128, 512], F32) for i in range(8)]

    sd = Sem(nc, "sd")
    sg = Sem(nc, "sg")
    sv = Sem(nc, "sv")
    ss = Sem(nc, "ss")
    st = Sem(nc, "st")
    scc = Sem(nc, "scc")
    so = Sem(nc, "so")

    B = Builder()
    bank_ticket = [None] * 8

    def use_bank(i):
        B.wait_t("tensor", bank_ticket[i])

    def dbg(tag, ap, dtype=None):
        if tag not in dump:
            return
        dt = ap.tensor.dtype if dtype is None else dtype
        dd = nc.dram_tensor("dbg_" + tag, list(ap.shape), dt, kind="ExternalOutput").ap()
        for sem in (st, sv, ss):
            B.wait("sync", sem)
        B.op("sync", lambda e, dd=dd, ap=ap: e.dma_start(out=dd, in_=ap), sd, 16)
        for eng in ("vector", "scalar", "tensor"):
            B.wait(eng, sd)

    C_AR, C_B4, C_GB3, C_EPS, C_HB4, C_HGB3, C_K1, C_K2, C_C3, C_GTH, C_GF, C_C0 = range(12)

    def cell(i):
        return CELLS[0:1, i:i + 1]

    R_C, R_A, R_B, R_T1, R_T2, R_T3 = (0, 512, 1024, 1536, 2048, 2560)

    def trow(slot):
        return TROW[0:1, slot:slot + 512]

    def mm(out, lhsT, rhs, start, stop, inc=False):
        if inc:
            return B.op("tensor", lambda e: e.matmul(out=out, lhsT=lhsT, rhs=rhs,
                                                     start=start, stop=stop), st, 1)
        return B.op("tensor", lambda e: e.matmul(out=out, lhsT=lhsT, rhs=rhs,
                                                 start=start, stop=stop))

    def end_block():
        B.wait("gpsimd", so, so.val)
        B.wait("sync", so, so.val)
        with nc.Block() as block:
            for eng in Builder.ENGS:
                fns = B.ops[eng]
                def body(e, fns=fns):
                    for f in fns:
                        f(e)
                getattr(block, eng)(body)
        return nc

    def xo_of(k):
        return k * 4096

    def emit_store_xr():
        for k in range(NTILES):
            xo = xo_of(k)
            B.wait("sync", sv, sv.val)
            B.wait("sync", ss, ss.val)
            B.op("sync", lambda e, xo=xo: e.dma_start(
                out=AP(out_d.tensor, xo, [[NCH * T, 128], [1, 4096]]),
                in_=XR[:, xo:xo + 4096]), so, 16)

    B.op("sync", lambda e: e.dma_start(out=XR[:, :], in_=x_d), sd, 16)
    x_tk = (sd, sd.val)
    B.op("sync", lambda e: e.dma_start(out=BIASC[:, :], in_=biasc_d), sd, 16)
    B.op("sync", lambda e: e.dma_start(out=CELLS[0:1, 0:4], in_=scal_d), sd, 16)
    init_d = (sd, sd.val)
    B.op("gpsimd", lambda e: e.dma_start(out=W1SB[:, :], in_=w1_d), sg, 16)
    B.op("gpsimd", lambda e: e.dma_start(out=W2SB[:, :], in_=w2_d), sg, 16)
    B.op("gpsimd", lambda e: e.dma_start(out=W3SB[:, :], in_=w3_d), sg, 16)
    B.op("gpsimd", lambda e: e.dma_start(out=W4SB[:, :], in_=w4_d), sg, 16)
    B.op("gpsimd", lambda e: e.dma_start(out=BROWS[:, :], in_=brows_d), sg, 16)
    wffn_tk = (sg, sg.val)
    B.op("gpsimd", lambda e: e.dma_start(out=GW1SB[:, :], in_=gw1_d), sg, 16)
    B.op("gpsimd", lambda e: e.dma_start(out=GW2SB[:, :], in_=gw2_d), sg, 16)
    B.op("gpsimd", lambda e: e.dma_start(out=GW3SB[:, :], in_=gw3_d), sg, 16)
    gw_tk = {0: (sg, sg.val)}
    B.op("vector", lambda e: e.memset(ONESB[:, :], 1.0))
    B.op("vector", lambda e: e.memset(ONESF[:, :], 1.0))
    B.op("vector", lambda e: e.memset(ONE1B[:, :], 1.0))
    B.op("vector", lambda e: e.memset(ONECB[:, :], 1.0))
    B.op("vector", lambda e: e.memset(CELLS[:, C_EPS:C_EPS + 1], EPS_LN), sv, 1)
    B.wait("vector", sd, init_d[1])
    B.op("vector", lambda e: e.tensor_scalar(out=cell(C_HB4), in0=cell(C_B4),
                                             scalar1=0.5, scalar2=None, op0=ALU.mult))
    B.op("vector", lambda e: e.tensor_scalar(out=cell(C_HGB3), in0=cell(C_GB3),
                                             scalar1=0.5, scalar2=None, op0=ALU.mult))
    B.op("vector", lambda e: e.tensor_scalar(out=cell(C_K1), in0=cell(C_AR),
                                             scalar1=0.3, scalar2=None, op0=ALU.mult))
    B.op("vector", lambda e: e.tensor_scalar(out=cell(C_K2), in0=cell(C_AR),
                                             scalar1=1.4, scalar2=None, op0=ALU.mult))
    B.op("vector", lambda e: e.tensor_scalar(out=cell(C_C3), in0=cell(C_AR),
                                             scalar1=-0.7, scalar2=0.1,
                                             op0=ALU.mult, op1=ALU.add), sv, 1)
    cells_tk = (sv, sv.val)

    def emit_ffn(phase, w1_tk2=None):
        SQB = GPAIR[0]
        cp = [None] * (NTILES + 1)
        cp[0] = B.op("vector", lambda e: e.tensor_copy(
            out=XBF[:, :], in_=XR[:, 0:4096]), sv, 1)
        for k in range(NTILES):
            xo = xo_of(k)
            use_bank(6)
            B.wait("tensor", cp[k][0], cp[k][1])
            for ci in range(8):
                mm(pb[6][0:1, :], ONECB[:, :], XBF[:, ci * 512:(ci + 1) * 512],
                   ci == 0, ci == 7, inc=(ci == 7))
            s1_tk = (st, st.val)
            B.wait("scalar", st, s1_tk[1])
            B.wait("scalar", sv, sv.val)
            stk = B.op("scalar", lambda e: e.activation(
                out=SQB[:, :], in_=XBF[:, :], func=AF.Square), ss, 1)
            use_bank(7)
            B.wait("tensor", ss, stk[1])
            for ci in range(8):
                mm(pb[7][0:1, :], ONECB[:, :], SQB[:, ci * 512:(ci + 1) * 512],
                   ci == 0, ci == 7, inc=(ci == 7))
            s2_tk = (st, st.val)
            if k + 1 < NTILES:
                xo2 = xo_of(k + 1)
                B.wait("vector", st, s1_tk[1])
                B.wait("vector", ss, stk[1])
                cp[k + 1] = B.op("vector", lambda e, xo2=xo2: e.tensor_copy(
                    out=XBF[:, :], in_=XR[:, xo2:xo2 + 4096]), sv, 1)
            B.op("scalar", lambda e, k=k: e.mul(
                out=MROW[0:1, k * 512:(k + 1) * 512], in_=pb[6][0:1, :], mul=1.0 / C), ss, 1)
            B.op("scalar", lambda e: e.activation(
                out=trow(R_T1), in_=pb[6][0:1, :], func=AF.Square, scale=1.0 / C), ss, 1)
            msq_tk = (ss, ss.val)
            bank_ticket[6] = msq_tk
            B.wait("vector", ss, msq_tk[1])
            B.wait("vector", st, s2_tk[1])
            vv = B.op("vector", lambda e: e.scalar_tensor_tensor(
                out=trow(R_T2), in0=pb[7][0:1, :], scalar=1.0 / C, in1=trow(R_T1),
                op0=ALU.mult, op1=ALU.subtract), sv, 1)
            bank_ticket[7] = vv
            B.wait("scalar", sv, vv[1])
            B.op("scalar", lambda e: e.activation(
                out=trow(R_T3), in_=trow(R_T2), func=AF.Ln, bias=cell(C_EPS)), ss, 1)
            rtk = B.op("scalar", lambda e, k=k: e.activation(
                out=RROW[0:1, k * 512:(k + 1) * 512], in_=trow(R_T3), func=AF.Exp,
                scale=-0.5), ss, 1)
            use_bank(0)
            use_bank(1)
            B.wait("tensor", ss, rtk[1])
            mm(pb[0][:, :], ONESF[:, :], MROW[0:1, k * 512:(k + 1) * 512], True, True, inc=True)
            bc = mm(pb[1][:, :], ONESF[:, :], RROW[0:1, k * 512:(k + 1) * 512], True, True, inc=True)
            B.wait("vector", st, bc[1])
            B.op("vector", lambda e, xo=xo: e.tensor_sub(
                out=SQB[:, :], in0=XR[:, xo:xo + 4096], in1=bcast3(pb[0].ap(), 8, 512)), sv, 1)
            for ci in range(8):
                B.op("vector", lambda e, ci=ci, k=k: e.scalar_tensor_tensor(
                    out=SQB[:, ci * 512:(ci + 1) * 512],
                    in0=SQB[:, ci * 512:(ci + 1) * 512], scalar=1.0,
                    in1=pb[1][:, :], op0=ALU.bypass, op1=ALU.mult,
                    accum_out=GCP[:, k * 8 + ci:k * 8 + ci + 1]), sv, 1)
            gc_tk = (sv, sv.val)
            bank_ticket[0] = gc_tk
            bank_ticket[1] = gc_tk

        B.op("vector", lambda e: e.tensor_add(out=GCN[:, :], in0=GCP[:, 0:8], in1=GCP[:, 8:16]))
        B.op("vector", lambda e: e.tensor_add(out=GCN[:, :], in0=GCN[:, :], in1=GCP[:, 16:24]))
        B.op("vector", lambda e: e.tensor_add(out=GCN[:, :], in0=GCN[:, :], in1=GCP[:, 24:32]), sv, 1)
        B.wait("sync", sv, sv.val)
        B.op("sync", lambda e: e.dma_start(out=ccg_in.ap(), in_=GCN[:, :]), sd, 16)
        B.wait("gpsimd", sd, sd.val)
        B.op("gpsimd", lambda e: e.collective_compute(
            "AllReduce", ALU.add, replica_groups=[list(range(NCORES))],
            ins=[ccg_in.ap()], outs=[ccg_out.ap()]), scc, 1)
        B.wait("sync", scc, scc.val)
        B.op("sync", lambda e: e.dma_start(out=GCN[:, :], in_=ccg_out.ap()), sd, 16)
        ar_done = (sd, sd.val)

        def emit_prep(k, par):
            xo = xo_of(k)
            use_bank(0)
            use_bank(1)
            mm(pb[0][:, :], ONESF[:, :], MROW[0:1, k * 512:(k + 1) * 512], True, True, inc=True)
            bc = mm(pb[1][:, :], ONESF[:, :], RROW[0:1, k * 512:(k + 1) * 512], True, True, inc=True)
            B.wait("vector", st, bc[1])
            B.op("vector", lambda e, xo=xo: e.tensor_sub(
                out=XBF[:, :], in0=XR[:, xo:xo + 4096], in1=bcast3(pb[0].ap(), 8, 512)), sv, 1)
            o0 = B.op("vector", lambda e: e.tensor_tensor(
                out=r3(XBF[:, :], 8, 512), in0=r3(XBF[:, :], 8, 512),
                in1=bcast3(pb[1].ap(), 8, 512), op=ALU.mult), sv, 1)
            bank_ticket[0] = o0
            bank_ticket[1] = o0
            B.op("vector", lambda e: e.tensor_scalar(
                out=XBFN[:, 0:1024], in0=XBF[:, 6 * 512:8 * 512], scalar1=-1.0,
                scalar2=None, op0=ALU.mult))
            on = B.op("vector", lambda e: e.tensor_scalar(
                out=XBFN[:, 1024:2048], in0=XBF[:, 0:1024], scalar1=-1.0,
                scalar2=None, op0=ALU.mult), sv, 1)
            if k == 0:
                dbg("out0_p%d" % phase, XBF[:, :])
            return o0, on

        def emit_g0h0(k, par, o0, on):
            GT, HT = GPAIR[par], HPAIR[par]
            B.wait("tensor", sv, o0[1])
            if phase == 1 and k == 0:
                B.wait("tensor", w1_tk2[0], w1_tk2[1])
            for oj in range(8):
                bk = 2 + (oj % 2)
                use_bank(bk)
                for ci in range(8):
                    mm(pb[bk][:, :],
                       W1SB[:, ci * 1024 + oj * 128: ci * 1024 + (oj + 1) * 128],
                       XBF[:, ci * 512:(ci + 1) * 512], ci == 0, ci == 7, inc=(ci == 7))
                mtk = (st, st.val)
                B.wait("scalar", st, mtk[1])
                etk = B.op("scalar", lambda e, oj=oj, bk=bk, GT=GT: e.copy(
                    out=GT[:, oj * 512:(oj + 1) * 512], in_=pb[bk][:, :]), ss, 1)
                bank_ticket[bk] = etk
            B.wait("tensor", sv, on[1])
            for oj in range(8):
                bk = 2 + (oj % 2)
                use_bank(bk)
                for ci in range(8):
                    if SGN[ci] > 0:
                        rhs = XBF[:, SIG(ci) * 512:(SIG(ci) + 1) * 512]
                    else:
                        sl = XBFN_SLOT[SIG(ci)]
                        rhs = XBFN[:, sl * 512:(sl + 1) * 512]
                    mm(pb[bk][:, :],
                       W1SB[:, ci * 1024 + oj * 128: ci * 1024 + (oj + 1) * 128],
                       rhs, ci == 0, ci == 7, inc=(ci == 7))
                mtk = (st, st.val)
                B.wait("vector", st, mtk[1])
                etk = B.op("vector", lambda e, oj=oj, bk=bk, HT=HT: e.tensor_copy(
                    out=HT[:, oj * 512:(oj + 1) * 512], in_=pb[bk][:, :]), sv, 1)
                bank_ticket[bk] = etk
            if k == 0:
                dbg("G0_p%d" % phase, GT[:, :])
                dbg("H0_p%d" % phase, HT[:, :])

        def emit_gf_net():
            B.wait("vector", ar_done[0], ar_done[1])
            gcn_tk = B.op("vector", lambda e: e.tensor_scalar(
                out=GCNB[:, :], in0=GCN[:, :], scalar1=1.0 / TFULL, scalar2=None,
                op0=ALU.mult), sv, 1)
            use_bank(6)
            B.wait("tensor", sv, gcn_tk[1])
            B.wait("tensor", gw_tk[phase][0], gw_tk[phase][1])
            for oj in range(4):
                for ci in range(8):
                    mm(pb[6][:, oj:oj + 1],
                       GW1SB[:, ci * 512 + oj * 128: ci * 512 + (oj + 1) * 128],
                       GCNB[:, ci:ci + 1], ci == 0, False)
                mm(pb[6][:, oj:oj + 1], BROWS[0:1, 1792 + oj * 128:1792 + (oj + 1) * 128],
                   ONE1B[0:1, 0:1], False, True, inc=True)
            g1_tk = (st, st.val)
            B.wait("scalar", st, g1_tk[1])
            e1 = B.op("scalar", lambda e: e.activation(out=GH1[:, :], in_=pb[6][:, 0:4],
                                                       func=AF.Gelu), ss, 1)
            bank_ticket[6] = e1
            use_bank(7)
            B.wait("tensor", ss, e1[1])
            for oj in range(2):
                for ci in range(4):
                    mm(pb[7][:, oj:oj + 1],
                       GW2SB[:, ci * 256 + oj * 128: ci * 256 + (oj + 1) * 128],
                       GH1[:, ci:ci + 1], ci == 0, False)
                mm(pb[7][:, oj:oj + 1], BROWS[0:1, 2304 + oj * 128:2304 + (oj + 1) * 128],
                   ONE1B[0:1, 0:1], False, True, inc=True)
            g2_tk = (st, st.val)
            B.wait("scalar", st, g2_tk[1])
            e2 = B.op("scalar", lambda e: e.activation(out=GH2[:, :], in_=pb[7][:, 0:2],
                                                       func=AF.Gelu), ss, 1)
            bank_ticket[7] = e2
            use_bank(4)
            B.wait("tensor", ss, e2[1])
            for ci in range(2):
                mm(pb[4][0:1, 0:1], GW3SB[:, ci:ci + 1], GH2[:, ci:ci + 1],
                   ci == 0, ci == 1, inc=(ci == 1))
            g3_tk = (st, st.val)
            B.wait("scalar", st, g3_tk[1])
            e3 = B.op("scalar", lambda e: e.activation(
                out=cell(C_GTH), in_=pb[4][0:1, 0:1], func=AF.Tanh, scale=0.5,
                bias=cell(C_HGB3)), ss, 1)
            bank_ticket[4] = e3
            B.wait("vector", ss, e3[1])
            B.op("vector", lambda e: e.tensor_scalar(
                out=cell(C_GF), in0=cell(C_GTH), scalar1=0.5, scalar2=0.5,
                op0=ALU.mult, op1=ALU.add))
            cz = B.op("vector", lambda e: e.tensor_scalar(
                out=cell(C_C0), in0=cell(C_GF), scalar1=cell(C_K2), scalar2=cell(C_C3),
                op0=ALU.mult, op1=ALU.add), sv, 1)
            return cz

        RS = [{"c": R_C, "A": R_A, "B": R_B}, {"c": R_T1, "A": R_T2, "B": R_T3}]

        def rowp(par, nm):
            return trow(RS[par][nm])

        last_fin = [None]

        def emit_cycles(k, par, cz_tk):
            GT, HT = GPAIR[par], HPAIR[par]
            scr = SCR[par]
            upd_tk = [None] * 4
            for cyc in range(3):
                if cyc == 0 and cz_tk is not None:
                    B.wait("scalar", cz_tk[0], cz_tk[1])
                for g2 in range(4):
                    c0_, c1_ = 2 * g2, 2 * g2 + 1
                    if cyc == 0 and g2 == 0 and last_fin[0] is not None:
                        B.wait("scalar", last_fin[0][0], last_fin[0][1])
                    if upd_tk[g2] is not None:
                        B.wait("scalar", sv, upd_tk[g2][1])
                    for ci in (c0_, c1_):
                        B.op("scalar", lambda e, ci=ci, GT=GT, scr=scr: e.activation(
                            out=scr["H1G"][:, ci * 512:(ci + 1) * 512],
                            in_=GT[:, ci * 512:(ci + 1) * 512], func=AF.Gelu,
                            bias=BIASC[:, 8 + ci:9 + ci]),
                            ss if ci == c1_ else None, 1)
                    g1tk = (ss, ss.val)
                    B.wait("tensor", ss, g1tk[1])
                    if g2 == 0:
                        for bk in range(4):
                            use_bank(bk)
                    for ci in (c0_, c1_):
                        for oj in range(4):
                            mm(pb[oj][:, :],
                               W2SB[:, ci * 512 + oj * 128: ci * 512 + (oj + 1) * 128],
                               scr["H1G"][:, ci * 512:(ci + 1) * 512],
                               ci == 0, ci == 7 and oj == 3, inc=(ci == 7 and oj == 3))
                w2tk = (st, st.val)
                B.wait("scalar", st, w2tk[1])
                for oj in range(4):
                    etk = B.op("scalar", lambda e, oj=oj, scr=scr: e.activation(
                        out=scr["H2G"][:, oj * 512:(oj + 1) * 512], in_=pb[oj][:, :],
                        func=AF.Gelu), ss, 1)
                    bank_ticket[oj] = etk
                h2tk = (ss, ss.val)
                use_bank(0)
                use_bank(1)
                B.wait("tensor", ss, h2tk[1])
                for ci in range(4):
                    for oj in range(2):
                        mm(pb[oj][:, :],
                           W3SB[:, ci * 256 + oj * 128: ci * 256 + (oj + 1) * 128],
                           scr["H2G"][:, ci * 512:(ci + 1) * 512],
                           ci == 0, ci == 3 and oj == 1, inc=(ci == 3 and oj == 1))
                w3tk = (st, st.val)
                B.wait("scalar", st, w3tk[1])
                for oj in range(2):
                    etk = B.op("scalar", lambda e, oj=oj, scr=scr: e.activation(
                        out=scr["H3G"][:, oj * 512:(oj + 1) * 512], in_=pb[oj][:, :],
                        func=AF.Gelu), ss, 1)
                    bank_ticket[oj] = etk
                h3tk = (ss, ss.val)
                use_bank(2)
                B.wait("tensor", ss, h3tk[1])
                for ci in range(2):
                    mm(pb[2][0:1, :], W4SB[:, ci:ci + 1],
                       scr["H3G"][:, ci * 512:(ci + 1) * 512], ci == 0, ci == 1,
                       inc=(ci == 1))
                ztk = (st, st.val)
                B.wait("scalar", st, ztk[1])
                thtk = B.op("scalar", lambda e, par=par: e.activation(
                    out=rowp(par, "c"), in_=pb[2][0:1, :], func=AF.Tanh, scale=0.5,
                    bias=cell(C_HB4)), ss, 1)
                bank_ticket[2] = thtk
                B.wait("vector", ss, thtk[1])
                if cyc == 0 and cz_tk is not None:
                    B.wait("vector", cz_tk[0], cz_tk[1])
                B.op("vector", lambda e, par=par: e.tensor_scalar(
                    out=rowp(par, "c"), in0=rowp(par, "c"), scalar1=cell(C_K1),
                    scalar2=cell(C_C0), op0=ALU.mult, op1=ALU.add), sv, 1)
                if cyc == 0:
                    B.op("vector", lambda e, par=par: e.tensor_copy(
                        out=rowp(par, "B"), in_=rowp(par, "c")))
                    B.op("vector", lambda e, par=par: e.memset(rowp(par, "A"), 1.0), sv, 1)
                else:
                    B.op("vector", lambda e, par=par: e.tensor_mul(
                        out=SPROW[0:1, par * 512:(par + 1) * 512], in0=rowp(par, "c"),
                        in1=rowp(par, "B")))
                    B.op("vector", lambda e, par=par: e.tensor_mul(
                        out=SPROW[0:1, 1024 + par * 512:1024 + (par + 1) * 512],
                        in0=rowp(par, "c"), in1=rowp(par, "A")))
                    B.op("vector", lambda e, par=par: e.tensor_sub(
                        out=rowp(par, "A"), in0=rowp(par, "A"),
                        in1=SPROW[0:1, par * 512:(par + 1) * 512]))
                    B.op("vector", lambda e, par=par: e.tensor_add(
                        out=rowp(par, "B"), in0=rowp(par, "B"),
                        in1=SPROW[0:1, 1024 + par * 512:1024 + (par + 1) * 512]), sv, 1)
                if cyc < 2:
                    cbtk = B.op("vector", lambda e, par=par: e.tensor_copy(
                        out=CBROW[:, :], in_=rowp(par, "c")), sv, 1)
                    cbb = 6 if cyc == 0 else 7
                    use_bank(cbb)
                    B.wait("tensor", sv, cbtk[1])
                    bmk = mm(pb[cbb][:, :], ONESB[0:1, 0:128], CBROW[:, :], True, True, inc=True)
                    B.wait("vector", st, bmk[1])
                    for g2 in range(4):
                        sl = slice(g2 * 1024, (g2 + 1) * 1024)
                        B.op("vector", lambda e, sl=sl, cbb=cbb, HT=HT: e.tensor_tensor(
                            out=r3(TMP1[:, sl], 2, 512), in0=r3(HT[:, sl], 2, 512),
                            in1=bcast3(pb[cbb].ap(), 2, 512), op=ALU.mult), sv, 1)
                        if cyc == 0:
                            B.op("vector", lambda e, sl=sl, cbb=cbb, scr=scr, GT=GT: e.tensor_tensor(
                                out=r3(scr["H1G"][:, sl], 2, 512), in0=r3(GT[:, sl], 2, 512),
                                in1=bcast3(pb[cbb].ap(), 2, 512), op=ALU.mult))
                            B.op("vector", lambda e, sl=sl, scr=scr, HT=HT: e.tensor_sub(
                                out=HT[:, sl], in0=HT[:, sl], in1=scr["H1G"][:, sl]))
                        gadd = B.op("vector", lambda e, sl=sl, GT=GT: e.tensor_add(
                            out=GT[:, sl], in0=GT[:, sl], in1=TMP1[:, sl]), sv, 1)
                        upd_tk[g2] = gadd
                    bank_ticket[cbb] = upd_tk[3]

        def emit_final(k, par):
            xo = xo_of(k)
            if k == 0:
                dbg("mr_p%d" % phase, MROW[0:1, :])
                dbg("rr_p%d" % phase, RROW[0:1, :])
            rr_ = RROW[0:1, k * 512:(k + 1) * 512]
            mr_ = MROW[0:1, k * 512:(k + 1) * 512]
            B.op("vector", lambda e, par=par, rr_=rr_: e.scalar_tensor_tensor(
                out=rowp(par, "A"), in0=rowp(par, "A"), scalar=0.5, in1=rr_,
                op0=ALU.mult, op1=ALU.mult))
            B.op("vector", lambda e, par=par, rr_=rr_: e.scalar_tensor_tensor(
                out=rowp(par, "B"), in0=rowp(par, "B"), scalar=0.5, in1=rr_,
                op0=ALU.mult, op1=ALU.mult))
            B.op("vector", lambda e, par=par, mr_=mr_: e.tensor_mul(
                out=rowp(par, "c"), in0=rowp(par, "A"), in1=mr_))
            B.op("vector", lambda e, par=par, mr_=mr_: e.tensor_mul(
                out=SPROW[0:1, 0:512], in0=rowp(par, "B"), in1=mr_))
            B.op("vector", lambda e, par=par: e.tensor_add(
                out=SPROW[0:1, 512:1024], in0=rowp(par, "c"), in1=SPROW[0:1, 0:512]))
            B.op("vector", lambda e, par=par: e.tensor_sub(
                out=rowp(par, "c"), in0=rowp(par, "c"), in1=SPROW[0:1, 0:512]))
            rows_tk = B.op("vector", lambda e, par=par: e.tensor_scalar(
                out=rowp(par, "A"), in0=rowp(par, "A"), scalar1=1.0, scalar2=None,
                op0=ALU.add), sv, 1)
            use_bank(6)
            use_bank(7)
            use_bank(4)
            use_bank(5)
            B.wait("tensor", sv, rows_tk[1])
            mm(pb[6][:, :], ONESF[:, :], rowp(par, "A"), True, True, inc=True)
            mm(pb[7][:, :], ONESF[:, :], rowp(par, "B"), True, True, inc=True)
            mm(pb[4][:, :], ONESF[:, :], SPROW[0:1, 512:1024], True, True, inc=True)
            fb = mm(pb[5][:, :], ONESF[:, :], rowp(par, "c"), True, True, inc=True)
            B.wait("vector", st, fb[1])
            B.op("vector", lambda e, xo=xo: e.tensor_tensor(
                out=r3(QFIN[:, 0:2048], 4, 512), in0=r3(XR[:, xo + 2048:xo + 4096], 4, 512),
                in1=bcast3(pb[7].ap(), 4, 512), op=ALU.mult))
            B.op("vector", lambda e, xo=xo: e.tensor_tensor(
                out=r3(QFIN[:, 2048:4096], 4, 512), in0=r3(XR[:, xo:xo + 2048], 4, 512),
                in1=bcast3(pb[7].ap(), 4, 512), op=ALU.mult), sv, 1)
            B.op("vector", lambda e, xo=xo: e.tensor_tensor(
                out=r3(XR[:, xo:xo + 4096], 8, 512), in0=r3(XR[:, xo:xo + 4096], 8, 512),
                in1=bcast3(pb[6].ap(), 8, 512), op=ALU.mult), sv, 1)
            for rq in range(4):
                opx = ALU.add if SGN[2 * rq] > 0 else ALU.subtract
                B.op("vector", lambda e, rq=rq, opx=opx, xo=xo: e.tensor_tensor(
                    out=XR[:, xo + rq * 1024: xo + (rq + 1) * 1024],
                    in0=XR[:, xo + rq * 1024: xo + (rq + 1) * 1024],
                    in1=QFIN[:, rq * 1024:(rq + 1) * 1024], op=opx), sv, 1)
            for rq in range(4):
                rbank = pb[4] if SGN[2 * rq] > 0 else pb[5]
                B.op("vector", lambda e, rq=rq, rbank=rbank, xo=xo: e.tensor_sub(
                    out=XR[:, xo + rq * 1024: xo + (rq + 1) * 1024],
                    in0=XR[:, xo + rq * 1024: xo + (rq + 1) * 1024],
                    in1=bcast3(rbank.ap(), 2, 512)), sv, 1)
            fin_tk = (sv, sv.val)
            last_fin[0] = fin_tk
            for bknum in (6, 7, 4, 5):
                bank_ticket[bknum] = fin_tk
            if phase == 1:
                B.wait("sync", sv, fin_tk[1])
                B.op("sync", lambda e, xo=xo: e.dma_start(
                    out=AP(out_d.tensor, xo, [[NCH * T, 128], [1, 4096]]),
                    in_=XR[:, xo:xo + 4096]), so, 16)

        o0 = emit_prep(0, 0)
        emit_g0h0(0, 0, *o0)
        cz_tk = emit_gf_net()
        for k in range(NTILES):
            emit_cycles(k, k % 2, cz_tk)
            if k + 1 < NTILES:
                o0n = emit_prep(k + 1, (k + 1) % 2)
                emit_g0h0(k + 1, (k + 1) % 2, *o0n)
            emit_final(k, k % 2)

    def emit_attn(wqkv_tk):
        B.op("vector", lambda e: e.memset(VEXT[:, :], 0.0))
        for h in range(H):
            B.op("vector", lambda e, h=h: e.memset(VEXT[:, h * 66 + 64:h * 66 + 65], 1.0))
        B.op("vector", lambda e: e.memset(KSF[:, :], 0.0), sv, 1)
        vext_tk = (sv, sv.val)
        ksf_zero_tk = vext_tk

        kv_first = [True]
        XNP = [XNBF, XNBF2]
        PHP = [(PHA, PHB), (PHA2, PHB2)]
        xn_tks = [None] * NTILES
        ph_war = [0, 0]
        ln_st = {}

        def ln_a(k, half):
            xo = xo_of(k)
            XN = XNP[k % 2]
            B.wait("vector", st, st.val)
            cp = B.op("vector", lambda e, xo=xo, XN=XN, half=half: e.tensor_copy(
                out=XN[:, half * 2048:(half + 1) * 2048],
                in_=XR[:, xo + half * 2048: xo + (half + 1) * 2048]), sv, 1)
            if half == 0:
                ln_st[k] = {}
            ln_st[k]["cp%d" % half] = cp

        def ln_b(k):
            XN = XNP[k % 2]
            stt = ln_st[k]
            use_bank(6)
            B.wait("tensor", sv, stt["cp1"][1])
            for ci in range(8):
                mm(pb[6][0:1, :], ONECB[:, :], XN[:, ci * 512:(ci + 1) * 512],
                   ci == 0, ci == 7, inc=(ci == 7))
            stt["s1"] = (st, st.val)
            B.wait("scalar", sv, stt["cp1"][1])
            B.wait("scalar", st, stt["s1"][1])
            stt["sq"] = B.op("scalar", lambda e, XN=XN: e.activation(
                out=OUTPRE[:, :], in_=XN[:, :], func=AF.Square), ss, 1)

        def ln_c(k):
            stt = ln_st[k]
            use_bank(7)
            B.wait("tensor", ss, stt["sq"][1])
            for ci in range(8):
                mm(pb[7][0:1, :], ONECB[:, :], OUTPRE[:, ci * 512:(ci + 1) * 512],
                   ci == 0, ci == 7, inc=(ci == 7))
            s2 = (st, st.val)
            B.op("scalar", lambda e: e.mul(out=trow(R_T1), in_=pb[6][0:1, :], mul=1.0 / C), ss, 1)
            msq = B.op("scalar", lambda e: e.activation(
                out=trow(R_T3), in_=pb[6][0:1, :], func=AF.Square, scale=1.0 / C), ss, 1)
            bank_ticket[6] = msq
            B.wait("vector", ss, msq[1])
            B.wait("vector", st, s2[1])
            vv = B.op("vector", lambda e: e.scalar_tensor_tensor(
                out=trow(R_C), in0=pb[7][0:1, :], scalar=1.0 / C, in1=trow(R_T3),
                op0=ALU.mult, op1=ALU.subtract), sv, 1)
            bank_ticket[7] = vv
            B.wait("scalar", sv, vv[1])
            B.op("scalar", lambda e: e.activation(
                out=trow(R_C), in_=trow(R_C), func=AF.Ln, bias=cell(C_EPS)), ss, 1)
            rtk = B.op("scalar", lambda e: e.activation(
                out=trow(R_T2), in_=trow(R_C), func=AF.Exp, scale=-0.5), ss, 1)
            use_bank(6)
            use_bank(7)
            B.wait("tensor", ss, rtk[1])
            mm(pb[6][:, :], ONESF[:, :], trow(R_T1), True, True, inc=True)
            bc = mm(pb[7][:, :], ONESF[:, :], trow(R_T2), True, True, inc=True)
            stt["bc"] = (st, st.val)

        def ln_d(k):
            xo = xo_of(k)
            XN = XNP[k % 2]
            stt = ln_st[k]
            B.wait("vector", st, stt["bc"][1])
            B.op("vector", lambda e, xo=xo, XN=XN: e.tensor_sub(
                out=r3(XN[:, :], 8, 512), in0=r3(XR[:, xo:xo + 4096], 8, 512),
                in1=bcast3(pb[6].ap(), 8, 512)), sv, 1)
            xn = B.op("vector", lambda e, XN=XN: e.tensor_tensor(
                out=r3(XN[:, :], 8, 512), in0=r3(XN[:, :], 8, 512),
                in1=bcast3(pb[7].ap(), 8, 512), op=ALU.mult), sv, 1)
            bank_ticket[6] = xn
            bank_ticket[7] = xn
            return xn

        ln_a(0, 0)
        ln_a(0, 1)
        ln_b(0)
        ln_c(0)
        xn_tks[0] = ln_d(0)
        for k in range(NTILES):
            XN = XNP[k % 2]
            B.wait("tensor", sv, xn_tks[k][1])
            if k == 0:
                B.wait("tensor", wqkv_tk[0], wqkv_tk[1])
            for oj in range(8):
                bk = oj % 4
                use_bank(bk)
                for ci in range(8):
                    mm(pb[bk][:, :],
                       WQKV[:, ci * 3072 + oj * 128: ci * 3072 + (oj + 1) * 128],
                       XN[:, ci * 512:(ci + 1) * 512], ci == 0, ci == 7, inc=(ci == 7))
                mtk = (st, st.val)
                pa, pbb = PHP[oj % 2]
                B.wait("scalar", st, mtk[1])
                B.wait("scalar", sv, ph_war[oj % 2])
                B.op("scalar", lambda e, bk=bk, pa=pa: e.activation(
                    out=pa[:, :], in_=pb[bk][:, :], func=AF.Relu, scale=-1.0), ss, 1)
                e_tk = B.op("scalar", lambda e, pa=pa, pbb=pbb: e.activation(
                    out=pbb[:, :], in_=pa[:, :], func=AF.Exp, scale=-1.0), ss, 1)
                B.wait("vector", st, mtk[1])
                B.wait("vector", ss, e_tk[1])
                r1 = B.op("vector", lambda e, bk=bk, pa=pa: e.tensor_scalar(
                    out=pa[:, :], in0=pb[bk][:, :], scalar1=0.0, scalar2=None,
                    op0=ALU.max), sv, 1)
                bank_ticket[bk] = r1
                ad = B.op("vector", lambda e, oj=oj, k=k, pa=pa, pbb=pbb: e.tensor_add(
                    out=QPHI[:, k * 4096 + oj * 512: k * 4096 + (oj + 1) * 512],
                    in0=pbb[:, :], in1=pa[:, :]), sv, 1)
                ph_war[oj % 2] = ad[1]
            if k == 0:
                dbg("qphi_a", QPHI[:, 0:4096])

            for sub in range(4):
                for bk in range(4):
                    use_bank(bk)
                if kv_first[0]:
                    use_bank(4)
                    use_bank(5)
                for sl in range(4):
                    eoff = 1024 + sl * 512
                    for ci in range(8):
                        mm(pb[sl][:, :],
                           XN[:, ci * 512 + sub * 128: ci * 512 + (sub + 1) * 128],
                           WQKV[:, ci * 3072 + eoff: ci * 3072 + eoff + 512],
                           ci == 0, ci == 7, inc=(ci == 7))
                kv_tk = (st, st.val)
                for sl in range(2):
                    pa, pbb = PHP[sl]
                    B.wait("scalar", st, kv_tk[1] - 3 + sl)
                    B.wait("scalar", sv, ph_war[sl])
                    B.op("scalar", lambda e, sl=sl, pa=pa: e.activation(
                        out=pa[:, :], in_=pb[sl][:, :], func=AF.Relu, scale=-1.0), ss, 1)
                    e_tk = B.op("scalar", lambda e, pa=pa, pbb=pbb: e.activation(
                        out=pbb[:, :], in_=pa[:, :], func=AF.Exp, scale=-1.0), ss, 1)
                    B.wait("vector", st, kv_tk[1] - 3 + sl)
                    B.wait("vector", ss, e_tk[1])
                    r1 = B.op("vector", lambda e, sl=sl, pa=pa: e.tensor_scalar(
                        out=pa[:, :], in0=pb[sl][:, :], scalar1=0.0, scalar2=None,
                        op0=ALU.max), sv, 1)
                    bank_ticket[sl] = r1
                    ad = B.op("vector", lambda e, sl=sl, pa=pa, pbb=pbb: e.tensor_add(
                        out=KTM[:, sl * 512:(sl + 1) * 512], in0=pbb[:, :], in1=pa[:, :]), sv, 1)
                    ph_war[sl] = ad[1]
                for sl in range(2):
                    B.wait("vector", st, kv_tk[1] - 1 + sl)
                    vtk2 = B.op("vector", lambda e, sl=sl: e.tensor_copy(
                        out=AP(VEXT, sl * 8 * 66, [[1056, 128], [66, 8], [1, 64]]),
                        in_=r3(pb[2 + sl][:, :], 8, 64)), sv, 1)
                    bank_ticket[2 + sl] = vtk2
                kv_ready = (sv, sv.val)
                if k + 1 < NTILES:
                    if sub == 0:
                        ln_a(k + 1, 0)
                    elif sub == 1:
                        ln_a(k + 1, 1)
                    elif sub == 2:
                        ln_b(k + 1)
                    else:
                        ln_c(k + 1)
                        xn_tks[k + 1] = ln_d(k + 1)
                B.wait("tensor", sv, kv_ready[1])
                first = kv_first[0]
                kv_first[0] = False
                last = (k == NTILES - 1 and sub == 3)
                for h in range(H):
                    p2 = h // 2
                    bk = 4 + p2 // 4
                    cb = (p2 % 4) * 65
                    pbase = (h % 2) * 64
                    mm(pb[bk][pbase:pbase + 64, cb:cb + 65],
                       KTM[:, h * 64:(h + 1) * 64],
                       VEXT[:, h * 66:h * 66 + 65],
                       first and (h % 8 == 0), last, inc=(h == H - 1))
        kvdone_tk = (st, st.val)

        B.wait("sync", st, kvdone_tk[1])
        B.op("sync", lambda e: e.dma_start(out=EALL[:, :], in_=eall_d), sd, 16)
        eall_tk = (sd, sd.val)
        B.wait("vector", st, kvdone_tk[1])
        for b in range(2):
            B.op("vector", lambda e, b=b: e.tensor_copy(
                out=KVSB[:, b * 260:(b + 1) * 260], in_=pb[4 + b][:, 0:260]), sv, 1)
        kvsb_tk = (sv, sv.val)
        bank_ticket[4] = kvsb_tk
        bank_ticket[5] = kvsb_tk
        B.wait("sync", sv, kvsb_tk[1])
        B.op("sync", lambda e: e.dma_start(out=cck_in.ap(), in_=KVSB[:, :]), sd, 16)
        B.wait("gpsimd", sd, sd.val)
        B.op("gpsimd", lambda e: e.collective_compute(
            "AllReduce", ALU.add, replica_groups=[list(range(NCORES))],
            ins=[cck_in.ap()], outs=[cck_out.ap()]), scc, 1)
        kv_ar = (scc, scc.val)
        B.wait("gpsimd", st, kvdone_tk[1])
        B.op("gpsimd", lambda e: e.dma_start(out=WPROJ[:, :], in_=wproj_d), sg, 16)
        wproj_tk = (sg, sg.val)
        B.wait("gpsimd", kv_ar[0], kv_ar[1])
        B.wait("gpsimd", ksf_zero_tk[0], ksf_zero_tk[1])
        B.op("gpsimd", lambda e: e.dma_start(
            out=AP(KSF, 0, [[128, 64], [18, 8], [1, 1]]),
            in_=AP(cck_out, 64, [[520, 64], [65, 8], [1, 1]]),
            allow_slow_non_contiguous=True), sg, 16)
        B.op("gpsimd", lambda e: e.dma_start(
            out=AP(KSF, 64 * 128 + 1, [[128, 64], [18, 8], [1, 1]]),
            in_=AP(cck_out, 64 * 520 + 64, [[520, 64], [65, 8], [1, 1]]),
            allow_slow_non_contiguous=True), sg, 16)
        ksf_tk = (sg, sg.val)
        B.op("gpsimd", lambda e: e.dma_start(out=KVBF[:, :], in_=cck_out.ap()), sg, 16)
        kvpost_tk = (sg, sg.val)
        dbg("kvsb_a", KVSB[:, :])

        B.wait("tensor", ksf_tk[0], ksf_tk[1])
        B.wait("tensor", eall_tk[0], eall_tk[1])
        kv_waited = [False]
        for k in range(NTILES):
            xo = xo_of(k)
            use_bank(6)
            for ci in range(8):
                mm(pb[6][0:16, :], KSF[:, ci * 16:(ci + 1) * 16],
                   QPHI[:, k * 4096 + ci * 512: k * 4096 + (ci + 1) * 512],
                   ci == 0, ci == 7, inc=(ci == 7))
            n_tk = (st, st.val)
            B.wait("vector", st, n_tk[1])
            nv = B.op("vector", lambda e: e.tensor_scalar(
                out=NRS[:, :], in0=pb[6][0:16, :], scalar1=EPS_AT, scalar2=None,
                op0=ALU.add), sv, 1)
            bank_ticket[6] = nv
            B.wait("scalar", sv, nv[1])
            B.op("scalar", lambda e: e.activation(
                out=REC16[:, :], in_=NRS[:, :], func=AF.Ln), ss, 1)
            rec_tk = B.op("scalar", lambda e: e.activation(
                out=REC16[:, :], in_=REC16[:, :], func=AF.Exp, scale=-1.0), ss, 1)
            B.wait("tensor", ss, rec_tk[1])
            if not kv_waited[0]:
                kv_waited[0] = True
                B.wait("tensor", kvpost_tk[0], kvpost_tk[1])
                B.wait("tensor", wproj_tk[0], wproj_tk[1])
            for j in range(8):
                bko = j % 2
                bkr = 2 + (j % 2)
                use_bank(bko)
                use_bank(bkr)
                mm(pb[bko][0:64, :], KVBF[0:64, j * 65:j * 65 + 64],
                   QPHI[0:64, k * 4096 + j * 512: k * 4096 + (j + 1) * 512], True, True)
                mm(pb[bko][64:128, :], KVBF[64:128, j * 65:j * 65 + 64],
                   QPHI[64:128, k * 4096 + j * 512: k * 4096 + (j + 1) * 512],
                   True, True, inc=True)
                ro = mm(pb[bkr][:, :], EALL[:, j * 128:(j + 1) * 128], REC16[:, :],
                        True, True, inc=True)
                rb = PHA if (j % 2 == 0) else PHB
                B.wait("scalar", st, ro[1])
                B.wait("scalar", sv, sv.val)
                rc = B.op("scalar", lambda e, bkr=bkr, rb=rb: e.copy(
                    out=rb[:, :], in_=pb[bkr][:, :]), ss, 1)
                bank_ticket[bkr] = rc
                B.wait("vector", st, ro[1])
                B.wait("vector", ss, rc[1])
                dv = B.op("vector", lambda e, j=j, bko=bko, rb=rb: e.tensor_mul(
                    out=OUTPRE[:, j * 512:(j + 1) * 512], in0=pb[bko][:, :],
                    in1=rb[:, :]), sv, 1)
                bank_ticket[bko] = dv
            op_tk = (sv, sv.val)
            if k == 0:
                dbg("outpre_a", OUTPRE[:, :])
            B.wait("tensor", sv, op_tk[1])
            for oj in range(8):
                bk = 5 + (oj % 2)
                use_bank(bk)
                for ci in range(8):
                    mm(pb[bk][:, :],
                       WPROJ[:, ci * 1024 + oj * 128: ci * 1024 + (oj + 1) * 128],
                       OUTPRE[:, ci * 512:(ci + 1) * 512], ci == 0, ci == 7, inc=(ci == 7))
                mtk = (st, st.val)
                B.wait("vector", st, mtk[1])
                ev = B.op("vector", lambda e, oj=oj, bk=bk, xo=xo: e.scalar_tensor_tensor(
                    out=XR[:, xo + oj * 512: xo + (oj + 1) * 512], in0=pb[bk][:, :],
                    scalar=BIASC[:, oj:oj + 1],
                    in1=XR[:, xo + oj * 512: xo + (oj + 1) * 512],
                    op0=ALU.add, op1=ALU.add), sv, 1)
                bank_ticket[bk] = ev

    B.wait("vector", sd, x_tk[1])
    B.wait("vector", sv, cells_tk[1])
    B.wait("tensor", sg, wffn_tk[1])
    emit_ffn(0)
    if stage == 1:
        emit_store_xr()
        return end_block()

    B.wait("gpsimd", st, st.val)
    B.wait("gpsimd", sv, sv.val)
    B.op("gpsimd", lambda e: e.dma_start(out=WQKV[:, :], in_=wqkv_d), sg, 16)
    wqkv_tk = (sg, sg.val)
    for eng in Builder.ENGS:
        for sem in (st, sv, ss):
            B.wait(eng, sem)
    emit_attn(wqkv_tk)
    if stage == 2:
        emit_store_xr()
        return end_block()

    B.wait("gpsimd", st, st.val)
    B.wait("gpsimd", sv, sv.val)
    B.op("gpsimd", lambda e: e.dma_start(out=W1SB[:, :], in_=w1_d), sg, 16)
    w1_tk2 = (sg, sg.val)
    B.op("gpsimd", lambda e: e.dma_start(out=W2SB[:, :], in_=w2_d), sg, 16)
    B.op("gpsimd", lambda e: e.dma_start(out=W3SB[:, :], in_=w3_d), sg, 16)
    B.op("gpsimd", lambda e: e.dma_start(out=W4SB[:, :], in_=w4_d), sg, 16)
    B.op("gpsimd", lambda e: e.dma_start(out=BROWS[:, :], in_=brows_d), sg, 16)
    B.op("gpsimd", lambda e: e.dma_start(out=GW1SB[:, :], in_=gw1_d), sg, 16)
    B.op("gpsimd", lambda e: e.dma_start(out=GW2SB[:, :], in_=gw2_d), sg, 16)
    B.op("gpsimd", lambda e: e.dma_start(out=GW3SB[:, :], in_=gw3_d), sg, 16)
    gw_tk[1] = (sg, sg.val)
    for eng in Builder.ENGS:
        for sem in (st, sv, ss):
            B.wait(eng, sem)
    emit_ffn(1, w1_tk2=w1_tk2)
    return end_block()


def _images(inputs):
    f32 = np.float32
    x = np.asarray(inputs["L0_attn_x"], f32).reshape(TFULL, C)
    Wqkv = np.asarray(inputs["L0_attn_Wqkv"], f32)
    Wproj = np.asarray(inputs["L0_attn_Wproj"], f32)
    bproj = np.asarray(inputs["L0_attn_bproj"], f32)
    w = {i: np.asarray(inputs["L0_ffn_cn_w%d" % i], f32) for i in (1, 2, 3, 4)}
    b = {i: np.asarray(inputs["L0_ffn_cn_b%d" % i], f32) for i in (1, 2, 3, 4)}
    gw = {i: np.asarray(inputs["L0_ffn_gc_w%d" % i], f32) for i in (1, 2, 3)}
    gb = {i: np.asarray(inputs["L0_ffn_gc_b%d" % i], f32) for i in (1, 2, 3)}
    ar = np.asarray(inputs["L0_ffn_adaptive_range"], f32)

    def img(wm, kch):
        K, O = wm.shape
        return np.ascontiguousarray(
            wm.reshape(kch, 128, O).transpose(1, 0, 2).reshape(128, kch * O))

    com = {
        "w1": img(w[1], 8), "w2": img(w[2], 8), "w3": img(w[3], 4),
        "w4": np.ascontiguousarray(w[4].reshape(2, 128).T),
        "gw1": img(gw[1], 8), "gw2": img(gw[2], 4),
        "gw3": np.ascontiguousarray(gw[3].reshape(2, 128).T),
        "wqkv": img(np.ascontiguousarray(Wqkv.T), 8),
        "wproj": img(np.ascontiguousarray(Wproj.T), 8),
    }
    biasc = np.zeros((128, 16), f32)
    biasc[:, 0:8] = bproj.reshape(8, 128).T
    biasc[:, 8:16] = b[1].reshape(8, 128).T
    com["biasc"] = biasc
    brows = np.zeros((1, 2560), f32)
    brows[0, 0:1024] = b[1]
    brows[0, 1024:1536] = b[2]
    brows[0, 1536:1792] = b[3]
    brows[0, 1792:2304] = gb[1]
    brows[0, 2304:2560] = gb[2]
    com["brows"] = brows
    com["scal"] = np.array([[float(ar[0]), float(b[4][0]), float(gb[3][0]), 0.0]], f32)
    eall = np.zeros((16, 1024), f32)
    for j in range(8):
        eall[2 * j, j * 128: j * 128 + 64] = 1.0
        eall[2 * j + 1, j * 128 + 64: (j + 1) * 128] = 1.0
    com["eall"] = eall

    per_core = []
    for i in range(NCORES):
        sl = x[i * T:(i + 1) * T]
        xi = np.ascontiguousarray(
            sl.reshape(NTILES, TT, NCH, 128).transpose(3, 0, 2, 1).reshape(128, NCH * T))
        m = dict(com)
        m["x"] = xi
        per_core.append(m)
    return per_core


def kernel(**inputs):
    nc = build_program()
    in_maps = _images(inputs)
    res = run_bass_kernel_spmd(nc, in_maps, core_ids=list(range(NCORES)))
    out = np.empty((TFULL, C), np.float32)
    for i in range(NCORES):
        oi = np.asarray(res.results[i]["out"])
        out[i * T:(i + 1) * T] = (
            oi.reshape(128, NTILES, NCH, TT).transpose(1, 3, 2, 0).reshape(T, C))
    return out.reshape(1, TFULL, C)


# version 23
# speedup vs baseline: 1.0102x; 1.0102x over previous
import numpy as np
import concourse.bass as bass
import concourse.mybir as mybir
from concourse.ap import AP
from concourse.bass_utils import run_bass_kernel_spmd

F32 = mybir.dt.float32
BF16 = mybir.dt.bfloat16
AF = mybir.ActivationFunctionType
ALU = mybir.AluOpType

NCORES = 8
TFULL = 16384
C = 1024
NCH = 8
T = TFULL // NCORES
TT = 512
NTILES = T // TT
H = 16
EPS_LN = 1e-5
EPS_AT = 1e-6

SGN = [1.0, 1.0, -1.0, -1.0, -1.0, -1.0, 1.0, 1.0]
SIG = lambda j: (j + 4) % 8
XBFN_SLOT = {6: 0, 7: 1, 0: 2, 1: 3}


class Sem:
    def __init__(self, nc, name):
        self.h = nc.alloc_semaphore(name)
        self.val = 0


class Builder:
    ENGS = ("sync", "gpsimd", "vector", "scalar", "tensor")

    def __init__(self):
        self.ops = {e: [] for e in self.ENGS}

    def op(self, eng, fn, inc=None, n=1):
        if inc is not None:
            inc.val += n
            def run(e, fn=fn, sem=inc.h, n=n):
                fn(e).then_inc(sem, n)
            self.ops[eng].append(run)
            return (inc, inc.val)
        self.ops[eng].append(lambda e, fn=fn: fn(e))
        return None

    def wait(self, eng, sem, val=None):
        v = sem.val if val is None else val
        if v <= 0:
            return
        self.ops[eng].append(lambda e, s=sem.h, v=v: e.wait_ge(s, v))

    def wait_t(self, eng, ticket):
        if ticket is not None:
            self.wait(eng, ticket[0], ticket[1])


def bcast3(tile_ap, reps, n):
    return AP(tile_ap.tensor, tile_ap.offset,
              [[tile_ap.ap[0][0], tile_ap.ap[0][1]], [0, reps], [1, n]])


def r3(ap2d, c, n):
    return ap2d.rearrange("p (c n) -> p c n", c=c, n=n)


def build_program(stage=3, dump=()):
    nc = bass.Bass("TRN2", target_bir_lowering=False, detect_race_conditions=False)

    x_d = nc.dram_tensor("x", [128, NCH * T], F32, kind="ExternalInput").ap()
    out_d = nc.dram_tensor("out", [128, NCH * T], F32, kind="ExternalOutput").ap()
    w1_d = nc.dram_tensor("w1", [128, 8192], F32, kind="ExternalInput").ap()
    w2_d = nc.dram_tensor("w2", [128, 4096], F32, kind="ExternalInput").ap()
    w3_d = nc.dram_tensor("w3", [128, 1024], F32, kind="ExternalInput").ap()
    w4_d = nc.dram_tensor("w4", [128, 2], F32, kind="ExternalInput").ap()
    gw1_d = nc.dram_tensor("gw1", [128, 4096], F32, kind="ExternalInput").ap()
    gw2_d = nc.dram_tensor("gw2", [128, 1024], F32, kind="ExternalInput").ap()
    gw3_d = nc.dram_tensor("gw3", [128, 2], F32, kind="ExternalInput").ap()
    wqkv_d = nc.dram_tensor("wqkv", [128, 24576], F32, kind="ExternalInput").ap()
    wproj_d = nc.dram_tensor("wproj", [128, 8192], F32, kind="ExternalInput").ap()
    biasc_d = nc.dram_tensor("biasc", [128, 16], F32, kind="ExternalInput").ap()
    brows_d = nc.dram_tensor("brows", [1, 2560], F32, kind="ExternalInput").ap()
    scal_d = nc.dram_tensor("scal", [1, 4], F32, kind="ExternalInput").ap()
    eall_d = nc.dram_tensor("eall", [16, 1024], F32, kind="ExternalInput").ap()
    ccg_in = nc.dram_tensor("ccg_in", [128, 8], F32, kind="Internal")
    ccg_out = nc.dram_tensor("ccg_out", [128, 8], F32, kind="Internal")
    cck_in = nc.dram_tensor("cck_in", [128, 520], F32, kind="Internal")
    cck_out = nc.dram_tensor("cck_out", [128, 520], F32, kind="Internal")
    ccs_in = nc.dram_tensor("ccs_in", [128, 8], F32, kind="Internal")
    ccs_out = nc.dram_tensor("ccs_out", [128, 8], F32, kind="Internal")

    base = nc.sbuf_base
    cur = [base]
    offs = {}

    def alloc(name, shape, dtype, at=None):
        nbytes = int(np.prod(shape[1:])) * (4 if dtype == F32 else 2)
        if at is None:
            o = (cur[0] + 63) & ~63
            cur[0] = o + nbytes
        else:
            o = at
        t = nc.alloc_sbuf_tensor_at(name, shape, dtype, offset=o)
        offs[name] = o
        return t

    XR = alloc("XR", [128, NCH * T], F32)
    TROW = alloc("TROW", [1, 3072], F32)
    SPROW = alloc("SPROW", [1, 2048], F32)
    CELLS = alloc("CELLS", [128, 16], F32)
    BIASC = alloc("BIASC", [128, 16], F32)
    ONESB = alloc("ONESB", [1, 512], BF16)
    ONESF = alloc("ONESF", [1, 128], F32)
    ONE1B = alloc("ONE1B", [1, 32], BF16)
    ONECB = alloc("ONECB", [128, 1], BF16)
    PERS_END = cur[0]

    cur[0] = PERS_END
    W1SB = alloc("W1SB", [128, 8192], BF16)
    W2SB = alloc("W2SB", [128, 4096], BF16)
    W3SB = alloc("W3SB", [128, 1024], BF16)
    W4SB = alloc("W4SB", [128, 2], BF16)
    BROWS = alloc("BROWS", [1, 2560], BF16)
    CBROW = alloc("CBROW", [1, 512], BF16)
    GCP = alloc("GCP", [128, 32], F32)
    GCN = alloc("GCN", [128, 8], F32)
    GCNB = alloc("GCNB", [128, 8], BF16)
    GH1 = alloc("GH1", [128, 4], BF16)
    GH2 = alloc("GH2", [128, 2], BF16)
    MROW = alloc("MROW", [1, 2048], F32)
    RROW = alloc("RROW", [1, 2048], F32)
    XBF = alloc("XBF", [128, 4096], BF16)
    XBFN = alloc("XBFN", [128, 2048], BF16)
    GPAIR = [alloc("G_a", [128, 4096], BF16), alloc("G_b", [128, 4096], BF16)]
    HPAIR = [alloc("H_a", [128, 4096], BF16), alloc("H_b", [128, 4096], BF16)]
    SCR = []
    for pi, par in enumerate(("a", "b")):
        s0 = (cur[0] + 63) & ~63
        d = {
            "H1G": alloc("H1G_" + par, [128, 4096], BF16, at=s0),
            "H2G": alloc("H2G_" + par, [128, 2048], BF16, at=s0 + 8192),
            "H3G": alloc("H3G_" + par, [128, 1024], BF16, at=s0 + 12288),
            "CBC": alloc("CBC_" + par, [128, 512], BF16, at=offs["XBFN"] + pi * 1024),
        }
        cur[0] = s0 + 14336
        SCR.append(d)
    GW1SB = alloc("GW1SB", [128, 4096], BF16, at=offs["H1G_a"])
    GW2SB = alloc("GW2SB", [128, 1024], BF16, at=offs["H2G_a"])
    GW3SB = alloc("GW3SB", [128, 2], BF16, at=offs["H3G_a"])
    QFIN = alloc("QFIN", [128, 4096], F32, at=offs["H1G_a"])
    TMP1 = alloc("TMP1", [128, 4096], BF16, at=offs["XBF"])
    FFN_END = cur[0]

    cur[0] = PERS_END
    WQKV = alloc("WQKV", [128, 24576], BF16)
    WPROJ = alloc("WPROJ", [128, 8192], BF16, at=offs["WQKV"])
    QPHI = alloc("QPHI", [128, NCH * T], BF16)
    XNBF = alloc("XNBF", [128, 4096], BF16)
    XNBF2 = alloc("XNBF2", [128, 4096], BF16)
    KTM = alloc("KTM", [128, 1024], BF16)
    VEXT = alloc("VEXT", [128, 1056], BF16)
    PHA = alloc("PHA", [128, 512], BF16)
    PHB = alloc("PHB", [128, 512], BF16)
    PHA2 = alloc("PHA2", [128, 512], BF16)
    PHB2 = alloc("PHB2", [128, 512], BF16)
    OUTPRE = alloc("OUTPRE", [128, 4096], BF16)
    KVSB = alloc("KVSB", [128, 520], F32)
    KVBF = alloc("KVBF", [128, 520], BF16)
    NRS = alloc("NRS", [16, 512], F32)
    REC16 = alloc("REC16", [16, 512], F32)
    EALL = alloc("EALL", [16, 1024], F32, at=offs["KTM"])
    KS8 = alloc("KS8", [128, 8], F32)
    KSF = alloc("KSF", [128, 128], BF16)
    ATTN_END = cur[0]

    cap = nc.sbuf_top
    assert FFN_END <= cap, (FFN_END, cap)
    assert ATTN_END <= cap, (ATTN_END, cap)

    pb = [nc.alloc_psum_tensor("pb%d" % i, [128, 512], F32) for i in range(8)]

    sd = Sem(nc, "sd")
    sg = Sem(nc, "sg")
    sv = Sem(nc, "sv")
    ss = Sem(nc, "ss")
    st = Sem(nc, "st")
    scc = Sem(nc, "scc")
    so = Sem(nc, "so")

    B = Builder()
    bank_ticket = [None] * 8

    def use_bank(i):
        B.wait_t("tensor", bank_ticket[i])

    def dbg(tag, ap, dtype=None):
        if tag not in dump:
            return
        dt = ap.tensor.dtype if dtype is None else dtype
        dd = nc.dram_tensor("dbg_" + tag, list(ap.shape), dt, kind="ExternalOutput").ap()
        for sem in (st, sv, ss):
            B.wait("sync", sem)
        B.op("sync", lambda e, dd=dd, ap=ap: e.dma_start(out=dd, in_=ap), sd, 16)
        for eng in ("vector", "scalar", "tensor"):
            B.wait(eng, sd)

    C_AR, C_B4, C_GB3, C_EPS, C_HB4, C_HGB3, C_K1, C_K2, C_C3, C_GTH, C_GF, C_C0 = range(12)

    def cell(i):
        return CELLS[0:1, i:i + 1]

    R_C, R_A, R_B, R_T1, R_T2, R_T3 = (0, 512, 1024, 1536, 2048, 2560)

    def trow(slot):
        return TROW[0:1, slot:slot + 512]

    def mm(out, lhsT, rhs, start, stop, inc=False):
        if inc:
            return B.op("tensor", lambda e: e.matmul(out=out, lhsT=lhsT, rhs=rhs,
                                                     start=start, stop=stop), st, 1)
        return B.op("tensor", lambda e: e.matmul(out=out, lhsT=lhsT, rhs=rhs,
                                                 start=start, stop=stop))

    def end_block():
        B.wait("gpsimd", so, so.val)
        B.wait("sync", so, so.val)
        with nc.Block() as block:
            for eng in Builder.ENGS:
                fns = B.ops[eng]
                def body(e, fns=fns):
                    for f in fns:
                        f(e)
                getattr(block, eng)(body)
        return nc

    def xo_of(k):
        return k * 4096

    def emit_store_xr():
        for k in range(NTILES):
            xo = xo_of(k)
            B.wait("sync", sv, sv.val)
            B.wait("sync", ss, ss.val)
            B.op("sync", lambda e, xo=xo: e.dma_start(
                out=AP(out_d.tensor, xo, [[NCH * T, 128], [1, 4096]]),
                in_=XR[:, xo:xo + 4096]), so, 16)

    B.op("sync", lambda e: e.dma_start(out=XR[:, :], in_=x_d), sd, 16)
    x_tk = (sd, sd.val)
    B.op("sync", lambda e: e.dma_start(out=BIASC[:, :], in_=biasc_d), sd, 16)
    B.op("sync", lambda e: e.dma_start(out=CELLS[0:1, 0:4], in_=scal_d), sd, 16)
    init_d = (sd, sd.val)
    B.op("gpsimd", lambda e: e.dma_start(out=W1SB[:, :], in_=w1_d), sg, 16)
    B.op("gpsimd", lambda e: e.dma_start(out=W2SB[:, :], in_=w2_d), sg, 16)
    B.op("gpsimd", lambda e: e.dma_start(out=W3SB[:, :], in_=w3_d), sg, 16)
    B.op("gpsimd", lambda e: e.dma_start(out=W4SB[:, :], in_=w4_d), sg, 16)
    B.op("gpsimd", lambda e: e.dma_start(out=BROWS[:, :], in_=brows_d), sg, 16)
    wffn_tk = (sg, sg.val)
    B.op("gpsimd", lambda e: e.dma_start(out=GW1SB[:, :], in_=gw1_d), sg, 16)
    B.op("gpsimd", lambda e: e.dma_start(out=GW2SB[:, :], in_=gw2_d), sg, 16)
    B.op("gpsimd", lambda e: e.dma_start(out=GW3SB[:, :], in_=gw3_d), sg, 16)
    gw_tk = {0: (sg, sg.val)}
    B.op("vector", lambda e: e.memset(ONESB[:, :], 1.0))
    B.op("vector", lambda e: e.memset(ONESF[:, :], 1.0))
    B.op("vector", lambda e: e.memset(ONE1B[:, :], 1.0))
    B.op("vector", lambda e: e.memset(ONECB[:, :], 1.0))
    B.op("vector", lambda e: e.memset(CELLS[:, C_EPS:C_EPS + 1], EPS_LN), sv, 1)
    B.wait("vector", sd, init_d[1])
    B.op("vector", lambda e: e.tensor_scalar(out=cell(C_HB4), in0=cell(C_B4),
                                             scalar1=0.5, scalar2=None, op0=ALU.mult))
    B.op("vector", lambda e: e.tensor_scalar(out=cell(C_HGB3), in0=cell(C_GB3),
                                             scalar1=0.5, scalar2=None, op0=ALU.mult))
    B.op("vector", lambda e: e.tensor_scalar(out=cell(C_K1), in0=cell(C_AR),
                                             scalar1=0.3, scalar2=None, op0=ALU.mult))
    B.op("vector", lambda e: e.tensor_scalar(out=cell(C_K2), in0=cell(C_AR),
                                             scalar1=1.4, scalar2=None, op0=ALU.mult))
    B.op("vector", lambda e: e.tensor_scalar(out=cell(C_C3), in0=cell(C_AR),
                                             scalar1=-0.7, scalar2=0.1,
                                             op0=ALU.mult, op1=ALU.add), sv, 1)
    cells_tk = (sv, sv.val)

    def emit_ffn(phase, w1_tk2=None):
        SQB = GPAIR[0]
        cp = [None] * (NTILES + 1)
        cp[0] = B.op("vector", lambda e: e.tensor_copy(
            out=XBF[:, :], in_=XR[:, 0:4096]), sv, 1)
        for k in range(NTILES):
            xo = xo_of(k)
            use_bank(6)
            B.wait("tensor", cp[k][0], cp[k][1])
            for ci in range(8):
                mm(pb[6][0:1, :], ONECB[:, :], XBF[:, ci * 512:(ci + 1) * 512],
                   ci == 0, ci == 7, inc=(ci == 7))
            s1_tk = (st, st.val)
            B.wait("scalar", st, s1_tk[1])
            B.wait("scalar", sv, sv.val)
            stk = B.op("scalar", lambda e: e.activation(
                out=SQB[:, :], in_=XBF[:, :], func=AF.Square), ss, 1)
            use_bank(7)
            B.wait("tensor", ss, stk[1])
            for ci in range(8):
                mm(pb[7][0:1, :], ONECB[:, :], SQB[:, ci * 512:(ci + 1) * 512],
                   ci == 0, ci == 7, inc=(ci == 7))
            s2_tk = (st, st.val)
            if k + 1 < NTILES:
                xo2 = xo_of(k + 1)
                B.wait("vector", st, s1_tk[1])
                B.wait("vector", ss, stk[1])
                cp[k + 1] = B.op("vector", lambda e, xo2=xo2: e.tensor_copy(
                    out=XBF[:, :], in_=XR[:, xo2:xo2 + 4096]), sv, 1)
            B.op("scalar", lambda e, k=k: e.mul(
                out=MROW[0:1, k * 512:(k + 1) * 512], in_=pb[6][0:1, :], mul=1.0 / C), ss, 1)
            B.op("scalar", lambda e: e.activation(
                out=trow(R_T1), in_=pb[6][0:1, :], func=AF.Square, scale=1.0 / C), ss, 1)
            msq_tk = (ss, ss.val)
            bank_ticket[6] = msq_tk
            B.wait("vector", ss, msq_tk[1])
            B.wait("vector", st, s2_tk[1])
            vv = B.op("vector", lambda e: e.scalar_tensor_tensor(
                out=trow(R_T2), in0=pb[7][0:1, :], scalar=1.0 / C, in1=trow(R_T1),
                op0=ALU.mult, op1=ALU.subtract), sv, 1)
            bank_ticket[7] = vv
            B.wait("scalar", sv, vv[1])
            B.op("scalar", lambda e: e.activation(
                out=trow(R_T3), in_=trow(R_T2), func=AF.Ln, bias=cell(C_EPS)), ss, 1)
            rtk = B.op("scalar", lambda e, k=k: e.activation(
                out=RROW[0:1, k * 512:(k + 1) * 512], in_=trow(R_T3), func=AF.Exp,
                scale=-0.5), ss, 1)
            use_bank(0)
            use_bank(1)
            B.wait("tensor", ss, rtk[1])
            mm(pb[0][:, :], ONESF[:, :], MROW[0:1, k * 512:(k + 1) * 512], True, True, inc=True)
            bc = mm(pb[1][:, :], ONESF[:, :], RROW[0:1, k * 512:(k + 1) * 512], True, True, inc=True)
            B.wait("vector", st, bc[1])
            B.op("vector", lambda e, xo=xo: e.tensor_sub(
                out=SQB[:, :], in0=XR[:, xo:xo + 4096], in1=bcast3(pb[0].ap(), 8, 512)), sv, 1)
            for ci in range(8):
                B.op("vector", lambda e, ci=ci, k=k: e.scalar_tensor_tensor(
                    out=SQB[:, ci * 512:(ci + 1) * 512],
                    in0=SQB[:, ci * 512:(ci + 1) * 512], scalar=1.0,
                    in1=pb[1][:, :], op0=ALU.bypass, op1=ALU.mult,
                    accum_out=GCP[:, k * 8 + ci:k * 8 + ci + 1]), sv, 1)
            gc_tk = (sv, sv.val)
            bank_ticket[0] = gc_tk
            bank_ticket[1] = gc_tk

        B.op("vector", lambda e: e.tensor_add(out=GCN[:, :], in0=GCP[:, 0:8], in1=GCP[:, 8:16]))
        B.op("vector", lambda e: e.tensor_add(out=GCN[:, :], in0=GCN[:, :], in1=GCP[:, 16:24]))
        B.op("vector", lambda e: e.tensor_add(out=GCN[:, :], in0=GCN[:, :], in1=GCP[:, 24:32]), sv, 1)
        B.wait("sync", sv, sv.val)
        B.op("sync", lambda e: e.dma_start(out=ccg_in.ap(), in_=GCN[:, :]), sd, 16)
        B.wait("gpsimd", sd, sd.val)
        B.op("gpsimd", lambda e: e.collective_compute(
            "AllReduce", ALU.add, replica_groups=[list(range(NCORES))],
            ins=[ccg_in.ap()], outs=[ccg_out.ap()]), scc, 1)
        B.wait("sync", scc, scc.val)
        B.op("sync", lambda e: e.dma_start(out=GCN[:, :], in_=ccg_out.ap()), sd, 16)
        ar_done = (sd, sd.val)

        def emit_prep(k, par):
            xo = xo_of(k)
            use_bank(0)
            use_bank(1)
            mm(pb[0][:, :], ONESF[:, :], MROW[0:1, k * 512:(k + 1) * 512], True, True, inc=True)
            bc = mm(pb[1][:, :], ONESF[:, :], RROW[0:1, k * 512:(k + 1) * 512], True, True, inc=True)
            B.wait("vector", st, bc[1])
            B.op("vector", lambda e, xo=xo: e.tensor_sub(
                out=XBF[:, :], in0=XR[:, xo:xo + 4096], in1=bcast3(pb[0].ap(), 8, 512)), sv, 1)
            o0 = B.op("vector", lambda e: e.tensor_tensor(
                out=r3(XBF[:, :], 8, 512), in0=r3(XBF[:, :], 8, 512),
                in1=bcast3(pb[1].ap(), 8, 512), op=ALU.mult), sv, 1)
            bank_ticket[0] = o0
            bank_ticket[1] = o0
            B.op("vector", lambda e: e.tensor_scalar(
                out=XBFN[:, 0:1024], in0=XBF[:, 6 * 512:8 * 512], scalar1=-1.0,
                scalar2=None, op0=ALU.mult))
            on = B.op("vector", lambda e: e.tensor_scalar(
                out=XBFN[:, 1024:2048], in0=XBF[:, 0:1024], scalar1=-1.0,
                scalar2=None, op0=ALU.mult), sv, 1)
            if k == 0:
                dbg("out0_p%d" % phase, XBF[:, :])
            return o0, on

        def emit_g0h0(k, par, o0, on):
            GT, HT = GPAIR[par], HPAIR[par]
            B.wait("tensor", sv, o0[1])
            if phase == 1 and k == 0:
                B.wait("tensor", w1_tk2[0], w1_tk2[1])
            for oj in range(8):
                bk = 2 + (oj % 2)
                use_bank(bk)
                for ci in range(8):
                    mm(pb[bk][:, :],
                       W1SB[:, ci * 1024 + oj * 128: ci * 1024 + (oj + 1) * 128],
                       XBF[:, ci * 512:(ci + 1) * 512], ci == 0, ci == 7, inc=(ci == 7))
                mtk = (st, st.val)
                B.wait("scalar", st, mtk[1])
                etk = B.op("scalar", lambda e, oj=oj, bk=bk, GT=GT: e.copy(
                    out=GT[:, oj * 512:(oj + 1) * 512], in_=pb[bk][:, :]), ss, 1)
                bank_ticket[bk] = etk
            B.wait("tensor", sv, on[1])
            for oj in range(8):
                bk = 2 + (oj % 2)
                use_bank(bk)
                for ci in range(8):
                    if SGN[ci] > 0:
                        rhs = XBF[:, SIG(ci) * 512:(SIG(ci) + 1) * 512]
                    else:
                        sl = XBFN_SLOT[SIG(ci)]
                        rhs = XBFN[:, sl * 512:(sl + 1) * 512]
                    mm(pb[bk][:, :],
                       W1SB[:, ci * 1024 + oj * 128: ci * 1024 + (oj + 1) * 128],
                       rhs, ci == 0, ci == 7, inc=(ci == 7))
                mtk = (st, st.val)
                B.wait("vector", st, mtk[1])
                etk = B.op("vector", lambda e, oj=oj, bk=bk, HT=HT: e.tensor_copy(
                    out=HT[:, oj * 512:(oj + 1) * 512], in_=pb[bk][:, :]), sv, 1)
                bank_ticket[bk] = etk
            if k == 0:
                dbg("G0_p%d" % phase, GT[:, :])
                dbg("H0_p%d" % phase, HT[:, :])

        def emit_gf_net():
            B.wait("vector", ar_done[0], ar_done[1])
            gcn_tk = B.op("vector", lambda e: e.tensor_scalar(
                out=GCNB[:, :], in0=GCN[:, :], scalar1=1.0 / TFULL, scalar2=None,
                op0=ALU.mult), sv, 1)
            use_bank(6)
            B.wait("tensor", sv, gcn_tk[1])
            B.wait("tensor", gw_tk[phase][0], gw_tk[phase][1])
            for oj in range(4):
                for ci in range(8):
                    mm(pb[6][:, oj:oj + 1],
                       GW1SB[:, ci * 512 + oj * 128: ci * 512 + (oj + 1) * 128],
                       GCNB[:, ci:ci + 1], ci == 0, False)
                mm(pb[6][:, oj:oj + 1], BROWS[0:1, 1792 + oj * 128:1792 + (oj + 1) * 128],
                   ONE1B[0:1, 0:1], False, True, inc=True)
            g1_tk = (st, st.val)
            B.wait("scalar", st, g1_tk[1])
            e1 = B.op("scalar", lambda e: e.activation(out=GH1[:, :], in_=pb[6][:, 0:4],
                                                       func=AF.Gelu), ss, 1)
            bank_ticket[6] = e1
            use_bank(7)
            B.wait("tensor", ss, e1[1])
            for oj in range(2):
                for ci in range(4):
                    mm(pb[7][:, oj:oj + 1],
                       GW2SB[:, ci * 256 + oj * 128: ci * 256 + (oj + 1) * 128],
                       GH1[:, ci:ci + 1], ci == 0, False)
                mm(pb[7][:, oj:oj + 1], BROWS[0:1, 2304 + oj * 128:2304 + (oj + 1) * 128],
                   ONE1B[0:1, 0:1], False, True, inc=True)
            g2_tk = (st, st.val)
            B.wait("scalar", st, g2_tk[1])
            e2 = B.op("scalar", lambda e: e.activation(out=GH2[:, :], in_=pb[7][:, 0:2],
                                                       func=AF.Gelu), ss, 1)
            bank_ticket[7] = e2
            use_bank(4)
            B.wait("tensor", ss, e2[1])
            for ci in range(2):
                mm(pb[4][0:1, 0:1], GW3SB[:, ci:ci + 1], GH2[:, ci:ci + 1],
                   ci == 0, ci == 1, inc=(ci == 1))
            g3_tk = (st, st.val)
            B.wait("scalar", st, g3_tk[1])
            e3 = B.op("scalar", lambda e: e.activation(
                out=cell(C_GTH), in_=pb[4][0:1, 0:1], func=AF.Tanh, scale=0.5,
                bias=cell(C_HGB3)), ss, 1)
            bank_ticket[4] = e3
            B.wait("vector", ss, e3[1])
            B.op("vector", lambda e: e.tensor_scalar(
                out=cell(C_GF), in0=cell(C_GTH), scalar1=0.5, scalar2=0.5,
                op0=ALU.mult, op1=ALU.add))
            cz = B.op("vector", lambda e: e.tensor_scalar(
                out=cell(C_C0), in0=cell(C_GF), scalar1=cell(C_K2), scalar2=cell(C_C3),
                op0=ALU.mult, op1=ALU.add), sv, 1)
            return cz

        RS = [{"c": R_C, "A": R_A, "B": R_B}, {"c": R_T1, "A": R_T2, "B": R_T3}]

        def rowp(par, nm):
            return trow(RS[par][nm])

        last_fin = [None]

        def emit_cycles(k, par, cz_tk):
            GT, HT = GPAIR[par], HPAIR[par]
            scr = SCR[par]
            upd_tk = [None] * 4
            for cyc in range(3):
                if cyc == 0 and cz_tk is not None:
                    B.wait("scalar", cz_tk[0], cz_tk[1])
                for g2 in range(4):
                    c0_, c1_ = 2 * g2, 2 * g2 + 1
                    if cyc == 0 and g2 == 0 and last_fin[0] is not None:
                        B.wait("scalar", last_fin[0][0], last_fin[0][1])
                    if upd_tk[g2] is not None:
                        B.wait("scalar", sv, upd_tk[g2][1])
                    for ci in (c0_, c1_):
                        B.op("scalar", lambda e, ci=ci, GT=GT, scr=scr: e.activation(
                            out=scr["H1G"][:, ci * 512:(ci + 1) * 512],
                            in_=GT[:, ci * 512:(ci + 1) * 512], func=AF.Gelu,
                            bias=BIASC[:, 8 + ci:9 + ci]),
                            ss if ci == c1_ else None, 1)
                    g1tk = (ss, ss.val)
                    B.wait("tensor", ss, g1tk[1])
                    if g2 == 0:
                        for bk in range(4):
                            use_bank(bk)
                    for ci in (c0_, c1_):
                        for oj in range(4):
                            mm(pb[oj][:, :],
                               W2SB[:, ci * 512 + oj * 128: ci * 512 + (oj + 1) * 128],
                               scr["H1G"][:, ci * 512:(ci + 1) * 512],
                               ci == 0, ci == 7 and oj == 3, inc=(ci == 7 and oj == 3))
                w2tk = (st, st.val)
                B.wait("scalar", st, w2tk[1])
                for oj in range(4):
                    etk = B.op("scalar", lambda e, oj=oj, scr=scr: e.activation(
                        out=scr["H2G"][:, oj * 512:(oj + 1) * 512], in_=pb[oj][:, :],
                        func=AF.Gelu), ss, 1)
                    bank_ticket[oj] = etk
                h2tk = (ss, ss.val)
                use_bank(0)
                use_bank(1)
                B.wait("tensor", ss, h2tk[1])
                for ci in range(4):
                    for oj in range(2):
                        mm(pb[oj][:, :],
                           W3SB[:, ci * 256 + oj * 128: ci * 256 + (oj + 1) * 128],
                           scr["H2G"][:, ci * 512:(ci + 1) * 512],
                           ci == 0, ci == 3 and oj == 1, inc=(ci == 3 and oj == 1))
                w3tk = (st, st.val)
                B.wait("scalar", st, w3tk[1])
                for oj in range(2):
                    etk = B.op("scalar", lambda e, oj=oj, scr=scr: e.activation(
                        out=scr["H3G"][:, oj * 512:(oj + 1) * 512], in_=pb[oj][:, :],
                        func=AF.Gelu), ss, 1)
                    bank_ticket[oj] = etk
                h3tk = (ss, ss.val)
                use_bank(2)
                B.wait("tensor", ss, h3tk[1])
                for ci in range(2):
                    mm(pb[2][0:1, :], W4SB[:, ci:ci + 1],
                       scr["H3G"][:, ci * 512:(ci + 1) * 512], ci == 0, ci == 1,
                       inc=(ci == 1))
                ztk = (st, st.val)
                B.wait("scalar", st, ztk[1])
                thtk = B.op("scalar", lambda e, par=par: e.activation(
                    out=rowp(par, "c"), in_=pb[2][0:1, :], func=AF.Tanh, scale=0.5,
                    bias=cell(C_HB4)), ss, 1)
                bank_ticket[2] = thtk
                B.wait("vector", ss, thtk[1])
                if cyc == 0 and cz_tk is not None:
                    B.wait("vector", cz_tk[0], cz_tk[1])
                B.op("vector", lambda e, par=par: e.tensor_scalar(
                    out=rowp(par, "c"), in0=rowp(par, "c"), scalar1=cell(C_K1),
                    scalar2=cell(C_C0), op0=ALU.mult, op1=ALU.add), sv, 1)
                if cyc == 0:
                    B.op("vector", lambda e, par=par: e.tensor_copy(
                        out=rowp(par, "B"), in_=rowp(par, "c")))
                    B.op("vector", lambda e, par=par: e.memset(rowp(par, "A"), 1.0), sv, 1)
                else:
                    B.op("vector", lambda e, par=par: e.tensor_mul(
                        out=SPROW[0:1, par * 512:(par + 1) * 512], in0=rowp(par, "c"),
                        in1=rowp(par, "B")))
                    B.op("vector", lambda e, par=par: e.tensor_mul(
                        out=SPROW[0:1, 1024 + par * 512:1024 + (par + 1) * 512],
                        in0=rowp(par, "c"), in1=rowp(par, "A")))
                    B.op("vector", lambda e, par=par: e.tensor_sub(
                        out=rowp(par, "A"), in0=rowp(par, "A"),
                        in1=SPROW[0:1, par * 512:(par + 1) * 512]))
                    B.op("vector", lambda e, par=par: e.tensor_add(
                        out=rowp(par, "B"), in0=rowp(par, "B"),
                        in1=SPROW[0:1, 1024 + par * 512:1024 + (par + 1) * 512]), sv, 1)
                if cyc < 2:
                    cbtk = B.op("vector", lambda e, par=par: e.tensor_copy(
                        out=CBROW[:, :], in_=rowp(par, "c")), sv, 1)
                    cbb = 6 if cyc == 0 else 7
                    use_bank(cbb)
                    B.wait("tensor", sv, cbtk[1])
                    bmk = mm(pb[cbb][:, :], ONESB[0:1, 0:128], CBROW[:, :], True, True, inc=True)
                    B.wait("scalar", st, bmk[1])
                    cbe = B.op("scalar", lambda e, cbb=cbb, scr=scr: e.copy(
                        out=scr["CBC"][:, :], in_=pb[cbb][:, :]), ss, 1)
                    bank_ticket[cbb] = cbe
                    B.wait("vector", ss, cbe[1])
                    for g2 in range(4):
                        sl = slice(g2 * 1024, (g2 + 1) * 1024)
                        B.op("vector", lambda e, sl=sl, scr=scr, HT=HT: e.tensor_tensor(
                            out=r3(TMP1[:, sl], 2, 512), in0=r3(HT[:, sl], 2, 512),
                            in1=bcast3(scr["CBC"].ap(), 2, 512), op=ALU.mult), sv, 1)
                        if cyc == 0:
                            B.op("vector", lambda e, sl=sl, scr=scr, GT=GT: e.tensor_tensor(
                                out=r3(scr["H1G"][:, sl], 2, 512), in0=r3(GT[:, sl], 2, 512),
                                in1=bcast3(scr["CBC"].ap(), 2, 512), op=ALU.mult))
                            B.op("vector", lambda e, sl=sl, scr=scr, HT=HT: e.tensor_sub(
                                out=HT[:, sl], in0=HT[:, sl], in1=scr["H1G"][:, sl]))
                        gadd = B.op("vector", lambda e, sl=sl, GT=GT: e.tensor_add(
                            out=GT[:, sl], in0=GT[:, sl], in1=TMP1[:, sl]), sv, 1)
                        upd_tk[g2] = gadd

        def emit_final(k, par):
            xo = xo_of(k)
            if k == 0:
                dbg("mr_p%d" % phase, MROW[0:1, :])
                dbg("rr_p%d" % phase, RROW[0:1, :])
            rr_ = RROW[0:1, k * 512:(k + 1) * 512]
            mr_ = MROW[0:1, k * 512:(k + 1) * 512]
            B.op("vector", lambda e, par=par, rr_=rr_: e.scalar_tensor_tensor(
                out=rowp(par, "A"), in0=rowp(par, "A"), scalar=0.5, in1=rr_,
                op0=ALU.mult, op1=ALU.mult))
            B.op("vector", lambda e, par=par, rr_=rr_: e.scalar_tensor_tensor(
                out=rowp(par, "B"), in0=rowp(par, "B"), scalar=0.5, in1=rr_,
                op0=ALU.mult, op1=ALU.mult))
            B.op("vector", lambda e, par=par, mr_=mr_: e.tensor_mul(
                out=rowp(par, "c"), in0=rowp(par, "A"), in1=mr_))
            B.op("vector", lambda e, par=par, mr_=mr_: e.tensor_mul(
                out=SPROW[0:1, 0:512], in0=rowp(par, "B"), in1=mr_))
            B.op("vector", lambda e, par=par: e.tensor_add(
                out=SPROW[0:1, 512:1024], in0=rowp(par, "c"), in1=SPROW[0:1, 0:512]))
            B.op("vector", lambda e, par=par: e.tensor_sub(
                out=rowp(par, "c"), in0=rowp(par, "c"), in1=SPROW[0:1, 0:512]))
            rows_tk = B.op("vector", lambda e, par=par: e.tensor_scalar(
                out=rowp(par, "A"), in0=rowp(par, "A"), scalar1=1.0, scalar2=None,
                op0=ALU.add), sv, 1)
            use_bank(6)
            use_bank(7)
            use_bank(4)
            use_bank(5)
            B.wait("tensor", sv, rows_tk[1])
            mm(pb[6][:, :], ONESF[:, :], rowp(par, "A"), True, True, inc=True)
            mm(pb[7][:, :], ONESF[:, :], rowp(par, "B"), True, True, inc=True)
            mm(pb[4][:, :], ONESF[:, :], SPROW[0:1, 512:1024], True, True, inc=True)
            fb = mm(pb[5][:, :], ONESF[:, :], rowp(par, "c"), True, True, inc=True)
            B.wait("vector", st, fb[1])
            B.op("vector", lambda e, xo=xo: e.tensor_tensor(
                out=r3(QFIN[:, 0:2048], 4, 512), in0=r3(XR[:, xo + 2048:xo + 4096], 4, 512),
                in1=bcast3(pb[7].ap(), 4, 512), op=ALU.mult))
            B.op("vector", lambda e, xo=xo: e.tensor_tensor(
                out=r3(QFIN[:, 2048:4096], 4, 512), in0=r3(XR[:, xo:xo + 2048], 4, 512),
                in1=bcast3(pb[7].ap(), 4, 512), op=ALU.mult), sv, 1)
            B.op("vector", lambda e, xo=xo: e.tensor_tensor(
                out=r3(XR[:, xo:xo + 4096], 8, 512), in0=r3(XR[:, xo:xo + 4096], 8, 512),
                in1=bcast3(pb[6].ap(), 8, 512), op=ALU.mult), sv, 1)
            for rq in range(4):
                opx = ALU.add if SGN[2 * rq] > 0 else ALU.subtract
                B.op("vector", lambda e, rq=rq, opx=opx, xo=xo: e.tensor_tensor(
                    out=XR[:, xo + rq * 1024: xo + (rq + 1) * 1024],
                    in0=XR[:, xo + rq * 1024: xo + (rq + 1) * 1024],
                    in1=QFIN[:, rq * 1024:(rq + 1) * 1024], op=opx), sv, 1)
            for rq in range(4):
                rbank = pb[4] if SGN[2 * rq] > 0 else pb[5]
                B.op("vector", lambda e, rq=rq, rbank=rbank, xo=xo: e.tensor_sub(
                    out=XR[:, xo + rq * 1024: xo + (rq + 1) * 1024],
                    in0=XR[:, xo + rq * 1024: xo + (rq + 1) * 1024],
                    in1=bcast3(rbank.ap(), 2, 512)), sv, 1)
            fin_tk = (sv, sv.val)
            last_fin[0] = fin_tk
            for bknum in (6, 7, 4, 5):
                bank_ticket[bknum] = fin_tk
            if phase == 1:
                B.wait("sync", sv, fin_tk[1])
                B.op("sync", lambda e, xo=xo: e.dma_start(
                    out=AP(out_d.tensor, xo, [[NCH * T, 128], [1, 4096]]),
                    in_=XR[:, xo:xo + 4096]), so, 16)

        o0 = emit_prep(0, 0)
        emit_g0h0(0, 0, *o0)
        cz_tk = emit_gf_net()
        for k in range(NTILES):
            emit_cycles(k, k % 2, cz_tk)
            if k + 1 < NTILES:
                o0n = emit_prep(k + 1, (k + 1) % 2)
                emit_g0h0(k + 1, (k + 1) % 2, *o0n)
            emit_final(k, k % 2)

    def emit_attn(wqkv_tk):
        B.op("vector", lambda e: e.memset(VEXT[:, :], 0.0))
        for h in range(H):
            B.op("vector", lambda e, h=h: e.memset(VEXT[:, h * 66 + 64:h * 66 + 65], 1.0))
        B.op("vector", lambda e: e.memset(KSF[:, :], 0.0), sv, 1)
        vext_tk = (sv, sv.val)
        ksf_zero_tk = vext_tk

        kv_first = [True]
        XNP = [XNBF, XNBF2]
        PHP = [(PHA, PHB), (PHA2, PHB2)]
        xn_tks = [None] * NTILES
        ph_war = [0, 0]
        ln_st = {}

        def ln_a(k, half):
            xo = xo_of(k)
            XN = XNP[k % 2]
            B.wait("vector", st, st.val)
            cp = B.op("vector", lambda e, xo=xo, XN=XN, half=half: e.tensor_copy(
                out=XN[:, half * 2048:(half + 1) * 2048],
                in_=XR[:, xo + half * 2048: xo + (half + 1) * 2048]), sv, 1)
            if half == 0:
                ln_st[k] = {}
            ln_st[k]["cp%d" % half] = cp

        def ln_b(k):
            XN = XNP[k % 2]
            stt = ln_st[k]
            use_bank(6)
            B.wait("tensor", sv, stt["cp1"][1])
            for ci in range(8):
                mm(pb[6][0:1, :], ONECB[:, :], XN[:, ci * 512:(ci + 1) * 512],
                   ci == 0, ci == 7, inc=(ci == 7))
            stt["s1"] = (st, st.val)
            B.wait("scalar", sv, stt["cp1"][1])
            B.wait("scalar", st, stt["s1"][1])
            stt["sq"] = B.op("scalar", lambda e, XN=XN: e.activation(
                out=OUTPRE[:, :], in_=XN[:, :], func=AF.Square), ss, 1)

        def ln_c(k):
            stt = ln_st[k]
            use_bank(7)
            B.wait("tensor", ss, stt["sq"][1])
            for ci in range(8):
                mm(pb[7][0:1, :], ONECB[:, :], OUTPRE[:, ci * 512:(ci + 1) * 512],
                   ci == 0, ci == 7, inc=(ci == 7))
            s2 = (st, st.val)
            B.op("scalar", lambda e: e.mul(out=trow(R_T1), in_=pb[6][0:1, :], mul=1.0 / C), ss, 1)
            msq = B.op("scalar", lambda e: e.activation(
                out=trow(R_T3), in_=pb[6][0:1, :], func=AF.Square, scale=1.0 / C), ss, 1)
            bank_ticket[6] = msq
            B.wait("vector", ss, msq[1])
            B.wait("vector", st, s2[1])
            vv = B.op("vector", lambda e: e.scalar_tensor_tensor(
                out=trow(R_C), in0=pb[7][0:1, :], scalar=1.0 / C, in1=trow(R_T3),
                op0=ALU.mult, op1=ALU.subtract), sv, 1)
            bank_ticket[7] = vv
            B.wait("scalar", sv, vv[1])
            B.op("scalar", lambda e: e.activation(
                out=trow(R_C), in_=trow(R_C), func=AF.Ln, bias=cell(C_EPS)), ss, 1)
            rtk = B.op("scalar", lambda e: e.activation(
                out=trow(R_T2), in_=trow(R_C), func=AF.Exp, scale=-0.5), ss, 1)
            use_bank(6)
            use_bank(7)
            B.wait("tensor", ss, rtk[1])
            mm(pb[6][:, :], ONESF[:, :], trow(R_T1), True, True, inc=True)
            bc = mm(pb[7][:, :], ONESF[:, :], trow(R_T2), True, True, inc=True)
            stt["bc"] = (st, st.val)

        def ln_d(k):
            xo = xo_of(k)
            XN = XNP[k % 2]
            stt = ln_st[k]
            B.wait("vector", st, stt["bc"][1])
            B.op("vector", lambda e, xo=xo, XN=XN: e.tensor_sub(
                out=r3(XN[:, :], 8, 512), in0=r3(XR[:, xo:xo + 4096], 8, 512),
                in1=bcast3(pb[6].ap(), 8, 512)), sv, 1)
            xn = B.op("vector", lambda e, XN=XN: e.tensor_tensor(
                out=r3(XN[:, :], 8, 512), in0=r3(XN[:, :], 8, 512),
                in1=bcast3(pb[7].ap(), 8, 512), op=ALU.mult), sv, 1)
            bank_ticket[6] = xn
            bank_ticket[7] = xn
            return xn

        ln_a(0, 0)
        ln_a(0, 1)
        ln_b(0)
        ln_c(0)
        xn_tks[0] = ln_d(0)
        for k in range(NTILES):
            XN = XNP[k % 2]
            B.wait("tensor", sv, xn_tks[k][1])
            if k == 0:
                B.wait("tensor", wqkv_tk[0], wqkv_tk[1])
            for oj in range(8):
                bk = oj % 4
                use_bank(bk)
                for ci in range(8):
                    mm(pb[bk][:, :],
                       WQKV[:, ci * 3072 + oj * 128: ci * 3072 + (oj + 1) * 128],
                       XN[:, ci * 512:(ci + 1) * 512], ci == 0, ci == 7, inc=(ci == 7))
                mtk = (st, st.val)
                pa, pbb = PHP[oj % 2]
                B.wait("scalar", st, mtk[1])
                B.wait("scalar", sv, ph_war[oj % 2])
                B.op("scalar", lambda e, bk=bk, pa=pa: e.activation(
                    out=pa[:, :], in_=pb[bk][:, :], func=AF.Relu, scale=-1.0), ss, 1)
                e_tk = B.op("scalar", lambda e, pa=pa, pbb=pbb: e.activation(
                    out=pbb[:, :], in_=pa[:, :], func=AF.Exp, scale=-1.0), ss, 1)
                B.wait("vector", st, mtk[1])
                B.wait("vector", ss, e_tk[1])
                r1 = B.op("vector", lambda e, bk=bk, pa=pa: e.tensor_scalar(
                    out=pa[:, :], in0=pb[bk][:, :], scalar1=0.0, scalar2=None,
                    op0=ALU.max), sv, 1)
                bank_ticket[bk] = r1
                ad = B.op("vector", lambda e, oj=oj, k=k, pa=pa, pbb=pbb: e.tensor_add(
                    out=QPHI[:, k * 4096 + oj * 512: k * 4096 + (oj + 1) * 512],
                    in0=pbb[:, :], in1=pa[:, :]), sv, 1)
                ph_war[oj % 2] = ad[1]
            if k == 0:
                dbg("qphi_a", QPHI[:, 0:4096])

            for sub in range(4):
                for bk in range(4):
                    use_bank(bk)
                if kv_first[0]:
                    use_bank(4)
                    use_bank(5)
                for sl in range(4):
                    eoff = 1024 + sl * 512
                    for ci in range(8):
                        mm(pb[sl][:, :],
                           XN[:, ci * 512 + sub * 128: ci * 512 + (sub + 1) * 128],
                           WQKV[:, ci * 3072 + eoff: ci * 3072 + eoff + 512],
                           ci == 0, ci == 7, inc=(ci == 7))
                kv_tk = (st, st.val)
                for sl in range(2):
                    pa, pbb = PHP[sl]
                    B.wait("scalar", st, kv_tk[1] - 3 + sl)
                    B.wait("scalar", sv, ph_war[sl])
                    B.op("scalar", lambda e, sl=sl, pa=pa: e.activation(
                        out=pa[:, :], in_=pb[sl][:, :], func=AF.Relu, scale=-1.0), ss, 1)
                    e_tk = B.op("scalar", lambda e, pa=pa, pbb=pbb: e.activation(
                        out=pbb[:, :], in_=pa[:, :], func=AF.Exp, scale=-1.0), ss, 1)
                    B.wait("vector", st, kv_tk[1] - 3 + sl)
                    B.wait("vector", ss, e_tk[1])
                    r1 = B.op("vector", lambda e, sl=sl, pa=pa: e.tensor_scalar(
                        out=pa[:, :], in0=pb[sl][:, :], scalar1=0.0, scalar2=None,
                        op0=ALU.max), sv, 1)
                    bank_ticket[sl] = r1
                    ad = B.op("vector", lambda e, sl=sl, pa=pa, pbb=pbb: e.tensor_add(
                        out=KTM[:, sl * 512:(sl + 1) * 512], in0=pbb[:, :], in1=pa[:, :]), sv, 1)
                    ph_war[sl] = ad[1]
                for sl in range(2):
                    B.wait("vector", st, kv_tk[1] - 1 + sl)
                    vtk2 = B.op("vector", lambda e, sl=sl: e.tensor_copy(
                        out=AP(VEXT, sl * 8 * 66, [[1056, 128], [66, 8], [1, 64]]),
                        in_=r3(pb[2 + sl][:, :], 8, 64)), sv, 1)
                    bank_ticket[2 + sl] = vtk2
                kv_ready = (sv, sv.val)
                if k + 1 < NTILES:
                    if sub == 0:
                        ln_a(k + 1, 0)
                    elif sub == 1:
                        ln_a(k + 1, 1)
                    elif sub == 2:
                        ln_b(k + 1)
                    else:
                        ln_c(k + 1)
                        xn_tks[k + 1] = ln_d(k + 1)
                B.wait("tensor", sv, kv_ready[1])
                first = kv_first[0]
                kv_first[0] = False
                last = (k == NTILES - 1 and sub == 3)
                for h in range(H):
                    p2 = h // 2
                    bk = 4 + p2 // 4
                    cb = (p2 % 4) * 65
                    pbase = (h % 2) * 64
                    mm(pb[bk][pbase:pbase + 64, cb:cb + 65],
                       KTM[:, h * 64:(h + 1) * 64],
                       VEXT[:, h * 66:h * 66 + 65],
                       first and (h % 8 == 0), last, inc=(h == H - 1))
        kvdone_tk = (st, st.val)

        B.wait("sync", st, kvdone_tk[1])
        B.op("sync", lambda e: e.dma_start(out=EALL[:, :], in_=eall_d), sd, 16)
        eall_tk = (sd, sd.val)
        B.wait("vector", st, kvdone_tk[1])
        for b in range(2):
            B.op("vector", lambda e, b=b: e.tensor_copy(
                out=KVSB[:, b * 260:(b + 1) * 260], in_=pb[4 + b][:, 0:260]), sv, 1)
        kvsb_tk = (sv, sv.val)
        bank_ticket[4] = kvsb_tk
        bank_ticket[5] = kvsb_tk
        B.wait("sync", sv, kvsb_tk[1])
        B.op("sync", lambda e: e.dma_start(out=cck_in.ap(), in_=KVSB[:, :]), sd, 16)
        B.wait("gpsimd", sd, sd.val)
        B.op("gpsimd", lambda e: e.collective_compute(
            "AllReduce", ALU.add, replica_groups=[list(range(NCORES))],
            ins=[cck_in.ap()], outs=[cck_out.ap()]), scc, 1)
        kv_ar = (scc, scc.val)
        B.wait("gpsimd", st, kvdone_tk[1])
        B.op("gpsimd", lambda e: e.dma_start(out=WPROJ[:, :], in_=wproj_d), sg, 16)
        wproj_tk = (sg, sg.val)
        B.wait("gpsimd", kv_ar[0], kv_ar[1])
        B.wait("gpsimd", ksf_zero_tk[0], ksf_zero_tk[1])
        B.op("gpsimd", lambda e: e.dma_start(
            out=AP(KSF, 0, [[128, 64], [18, 8], [1, 1]]),
            in_=AP(cck_out, 64, [[520, 64], [65, 8], [1, 1]]),
            allow_slow_non_contiguous=True), sg, 16)
        B.op("gpsimd", lambda e: e.dma_start(
            out=AP(KSF, 64 * 128 + 1, [[128, 64], [18, 8], [1, 1]]),
            in_=AP(cck_out, 64 * 520 + 64, [[520, 64], [65, 8], [1, 1]]),
            allow_slow_non_contiguous=True), sg, 16)
        ksf_tk = (sg, sg.val)
        B.op("gpsimd", lambda e: e.dma_start(out=KVBF[:, :], in_=cck_out.ap()), sg, 16)
        kvpost_tk = (sg, sg.val)
        dbg("kvsb_a", KVSB[:, :])

        B.wait("tensor", ksf_tk[0], ksf_tk[1])
        B.wait("tensor", eall_tk[0], eall_tk[1])
        kv_waited = [False]
        for k in range(NTILES):
            xo = xo_of(k)
            use_bank(6)
            for ci in range(8):
                mm(pb[6][0:16, :], KSF[:, ci * 16:(ci + 1) * 16],
                   QPHI[:, k * 4096 + ci * 512: k * 4096 + (ci + 1) * 512],
                   ci == 0, ci == 7, inc=(ci == 7))
            n_tk = (st, st.val)
            B.wait("vector", st, n_tk[1])
            nv = B.op("vector", lambda e: e.tensor_scalar(
                out=NRS[:, :], in0=pb[6][0:16, :], scalar1=EPS_AT, scalar2=None,
                op0=ALU.add), sv, 1)
            bank_ticket[6] = nv
            B.wait("scalar", sv, nv[1])
            B.op("scalar", lambda e: e.activation(
                out=REC16[:, :], in_=NRS[:, :], func=AF.Ln), ss, 1)
            rec_tk = B.op("scalar", lambda e: e.activation(
                out=REC16[:, :], in_=REC16[:, :], func=AF.Exp, scale=-1.0), ss, 1)
            B.wait("tensor", ss, rec_tk[1])
            if not kv_waited[0]:
                kv_waited[0] = True
                B.wait("tensor", kvpost_tk[0], kvpost_tk[1])
                B.wait("tensor", wproj_tk[0], wproj_tk[1])
            for j in range(8):
                bko = j % 2
                bkr = 2 + (j % 2)
                use_bank(bko)
                use_bank(bkr)
                mm(pb[bko][0:64, :], KVBF[0:64, j * 65:j * 65 + 64],
                   QPHI[0:64, k * 4096 + j * 512: k * 4096 + (j + 1) * 512], True, True)
                mm(pb[bko][64:128, :], KVBF[64:128, j * 65:j * 65 + 64],
                   QPHI[64:128, k * 4096 + j * 512: k * 4096 + (j + 1) * 512],
                   True, True, inc=True)
                ro = mm(pb[bkr][:, :], EALL[:, j * 128:(j + 1) * 128], REC16[:, :],
                        True, True, inc=True)
                rb = PHA if (j % 2 == 0) else PHB
                B.wait("scalar", st, ro[1])
                B.wait("scalar", sv, sv.val)
                rc = B.op("scalar", lambda e, bkr=bkr, rb=rb: e.copy(
                    out=rb[:, :], in_=pb[bkr][:, :]), ss, 1)
                bank_ticket[bkr] = rc
                B.wait("vector", st, ro[1])
                B.wait("vector", ss, rc[1])
                dv = B.op("vector", lambda e, j=j, bko=bko, rb=rb: e.tensor_mul(
                    out=OUTPRE[:, j * 512:(j + 1) * 512], in0=pb[bko][:, :],
                    in1=rb[:, :]), sv, 1)
                bank_ticket[bko] = dv
            op_tk = (sv, sv.val)
            if k == 0:
                dbg("outpre_a", OUTPRE[:, :])
            B.wait("tensor", sv, op_tk[1])
            for oj in range(8):
                bk = 5 + (oj % 2)
                use_bank(bk)
                for ci in range(8):
                    mm(pb[bk][:, :],
                       WPROJ[:, ci * 1024 + oj * 128: ci * 1024 + (oj + 1) * 128],
                       OUTPRE[:, ci * 512:(ci + 1) * 512], ci == 0, ci == 7, inc=(ci == 7))
                mtk = (st, st.val)
                B.wait("vector", st, mtk[1])
                ev = B.op("vector", lambda e, oj=oj, bk=bk, xo=xo: e.scalar_tensor_tensor(
                    out=XR[:, xo + oj * 512: xo + (oj + 1) * 512], in0=pb[bk][:, :],
                    scalar=BIASC[:, oj:oj + 1],
                    in1=XR[:, xo + oj * 512: xo + (oj + 1) * 512],
                    op0=ALU.add, op1=ALU.add), sv, 1)
                bank_ticket[bk] = ev

    B.wait("vector", sd, x_tk[1])
    B.wait("vector", sv, cells_tk[1])
    B.wait("tensor", sg, wffn_tk[1])
    emit_ffn(0)
    if stage == 1:
        emit_store_xr()
        return end_block()

    B.wait("gpsimd", st, st.val)
    B.wait("gpsimd", sv, sv.val)
    B.op("gpsimd", lambda e: e.dma_start(out=WQKV[:, :], in_=wqkv_d), sg, 16)
    wqkv_tk = (sg, sg.val)
    for eng in Builder.ENGS:
        for sem in (st, sv, ss):
            B.wait(eng, sem)
    emit_attn(wqkv_tk)
    if stage == 2:
        emit_store_xr()
        return end_block()

    B.wait("gpsimd", st, st.val)
    B.wait("gpsimd", sv, sv.val)
    B.op("gpsimd", lambda e: e.dma_start(out=W1SB[:, :], in_=w1_d), sg, 16)
    w1_tk2 = (sg, sg.val)
    B.op("gpsimd", lambda e: e.dma_start(out=W2SB[:, :], in_=w2_d), sg, 16)
    B.op("gpsimd", lambda e: e.dma_start(out=W3SB[:, :], in_=w3_d), sg, 16)
    B.op("gpsimd", lambda e: e.dma_start(out=W4SB[:, :], in_=w4_d), sg, 16)
    B.op("gpsimd", lambda e: e.dma_start(out=BROWS[:, :], in_=brows_d), sg, 16)
    B.op("gpsimd", lambda e: e.dma_start(out=GW1SB[:, :], in_=gw1_d), sg, 16)
    B.op("gpsimd", lambda e: e.dma_start(out=GW2SB[:, :], in_=gw2_d), sg, 16)
    B.op("gpsimd", lambda e: e.dma_start(out=GW3SB[:, :], in_=gw3_d), sg, 16)
    gw_tk[1] = (sg, sg.val)
    for eng in Builder.ENGS:
        for sem in (st, sv, ss):
            B.wait(eng, sem)
    emit_ffn(1, w1_tk2=w1_tk2)
    return end_block()


def _images(inputs):
    f32 = np.float32
    x = np.asarray(inputs["L0_attn_x"], f32).reshape(TFULL, C)
    Wqkv = np.asarray(inputs["L0_attn_Wqkv"], f32)
    Wproj = np.asarray(inputs["L0_attn_Wproj"], f32)
    bproj = np.asarray(inputs["L0_attn_bproj"], f32)
    w = {i: np.asarray(inputs["L0_ffn_cn_w%d" % i], f32) for i in (1, 2, 3, 4)}
    b = {i: np.asarray(inputs["L0_ffn_cn_b%d" % i], f32) for i in (1, 2, 3, 4)}
    gw = {i: np.asarray(inputs["L0_ffn_gc_w%d" % i], f32) for i in (1, 2, 3)}
    gb = {i: np.asarray(inputs["L0_ffn_gc_b%d" % i], f32) for i in (1, 2, 3)}
    ar = np.asarray(inputs["L0_ffn_adaptive_range"], f32)

    def img(wm, kch):
        K, O = wm.shape
        return np.ascontiguousarray(
            wm.reshape(kch, 128, O).transpose(1, 0, 2).reshape(128, kch * O))

    com = {
        "w1": img(w[1], 8), "w2": img(w[2], 8), "w3": img(w[3], 4),
        "w4": np.ascontiguousarray(w[4].reshape(2, 128).T),
        "gw1": img(gw[1], 8), "gw2": img(gw[2], 4),
        "gw3": np.ascontiguousarray(gw[3].reshape(2, 128).T),
        "wqkv": img(np.ascontiguousarray(Wqkv.T), 8),
        "wproj": img(np.ascontiguousarray(Wproj.T), 8),
    }
    biasc = np.zeros((128, 16), f32)
    biasc[:, 0:8] = bproj.reshape(8, 128).T
    biasc[:, 8:16] = b[1].reshape(8, 128).T
    com["biasc"] = biasc
    brows = np.zeros((1, 2560), f32)
    brows[0, 0:1024] = b[1]
    brows[0, 1024:1536] = b[2]
    brows[0, 1536:1792] = b[3]
    brows[0, 1792:2304] = gb[1]
    brows[0, 2304:2560] = gb[2]
    com["brows"] = brows
    com["scal"] = np.array([[float(ar[0]), float(b[4][0]), float(gb[3][0]), 0.0]], f32)
    eall = np.zeros((16, 1024), f32)
    for j in range(8):
        eall[2 * j, j * 128: j * 128 + 64] = 1.0
        eall[2 * j + 1, j * 128 + 64: (j + 1) * 128] = 1.0
    com["eall"] = eall

    per_core = []
    for i in range(NCORES):
        sl = x[i * T:(i + 1) * T]
        xi = np.ascontiguousarray(
            sl.reshape(NTILES, TT, NCH, 128).transpose(3, 0, 2, 1).reshape(128, NCH * T))
        m = dict(com)
        m["x"] = xi
        per_core.append(m)
    return per_core


def kernel(**inputs):
    nc = build_program()
    in_maps = _images(inputs)
    res = run_bass_kernel_spmd(nc, in_maps, core_ids=list(range(NCORES)))
    out = np.empty((TFULL, C), np.float32)
    for i in range(NCORES):
        oi = np.asarray(res.results[i]["out"])
        out[i * T:(i + 1) * T] = (
            oi.reshape(128, NTILES, NCH, TT).transpose(1, 3, 2, 0).reshape(T, C))
    return out.reshape(1, TFULL, C)


# version 24
# speedup vs baseline: 1.0530x; 1.0424x over previous
import numpy as np
import concourse.bass as bass
import concourse.mybir as mybir
from concourse.ap import AP
from concourse.bass_utils import run_bass_kernel_spmd

F32 = mybir.dt.float32
BF16 = mybir.dt.bfloat16
AF = mybir.ActivationFunctionType
ALU = mybir.AluOpType

NCORES = 8
TFULL = 16384
C = 1024
NCH = 8
T = TFULL // NCORES
TT = 512
NTILES = T // TT
H = 16
EPS_LN = 1e-5
EPS_AT = 1e-6

SGN = [1.0, 1.0, -1.0, -1.0, -1.0, -1.0, 1.0, 1.0]
SIG = lambda j: (j + 4) % 8
XBFN_SLOT = {6: 0, 7: 1, 0: 2, 1: 3}


class Sem:
    def __init__(self, nc, name):
        self.h = nc.alloc_semaphore(name)
        self.val = 0


class Builder:
    ENGS = ("sync", "gpsimd", "vector", "scalar", "tensor")

    def __init__(self):
        self.ops = {e: [] for e in self.ENGS}

    def op(self, eng, fn, inc=None, n=1):
        if inc is not None:
            inc.val += n
            def run(e, fn=fn, sem=inc.h, n=n):
                fn(e).then_inc(sem, n)
            self.ops[eng].append(run)
            return (inc, inc.val)
        self.ops[eng].append(lambda e, fn=fn: fn(e))
        return None

    def wait(self, eng, sem, val=None):
        v = sem.val if val is None else val
        if v <= 0:
            return
        self.ops[eng].append(lambda e, s=sem.h, v=v: e.wait_ge(s, v))

    def wait_t(self, eng, ticket):
        if ticket is not None:
            self.wait(eng, ticket[0], ticket[1])


def bcast3(tile_ap, reps, n):
    return AP(tile_ap.tensor, tile_ap.offset,
              [[tile_ap.ap[0][0], tile_ap.ap[0][1]], [0, reps], [1, n]])


def r3(ap2d, c, n):
    return ap2d.rearrange("p (c n) -> p c n", c=c, n=n)


def build_program(stage=3, dump=()):
    nc = bass.Bass("TRN2", target_bir_lowering=False, detect_race_conditions=False)

    x_d = nc.dram_tensor("x", [128, NCH * T], F32, kind="ExternalInput").ap()
    out_d = nc.dram_tensor("out", [128, NCH * T], F32, kind="ExternalOutput").ap()
    w1_d = nc.dram_tensor("w1", [128, 8192], F32, kind="ExternalInput").ap()
    w2_d = nc.dram_tensor("w2", [128, 4096], F32, kind="ExternalInput").ap()
    w3_d = nc.dram_tensor("w3", [128, 1024], F32, kind="ExternalInput").ap()
    w4_d = nc.dram_tensor("w4", [128, 2], F32, kind="ExternalInput").ap()
    gw1_d = nc.dram_tensor("gw1", [128, 4096], F32, kind="ExternalInput").ap()
    gw2_d = nc.dram_tensor("gw2", [128, 1024], F32, kind="ExternalInput").ap()
    gw3_d = nc.dram_tensor("gw3", [128, 2], F32, kind="ExternalInput").ap()
    wqkv_d = nc.dram_tensor("wqkv", [128, 24576], F32, kind="ExternalInput").ap()
    wproj_d = nc.dram_tensor("wproj", [128, 8192], F32, kind="ExternalInput").ap()
    biasc_d = nc.dram_tensor("biasc", [128, 16], F32, kind="ExternalInput").ap()
    brows_d = nc.dram_tensor("brows", [1, 2560], F32, kind="ExternalInput").ap()
    scal_d = nc.dram_tensor("scal", [1, 4], F32, kind="ExternalInput").ap()
    eall_d = nc.dram_tensor("eall", [16, 1024], F32, kind="ExternalInput").ap()
    ccg_in = nc.dram_tensor("ccg_in", [128, 8], F32, kind="Internal")
    ccg_out = nc.dram_tensor("ccg_out", [128, 8], F32, kind="Internal")
    cck_in = nc.dram_tensor("cck_in", [128, 520], F32, kind="Internal")
    cck_out = nc.dram_tensor("cck_out", [128, 520], F32, kind="Internal")
    ccs_in = nc.dram_tensor("ccs_in", [128, 8], F32, kind="Internal")
    ccs_out = nc.dram_tensor("ccs_out", [128, 8], F32, kind="Internal")

    base = nc.sbuf_base
    cur = [base]
    offs = {}

    def alloc(name, shape, dtype, at=None):
        nbytes = int(np.prod(shape[1:])) * (4 if dtype == F32 else 2)
        if at is None:
            o = (cur[0] + 63) & ~63
            cur[0] = o + nbytes
        else:
            o = at
        t = nc.alloc_sbuf_tensor_at(name, shape, dtype, offset=o)
        offs[name] = o
        return t

    XR = alloc("XR", [128, NCH * T], F32)
    TROW = alloc("TROW", [1, 3072], F32)
    SPROW = alloc("SPROW", [1, 2048], F32)
    CELLS = alloc("CELLS", [128, 16], F32)
    BIASC = alloc("BIASC", [128, 16], F32)
    ONESB = alloc("ONESB", [1, 512], BF16)
    ONESF = alloc("ONESF", [1, 128], F32)
    ONE1B = alloc("ONE1B", [1, 32], BF16)
    ONECB = alloc("ONECB", [128, 1], BF16)
    PERS_END = cur[0]

    cur[0] = PERS_END
    W1SB = alloc("W1SB", [128, 8192], BF16)
    W2SB = alloc("W2SB", [128, 4096], BF16)
    W3SB = alloc("W3SB", [128, 1024], BF16)
    W4SB = alloc("W4SB", [128, 2], BF16)
    BROWS = alloc("BROWS", [1, 2560], BF16)
    CBROW = alloc("CBROW", [1, 512], BF16)
    CBROW2 = alloc("CBROW2", [1, 512], BF16)
    GCP = alloc("GCP", [128, 32], F32)
    GCN = alloc("GCN", [128, 8], F32)
    GCNB = alloc("GCNB", [128, 8], BF16)
    GH1 = alloc("GH1", [128, 4], BF16)
    GH2 = alloc("GH2", [128, 2], BF16)
    MROW = alloc("MROW", [1, 2048], F32)
    RROW = alloc("RROW", [1, 2048], F32)
    XBF = alloc("XBF", [128, 4096], BF16)
    XBFN = alloc("XBFN", [128, 2048], BF16)
    GPAIR = [alloc("G_a", [128, 4096], BF16), alloc("G_b", [128, 4096], BF16)]
    HPAIR = [alloc("H_a", [128, 4096], BF16), alloc("H_b", [128, 4096], BF16)]
    SCR = []
    for pi, par in enumerate(("a", "b")):
        s0 = (cur[0] + 63) & ~63
        d = {
            "H1G": alloc("H1G_" + par, [128, 4096], BF16, at=s0),
            "H2G": alloc("H2G_" + par, [128, 2048], BF16, at=s0 + 8192),
            "H3G": alloc("H3G_" + par, [128, 1024], BF16, at=s0 + 12288),
            "CBC": alloc("CBC_" + par, [128, 512], BF16, at=offs["XBFN"] + pi * 1024),
        }
        cur[0] = s0 + 14336
        SCR.append(d)
    GW1SB = alloc("GW1SB", [128, 4096], BF16, at=offs["H1G_a"])
    GW2SB = alloc("GW2SB", [128, 1024], BF16, at=offs["H2G_a"])
    GW3SB = alloc("GW3SB", [128, 2], BF16, at=offs["H3G_a"])
    QFIN = alloc("QFIN", [128, 4096], F32, at=offs["H1G_a"])
    TMP1 = alloc("TMP1", [128, 4096], BF16, at=offs["XBF"])
    FFN_END = cur[0]

    cur[0] = PERS_END
    WQKV = alloc("WQKV", [128, 24576], BF16)
    WPROJ = alloc("WPROJ", [128, 8192], BF16, at=offs["WQKV"])
    QPHI = alloc("QPHI", [128, NCH * T], BF16)
    XNBF = alloc("XNBF", [128, 4096], BF16)
    XNBF2 = alloc("XNBF2", [128, 4096], BF16)
    KTM = alloc("KTM", [128, 1024], BF16)
    VEXT = alloc("VEXT", [128, 1056], BF16)
    PHA = alloc("PHA", [128, 512], BF16)
    PHB = alloc("PHB", [128, 512], BF16)
    PHA2 = alloc("PHA2", [128, 512], BF16)
    PHB2 = alloc("PHB2", [128, 512], BF16)
    OUTPRE = alloc("OUTPRE", [128, 4096], BF16)
    KVSB = alloc("KVSB", [128, 520], F32)
    KVBF = alloc("KVBF", [128, 520], BF16)
    NRS = alloc("NRS", [16, 512], F32)
    REC16 = alloc("REC16", [16, 512], F32)
    EALL = alloc("EALL", [16, 1024], F32, at=offs["KTM"])
    KS8 = alloc("KS8", [128, 8], F32)
    KSF = alloc("KSF", [128, 128], BF16)
    ATTN_END = cur[0]

    cap = nc.sbuf_top
    assert FFN_END <= cap, (FFN_END, cap)
    assert ATTN_END <= cap, (ATTN_END, cap)

    pb = [nc.alloc_psum_tensor("pb%d" % i, [128, 512], F32) for i in range(8)]

    sd = Sem(nc, "sd")
    sg = Sem(nc, "sg")
    sv = Sem(nc, "sv")
    ss = Sem(nc, "ss")
    st = Sem(nc, "st")
    scc = Sem(nc, "scc")
    so = Sem(nc, "so")

    B = Builder()
    bank_ticket = [None] * 8

    def use_bank(i):
        B.wait_t("tensor", bank_ticket[i])

    def dbg(tag, ap, dtype=None):
        if tag not in dump:
            return
        dt = ap.tensor.dtype if dtype is None else dtype
        dd = nc.dram_tensor("dbg_" + tag, list(ap.shape), dt, kind="ExternalOutput").ap()
        for sem in (st, sv, ss):
            B.wait("sync", sem)
        B.op("sync", lambda e, dd=dd, ap=ap: e.dma_start(out=dd, in_=ap), sd, 16)
        for eng in ("vector", "scalar", "tensor"):
            B.wait(eng, sd)

    C_AR, C_B4, C_GB3, C_EPS, C_HB4, C_HGB3, C_K1, C_K2, C_C3, C_GTH, C_GF, C_C0 = range(12)

    def cell(i):
        return CELLS[0:1, i:i + 1]

    R_C, R_A, R_B, R_T1, R_T2, R_T3 = (0, 512, 1024, 1536, 2048, 2560)

    def trow(slot):
        return TROW[0:1, slot:slot + 512]

    def mm(out, lhsT, rhs, start, stop, inc=False):
        if inc:
            return B.op("tensor", lambda e: e.matmul(out=out, lhsT=lhsT, rhs=rhs,
                                                     start=start, stop=stop), st, 1)
        return B.op("tensor", lambda e: e.matmul(out=out, lhsT=lhsT, rhs=rhs,
                                                 start=start, stop=stop))

    def end_block():
        B.wait("gpsimd", so, so.val)
        B.wait("sync", so, so.val)
        with nc.Block() as block:
            for eng in Builder.ENGS:
                fns = B.ops[eng]
                def body(e, fns=fns):
                    for f in fns:
                        f(e)
                getattr(block, eng)(body)
        return nc

    def xo_of(k):
        return k * 4096

    def emit_store_xr():
        for k in range(NTILES):
            xo = xo_of(k)
            B.wait("sync", sv, sv.val)
            B.wait("sync", ss, ss.val)
            B.op("sync", lambda e, xo=xo: e.dma_start(
                out=AP(out_d.tensor, xo, [[NCH * T, 128], [1, 4096]]),
                in_=XR[:, xo:xo + 4096]), so, 16)

    B.op("sync", lambda e: e.dma_start(out=XR[:, :], in_=x_d), sd, 16)
    x_tk = (sd, sd.val)
    B.op("sync", lambda e: e.dma_start(out=BIASC[:, :], in_=biasc_d), sd, 16)
    B.op("sync", lambda e: e.dma_start(out=CELLS[0:1, 0:4], in_=scal_d), sd, 16)
    init_d = (sd, sd.val)
    B.op("gpsimd", lambda e: e.dma_start(out=W1SB[:, :], in_=w1_d), sg, 16)
    B.op("gpsimd", lambda e: e.dma_start(out=W2SB[:, :], in_=w2_d), sg, 16)
    B.op("gpsimd", lambda e: e.dma_start(out=W3SB[:, :], in_=w3_d), sg, 16)
    B.op("gpsimd", lambda e: e.dma_start(out=W4SB[:, :], in_=w4_d), sg, 16)
    B.op("gpsimd", lambda e: e.dma_start(out=BROWS[:, :], in_=brows_d), sg, 16)
    wffn_tk = (sg, sg.val)
    B.op("gpsimd", lambda e: e.dma_start(out=GW1SB[:, :], in_=gw1_d), sg, 16)
    B.op("gpsimd", lambda e: e.dma_start(out=GW2SB[:, :], in_=gw2_d), sg, 16)
    B.op("gpsimd", lambda e: e.dma_start(out=GW3SB[:, :], in_=gw3_d), sg, 16)
    gw_tk = {0: (sg, sg.val)}
    B.op("vector", lambda e: e.memset(ONESB[:, :], 1.0))
    B.op("vector", lambda e: e.memset(ONESF[:, :], 1.0))
    B.op("vector", lambda e: e.memset(ONE1B[:, :], 1.0))
    B.op("vector", lambda e: e.memset(ONECB[:, :], 1.0))
    B.op("vector", lambda e: e.memset(CELLS[:, C_EPS:C_EPS + 1], EPS_LN), sv, 1)
    B.wait("vector", sd, init_d[1])
    B.op("vector", lambda e: e.tensor_scalar(out=cell(C_HB4), in0=cell(C_B4),
                                             scalar1=0.5, scalar2=None, op0=ALU.mult))
    B.op("vector", lambda e: e.tensor_scalar(out=cell(C_HGB3), in0=cell(C_GB3),
                                             scalar1=0.5, scalar2=None, op0=ALU.mult))
    B.op("vector", lambda e: e.tensor_scalar(out=cell(C_K1), in0=cell(C_AR),
                                             scalar1=0.3, scalar2=None, op0=ALU.mult))
    B.op("vector", lambda e: e.tensor_scalar(out=cell(C_K2), in0=cell(C_AR),
                                             scalar1=1.4, scalar2=None, op0=ALU.mult))
    B.op("vector", lambda e: e.tensor_scalar(out=cell(C_C3), in0=cell(C_AR),
                                             scalar1=-0.7, scalar2=0.1,
                                             op0=ALU.mult, op1=ALU.add), sv, 1)
    cells_tk = (sv, sv.val)

    def emit_ffn(phase, w1_tk2=None):
        SQB = GPAIR[0]
        cp = [None] * (NTILES + 1)
        cp[0] = B.op("vector", lambda e: e.tensor_copy(
            out=XBF[:, :], in_=XR[:, 0:4096]), sv, 1)
        for k in range(NTILES):
            xo = xo_of(k)
            use_bank(6)
            B.wait("tensor", cp[k][0], cp[k][1])
            for ci in range(8):
                mm(pb[6][0:1, :], ONECB[:, :], XBF[:, ci * 512:(ci + 1) * 512],
                   ci == 0, ci == 7, inc=(ci == 7))
            s1_tk = (st, st.val)
            B.wait("scalar", st, s1_tk[1])
            B.wait("scalar", sv, sv.val)
            stk = B.op("scalar", lambda e: e.activation(
                out=SQB[:, :], in_=XBF[:, :], func=AF.Square), ss, 1)
            use_bank(7)
            B.wait("tensor", ss, stk[1])
            for ci in range(8):
                mm(pb[7][0:1, :], ONECB[:, :], SQB[:, ci * 512:(ci + 1) * 512],
                   ci == 0, ci == 7, inc=(ci == 7))
            s2_tk = (st, st.val)
            if k + 1 < NTILES:
                xo2 = xo_of(k + 1)
                B.wait("vector", st, s1_tk[1])
                B.wait("vector", ss, stk[1])
                cp[k + 1] = B.op("vector", lambda e, xo2=xo2: e.tensor_copy(
                    out=XBF[:, :], in_=XR[:, xo2:xo2 + 4096]), sv, 1)
            B.op("scalar", lambda e, k=k: e.mul(
                out=MROW[0:1, k * 512:(k + 1) * 512], in_=pb[6][0:1, :], mul=1.0 / C), ss, 1)
            B.op("scalar", lambda e: e.activation(
                out=trow(R_T1), in_=pb[6][0:1, :], func=AF.Square, scale=1.0 / C), ss, 1)
            msq_tk = (ss, ss.val)
            bank_ticket[6] = msq_tk
            B.wait("vector", ss, msq_tk[1])
            B.wait("vector", st, s2_tk[1])
            vv = B.op("vector", lambda e: e.scalar_tensor_tensor(
                out=trow(R_T2), in0=pb[7][0:1, :], scalar=1.0 / C, in1=trow(R_T1),
                op0=ALU.mult, op1=ALU.subtract), sv, 1)
            bank_ticket[7] = vv
            B.wait("scalar", sv, vv[1])
            B.op("scalar", lambda e: e.activation(
                out=trow(R_T3), in_=trow(R_T2), func=AF.Ln, bias=cell(C_EPS)), ss, 1)
            rtk = B.op("scalar", lambda e, k=k: e.activation(
                out=RROW[0:1, k * 512:(k + 1) * 512], in_=trow(R_T3), func=AF.Exp,
                scale=-0.5), ss, 1)
            use_bank(0)
            use_bank(1)
            B.wait("tensor", ss, rtk[1])
            mm(pb[0][:, :], ONESF[:, :], MROW[0:1, k * 512:(k + 1) * 512], True, True, inc=True)
            bc = mm(pb[1][:, :], ONESF[:, :], RROW[0:1, k * 512:(k + 1) * 512], True, True, inc=True)
            B.wait("vector", st, bc[1])
            B.op("vector", lambda e, xo=xo: e.tensor_sub(
                out=SQB[:, :], in0=XR[:, xo:xo + 4096], in1=bcast3(pb[0].ap(), 8, 512)), sv, 1)
            for ci in range(8):
                B.op("vector", lambda e, ci=ci, k=k: e.scalar_tensor_tensor(
                    out=SQB[:, ci * 512:(ci + 1) * 512],
                    in0=SQB[:, ci * 512:(ci + 1) * 512], scalar=1.0,
                    in1=pb[1][:, :], op0=ALU.bypass, op1=ALU.mult,
                    accum_out=GCP[:, k * 8 + ci:k * 8 + ci + 1]), sv, 1)
            gc_tk = (sv, sv.val)
            bank_ticket[0] = gc_tk
            bank_ticket[1] = gc_tk

        B.op("vector", lambda e: e.tensor_add(out=GCN[:, :], in0=GCP[:, 0:8], in1=GCP[:, 8:16]))
        B.op("vector", lambda e: e.tensor_add(out=GCN[:, :], in0=GCN[:, :], in1=GCP[:, 16:24]))
        B.op("vector", lambda e: e.tensor_add(out=GCN[:, :], in0=GCN[:, :], in1=GCP[:, 24:32]), sv, 1)
        B.wait("sync", sv, sv.val)
        B.op("sync", lambda e: e.dma_start(out=ccg_in.ap(), in_=GCN[:, :]), sd, 16)
        B.wait("gpsimd", sd, sd.val)
        B.op("gpsimd", lambda e: e.collective_compute(
            "AllReduce", ALU.add, replica_groups=[list(range(NCORES))],
            ins=[ccg_in.ap()], outs=[ccg_out.ap()]), scc, 1)
        B.wait("sync", scc, scc.val)
        B.op("sync", lambda e: e.dma_start(out=GCN[:, :], in_=ccg_out.ap()), sd, 16)
        ar_done = (sd, sd.val)

        def emit_prep(k, par):
            xo = xo_of(k)
            B.wait("vector", st, st.val)
            use_bank(0)
            use_bank(1)
            mm(pb[0][:, :], ONESF[:, :], MROW[0:1, k * 512:(k + 1) * 512], True, True, inc=True)
            bc = mm(pb[1][:, :], ONESF[:, :], RROW[0:1, k * 512:(k + 1) * 512], True, True, inc=True)
            B.wait("vector", st, bc[1])
            B.op("vector", lambda e, xo=xo: e.tensor_sub(
                out=XBF[:, :], in0=XR[:, xo:xo + 4096], in1=bcast3(pb[0].ap(), 8, 512)), sv, 1)
            o0 = B.op("vector", lambda e: e.tensor_tensor(
                out=r3(XBF[:, :], 8, 512), in0=r3(XBF[:, :], 8, 512),
                in1=bcast3(pb[1].ap(), 8, 512), op=ALU.mult), sv, 1)
            bank_ticket[0] = o0
            bank_ticket[1] = o0
            B.op("vector", lambda e: e.tensor_scalar(
                out=XBFN[:, 0:1024], in0=XBF[:, 6 * 512:8 * 512], scalar1=-1.0,
                scalar2=None, op0=ALU.mult))
            on = B.op("vector", lambda e: e.tensor_scalar(
                out=XBFN[:, 1024:2048], in0=XBF[:, 0:1024], scalar1=-1.0,
                scalar2=None, op0=ALU.mult), sv, 1)
            if k == 0:
                dbg("out0_p%d" % phase, XBF[:, :])
            return o0, on

        def emit_g0h0(k, par, o0, on):
            GT, HT = GPAIR[par], HPAIR[par]
            B.wait("tensor", sv, o0[1])
            if phase == 1 and k == 0:
                B.wait("tensor", w1_tk2[0], w1_tk2[1])
            for oj in range(8):
                bk = 2 + (oj % 2)
                use_bank(bk)
                for ci in range(8):
                    mm(pb[bk][:, :],
                       W1SB[:, ci * 1024 + oj * 128: ci * 1024 + (oj + 1) * 128],
                       XBF[:, ci * 512:(ci + 1) * 512], ci == 0, ci == 7, inc=(ci == 7))
                mtk = (st, st.val)
                B.wait("scalar", st, mtk[1])
                etk = B.op("scalar", lambda e, oj=oj, bk=bk, GT=GT: e.copy(
                    out=GT[:, oj * 512:(oj + 1) * 512], in_=pb[bk][:, :]), ss, 1)
                bank_ticket[bk] = etk
            B.wait("tensor", sv, on[1])
            for oj in range(8):
                bk = 2 + (oj % 2)
                use_bank(bk)
                for ci in range(8):
                    if SGN[ci] > 0:
                        rhs = XBF[:, SIG(ci) * 512:(SIG(ci) + 1) * 512]
                    else:
                        sl = XBFN_SLOT[SIG(ci)]
                        rhs = XBFN[:, sl * 512:(sl + 1) * 512]
                    mm(pb[bk][:, :],
                       W1SB[:, ci * 1024 + oj * 128: ci * 1024 + (oj + 1) * 128],
                       rhs, ci == 0, ci == 7, inc=(ci == 7))
                mtk = (st, st.val)
                B.wait("vector", st, mtk[1])
                etk = B.op("vector", lambda e, oj=oj, bk=bk, HT=HT: e.tensor_copy(
                    out=HT[:, oj * 512:(oj + 1) * 512], in_=pb[bk][:, :]), sv, 1)
                bank_ticket[bk] = etk
            if k == 0:
                dbg("G0_p%d" % phase, GT[:, :])
                dbg("H0_p%d" % phase, HT[:, :])

        def emit_gf_net():
            B.wait("vector", ar_done[0], ar_done[1])
            gcn_tk = B.op("vector", lambda e: e.tensor_scalar(
                out=GCNB[:, :], in0=GCN[:, :], scalar1=1.0 / TFULL, scalar2=None,
                op0=ALU.mult), sv, 1)
            use_bank(6)
            B.wait("tensor", sv, gcn_tk[1])
            B.wait("tensor", gw_tk[phase][0], gw_tk[phase][1])
            for oj in range(4):
                for ci in range(8):
                    mm(pb[6][:, oj:oj + 1],
                       GW1SB[:, ci * 512 + oj * 128: ci * 512 + (oj + 1) * 128],
                       GCNB[:, ci:ci + 1], ci == 0, False)
                mm(pb[6][:, oj:oj + 1], BROWS[0:1, 1792 + oj * 128:1792 + (oj + 1) * 128],
                   ONE1B[0:1, 0:1], False, True, inc=True)
            g1_tk = (st, st.val)
            B.wait("scalar", st, g1_tk[1])
            e1 = B.op("scalar", lambda e: e.activation(out=GH1[:, :], in_=pb[6][:, 0:4],
                                                       func=AF.Gelu), ss, 1)
            bank_ticket[6] = e1
            use_bank(7)
            B.wait("tensor", ss, e1[1])
            for oj in range(2):
                for ci in range(4):
                    mm(pb[7][:, oj:oj + 1],
                       GW2SB[:, ci * 256 + oj * 128: ci * 256 + (oj + 1) * 128],
                       GH1[:, ci:ci + 1], ci == 0, False)
                mm(pb[7][:, oj:oj + 1], BROWS[0:1, 2304 + oj * 128:2304 + (oj + 1) * 128],
                   ONE1B[0:1, 0:1], False, True, inc=True)
            g2_tk = (st, st.val)
            B.wait("scalar", st, g2_tk[1])
            e2 = B.op("scalar", lambda e: e.activation(out=GH2[:, :], in_=pb[7][:, 0:2],
                                                       func=AF.Gelu), ss, 1)
            bank_ticket[7] = e2
            use_bank(4)
            B.wait("tensor", ss, e2[1])
            for ci in range(2):
                mm(pb[4][0:1, 0:1], GW3SB[:, ci:ci + 1], GH2[:, ci:ci + 1],
                   ci == 0, ci == 1, inc=(ci == 1))
            g3_tk = (st, st.val)
            B.wait("scalar", st, g3_tk[1])
            e3 = B.op("scalar", lambda e: e.activation(
                out=cell(C_GTH), in_=pb[4][0:1, 0:1], func=AF.Tanh, scale=0.5,
                bias=cell(C_HGB3)), ss, 1)
            bank_ticket[4] = e3
            B.wait("vector", ss, e3[1])
            B.op("vector", lambda e: e.tensor_scalar(
                out=cell(C_GF), in0=cell(C_GTH), scalar1=0.5, scalar2=0.5,
                op0=ALU.mult, op1=ALU.add))
            cz = B.op("vector", lambda e: e.tensor_scalar(
                out=cell(C_C0), in0=cell(C_GF), scalar1=cell(C_K2), scalar2=cell(C_C3),
                op0=ALU.mult, op1=ALU.add), sv, 1)
            return cz

        RS = [{"c": R_C, "A": R_A, "B": R_B}, {"c": R_T1, "A": R_T2, "B": R_T3}]

        def rowp(par, nm):
            return trow(RS[par][nm])

        last_fin = [None]

        upd_st = [[None] * 4, [None] * 4]
        CBROWP = [CBROW, CBROW2]

        def emit_cycle(k, par, cyc, cz_tk):
            GT, HT = GPAIR[par], HPAIR[par]
            scr = SCR[par]
            bb = 0 if par == 0 else 4
            if cyc == 0:
                upd_st[par] = [None] * 4
            upd_tk = upd_st[par]
            if cyc == 0 and cz_tk is not None:
                B.wait("scalar", cz_tk[0], cz_tk[1])
            for g2 in range(4):
                c0_, c1_ = 2 * g2, 2 * g2 + 1
                if cyc == 0 and g2 == 0 and last_fin[0] is not None:
                    B.wait("scalar", last_fin[0][0], last_fin[0][1])
                if upd_tk[g2] is not None:
                    B.wait("scalar", sv, upd_tk[g2][1])
                for ci in (c0_, c1_):
                    B.op("scalar", lambda e, ci=ci, GT=GT, scr=scr: e.activation(
                        out=scr["H1G"][:, ci * 512:(ci + 1) * 512],
                        in_=GT[:, ci * 512:(ci + 1) * 512], func=AF.Gelu,
                        bias=BIASC[:, 8 + ci:9 + ci]),
                        ss if ci == c1_ else None, 1)
                g1tk = (ss, ss.val)
                B.wait("tensor", ss, g1tk[1])
                if g2 == 0:
                    for bk in range(bb, bb + 4):
                        use_bank(bk)
                for ci in (c0_, c1_):
                    for oj in range(4):
                        mm(pb[bb + oj][:, :],
                           W2SB[:, ci * 512 + oj * 128: ci * 512 + (oj + 1) * 128],
                           scr["H1G"][:, ci * 512:(ci + 1) * 512],
                           ci == 0, ci == 7 and oj == 3, inc=(ci == 7 and oj == 3))
            w2tk = (st, st.val)
            B.wait("scalar", st, w2tk[1])
            for oj in range(4):
                etk = B.op("scalar", lambda e, oj=oj, bb=bb, scr=scr: e.activation(
                    out=scr["H2G"][:, oj * 512:(oj + 1) * 512], in_=pb[bb + oj][:, :],
                    func=AF.Gelu), ss, 1)
                bank_ticket[bb + oj] = etk
            h2tk = (ss, ss.val)
            use_bank(bb)
            use_bank(bb + 1)
            B.wait("tensor", ss, h2tk[1])
            for ci in range(4):
                for oj in range(2):
                    mm(pb[bb + oj][:, :],
                       W3SB[:, ci * 256 + oj * 128: ci * 256 + (oj + 1) * 128],
                       scr["H2G"][:, ci * 512:(ci + 1) * 512],
                       ci == 0, ci == 3 and oj == 1, inc=(ci == 3 and oj == 1))
            w3tk = (st, st.val)
            B.wait("scalar", st, w3tk[1])
            for oj in range(2):
                etk = B.op("scalar", lambda e, oj=oj, bb=bb, scr=scr: e.activation(
                    out=scr["H3G"][:, oj * 512:(oj + 1) * 512], in_=pb[bb + oj][:, :],
                    func=AF.Gelu), ss, 1)
                bank_ticket[bb + oj] = etk
            h3tk = (ss, ss.val)
            use_bank(bb + 2)
            B.wait("tensor", ss, h3tk[1])
            for ci in range(2):
                mm(pb[bb + 2][0:1, :], W4SB[:, ci:ci + 1],
                   scr["H3G"][:, ci * 512:(ci + 1) * 512], ci == 0, ci == 1,
                   inc=(ci == 1))
            ztk = (st, st.val)
            B.wait("scalar", st, ztk[1])
            thtk = B.op("scalar", lambda e, par=par, bb=bb: e.activation(
                out=rowp(par, "c"), in_=pb[bb + 2][0:1, :], func=AF.Tanh, scale=0.5,
                bias=cell(C_HB4)), ss, 1)
            bank_ticket[bb + 2] = thtk
            B.wait("vector", ss, thtk[1])
            if cyc == 0 and cz_tk is not None:
                B.wait("vector", cz_tk[0], cz_tk[1])
            B.op("vector", lambda e, par=par: e.tensor_scalar(
                out=rowp(par, "c"), in0=rowp(par, "c"), scalar1=cell(C_K1),
                scalar2=cell(C_C0), op0=ALU.mult, op1=ALU.add), sv, 1)
            if cyc == 0:
                B.op("vector", lambda e, par=par: e.tensor_copy(
                    out=rowp(par, "B"), in_=rowp(par, "c")))
                B.op("vector", lambda e, par=par: e.memset(rowp(par, "A"), 1.0), sv, 1)
            else:
                B.op("vector", lambda e, par=par: e.tensor_mul(
                    out=SPROW[0:1, par * 512:(par + 1) * 512], in0=rowp(par, "c"),
                    in1=rowp(par, "B")))
                B.op("vector", lambda e, par=par: e.tensor_mul(
                    out=SPROW[0:1, 1024 + par * 512:1024 + (par + 1) * 512],
                    in0=rowp(par, "c"), in1=rowp(par, "A")))
                B.op("vector", lambda e, par=par: e.tensor_sub(
                    out=rowp(par, "A"), in0=rowp(par, "A"),
                    in1=SPROW[0:1, par * 512:(par + 1) * 512]))
                B.op("vector", lambda e, par=par: e.tensor_add(
                    out=rowp(par, "B"), in0=rowp(par, "B"),
                    in1=SPROW[0:1, 1024 + par * 512:1024 + (par + 1) * 512]), sv, 1)
            if cyc < 2:
                cbtk = B.op("vector", lambda e, par=par: e.tensor_copy(
                    out=CBROWP[par][:, :], in_=rowp(par, "c")), sv, 1)
                cbb = bb + 3
                use_bank(cbb)
                B.wait("tensor", sv, cbtk[1])
                bmk = mm(pb[cbb][:, :], ONESB[0:1, 0:128], CBROWP[par][:, :],
                         True, True, inc=True)
                B.wait("scalar", st, bmk[1])
                cbe = B.op("scalar", lambda e, cbb=cbb, scr=scr: e.copy(
                    out=scr["CBC"][:, :], in_=pb[cbb][:, :]), ss, 1)
                bank_ticket[cbb] = cbe
                B.wait("vector", ss, cbe[1])
                for g2 in range(4):
                    sl = slice(g2 * 1024, (g2 + 1) * 1024)
                    B.op("vector", lambda e, sl=sl, scr=scr, HT=HT: e.tensor_tensor(
                        out=r3(TMP1[:, sl], 2, 512), in0=r3(HT[:, sl], 2, 512),
                        in1=bcast3(scr["CBC"].ap(), 2, 512), op=ALU.mult), sv, 1)
                    if cyc == 0:
                        B.op("vector", lambda e, sl=sl, scr=scr, GT=GT: e.tensor_tensor(
                            out=r3(scr["H1G"][:, sl], 2, 512), in0=r3(GT[:, sl], 2, 512),
                            in1=bcast3(scr["CBC"].ap(), 2, 512), op=ALU.mult))
                        B.op("vector", lambda e, sl=sl, scr=scr, HT=HT: e.tensor_sub(
                            out=HT[:, sl], in0=HT[:, sl], in1=scr["H1G"][:, sl]))
                    gadd = B.op("vector", lambda e, sl=sl, GT=GT: e.tensor_add(
                        out=GT[:, sl], in0=GT[:, sl], in1=TMP1[:, sl]), sv, 1)
                    upd_tk[g2] = gadd

        def emit_final(k, par):
            xo = xo_of(k)
            bA, bB, bP, bM = (6, 7, 4, 5) if par == 0 else (2, 3, 0, 1)
            if k == 0:
                dbg("mr_p%d" % phase, MROW[0:1, :])
                dbg("rr_p%d" % phase, RROW[0:1, :])
            rr_ = RROW[0:1, k * 512:(k + 1) * 512]
            mr_ = MROW[0:1, k * 512:(k + 1) * 512]
            B.op("vector", lambda e, par=par, rr_=rr_: e.scalar_tensor_tensor(
                out=rowp(par, "A"), in0=rowp(par, "A"), scalar=0.5, in1=rr_,
                op0=ALU.mult, op1=ALU.mult))
            B.op("vector", lambda e, par=par, rr_=rr_: e.scalar_tensor_tensor(
                out=rowp(par, "B"), in0=rowp(par, "B"), scalar=0.5, in1=rr_,
                op0=ALU.mult, op1=ALU.mult))
            B.op("vector", lambda e, par=par, mr_=mr_: e.tensor_mul(
                out=rowp(par, "c"), in0=rowp(par, "A"), in1=mr_))
            B.op("vector", lambda e, par=par, mr_=mr_: e.tensor_mul(
                out=SPROW[0:1, 0:512], in0=rowp(par, "B"), in1=mr_))
            B.op("vector", lambda e, par=par: e.tensor_add(
                out=SPROW[0:1, 512:1024], in0=rowp(par, "c"), in1=SPROW[0:1, 0:512]))
            B.op("vector", lambda e, par=par: e.tensor_sub(
                out=rowp(par, "c"), in0=rowp(par, "c"), in1=SPROW[0:1, 0:512]))
            rows_tk = B.op("vector", lambda e, par=par: e.tensor_scalar(
                out=rowp(par, "A"), in0=rowp(par, "A"), scalar1=1.0, scalar2=None,
                op0=ALU.add), sv, 1)
            for bknum in (bA, bB, bP, bM):
                use_bank(bknum)
            B.wait("tensor", sv, rows_tk[1])
            mm(pb[bA][:, :], ONESF[:, :], rowp(par, "A"), True, True, inc=True)
            mm(pb[bB][:, :], ONESF[:, :], rowp(par, "B"), True, True, inc=True)
            mm(pb[bP][:, :], ONESF[:, :], SPROW[0:1, 512:1024], True, True, inc=True)
            fb = mm(pb[bM][:, :], ONESF[:, :], rowp(par, "c"), True, True, inc=True)
            B.wait("vector", st, fb[1])
            B.op("vector", lambda e, xo=xo, bB=bB: e.tensor_tensor(
                out=r3(QFIN[:, 0:2048], 4, 512), in0=r3(XR[:, xo + 2048:xo + 4096], 4, 512),
                in1=bcast3(pb[bB].ap(), 4, 512), op=ALU.mult))
            B.op("vector", lambda e, xo=xo, bB=bB: e.tensor_tensor(
                out=r3(QFIN[:, 2048:4096], 4, 512), in0=r3(XR[:, xo:xo + 2048], 4, 512),
                in1=bcast3(pb[bB].ap(), 4, 512), op=ALU.mult), sv, 1)
            B.op("vector", lambda e, xo=xo, bA=bA: e.tensor_tensor(
                out=r3(XR[:, xo:xo + 4096], 8, 512), in0=r3(XR[:, xo:xo + 4096], 8, 512),
                in1=bcast3(pb[bA].ap(), 8, 512), op=ALU.mult), sv, 1)
            for rq in range(4):
                opx = ALU.add if SGN[2 * rq] > 0 else ALU.subtract
                B.op("vector", lambda e, rq=rq, opx=opx, xo=xo: e.tensor_tensor(
                    out=XR[:, xo + rq * 1024: xo + (rq + 1) * 1024],
                    in0=XR[:, xo + rq * 1024: xo + (rq + 1) * 1024],
                    in1=QFIN[:, rq * 1024:(rq + 1) * 1024], op=opx), sv, 1)
            qdone = (sv, sv.val)
            last_fin[0] = qdone
            for rq in range(4):
                rbank = pb[bP] if SGN[2 * rq] > 0 else pb[bM]
                B.op("vector", lambda e, rq=rq, rbank=rbank, xo=xo: e.tensor_sub(
                    out=XR[:, xo + rq * 1024: xo + (rq + 1) * 1024],
                    in0=XR[:, xo + rq * 1024: xo + (rq + 1) * 1024],
                    in1=bcast3(rbank.ap(), 2, 512)), sv, 1)
            fin_tk = (sv, sv.val)
            for bknum in (bA, bB, bP, bM):
                bank_ticket[bknum] = fin_tk
            if phase == 1:
                B.wait("sync", sv, fin_tk[1])
                B.op("sync", lambda e, xo=xo: e.dma_start(
                    out=AP(out_d.tensor, xo, [[NCH * T, 128], [1, 4096]]),
                    in_=XR[:, xo:xo + 4096]), so, 16)

        o0a = emit_prep(0, 0)
        emit_g0h0(0, 0, *o0a)
        o0b = emit_prep(1, 1)
        emit_g0h0(1, 1, *o0b)
        cz_tk = emit_gf_net()
        for kp in (0, 2):
            for cyc in range(3):
                emit_cycle(kp, 0, cyc, cz_tk)
                if cyc == 2 and kp == 0:
                    o2 = emit_prep(2, 0)
                emit_cycle(kp + 1, 1, cyc, cz_tk)
            if kp == 0:
                emit_g0h0(2, 0, *o2)
                o3 = emit_prep(3, 1)
                emit_g0h0(3, 1, *o3)
            emit_final(kp, 0)
            emit_final(kp + 1, 1)

    def emit_attn(wqkv_tk):
        B.op("vector", lambda e: e.memset(VEXT[:, :], 0.0))
        for h in range(H):
            B.op("vector", lambda e, h=h: e.memset(VEXT[:, h * 66 + 64:h * 66 + 65], 1.0))
        B.op("vector", lambda e: e.memset(KSF[:, :], 0.0), sv, 1)
        vext_tk = (sv, sv.val)
        ksf_zero_tk = vext_tk

        kv_first = [True]
        XNP = [XNBF, XNBF2]
        PHP = [(PHA, PHB), (PHA2, PHB2)]
        xn_tks = [None] * NTILES
        ph_war = [0, 0]
        ln_st = {}

        def ln_a(k, half):
            xo = xo_of(k)
            XN = XNP[k % 2]
            B.wait("vector", st, st.val)
            cp = B.op("vector", lambda e, xo=xo, XN=XN, half=half: e.tensor_copy(
                out=XN[:, half * 2048:(half + 1) * 2048],
                in_=XR[:, xo + half * 2048: xo + (half + 1) * 2048]), sv, 1)
            if half == 0:
                ln_st[k] = {}
            ln_st[k]["cp%d" % half] = cp

        def ln_b(k):
            XN = XNP[k % 2]
            stt = ln_st[k]
            use_bank(6)
            B.wait("tensor", sv, stt["cp1"][1])
            for ci in range(8):
                mm(pb[6][0:1, :], ONECB[:, :], XN[:, ci * 512:(ci + 1) * 512],
                   ci == 0, ci == 7, inc=(ci == 7))
            stt["s1"] = (st, st.val)
            B.wait("scalar", sv, stt["cp1"][1])
            B.wait("scalar", st, stt["s1"][1])
            stt["sq"] = B.op("scalar", lambda e, XN=XN: e.activation(
                out=OUTPRE[:, :], in_=XN[:, :], func=AF.Square), ss, 1)

        def ln_c(k):
            stt = ln_st[k]
            use_bank(7)
            B.wait("tensor", ss, stt["sq"][1])
            for ci in range(8):
                mm(pb[7][0:1, :], ONECB[:, :], OUTPRE[:, ci * 512:(ci + 1) * 512],
                   ci == 0, ci == 7, inc=(ci == 7))
            s2 = (st, st.val)
            B.op("scalar", lambda e: e.mul(out=trow(R_T1), in_=pb[6][0:1, :], mul=1.0 / C), ss, 1)
            msq = B.op("scalar", lambda e: e.activation(
                out=trow(R_T3), in_=pb[6][0:1, :], func=AF.Square, scale=1.0 / C), ss, 1)
            bank_ticket[6] = msq
            B.wait("vector", ss, msq[1])
            B.wait("vector", st, s2[1])
            vv = B.op("vector", lambda e: e.scalar_tensor_tensor(
                out=trow(R_C), in0=pb[7][0:1, :], scalar=1.0 / C, in1=trow(R_T3),
                op0=ALU.mult, op1=ALU.subtract), sv, 1)
            bank_ticket[7] = vv
            B.wait("scalar", sv, vv[1])
            B.op("scalar", lambda e: e.activation(
                out=trow(R_C), in_=trow(R_C), func=AF.Ln, bias=cell(C_EPS)), ss, 1)
            rtk = B.op("scalar", lambda e: e.activation(
                out=trow(R_T2), in_=trow(R_C), func=AF.Exp, scale=-0.5), ss, 1)
            use_bank(6)
            use_bank(7)
            B.wait("tensor", ss, rtk[1])
            mm(pb[6][:, :], ONESF[:, :], trow(R_T1), True, True, inc=True)
            bc = mm(pb[7][:, :], ONESF[:, :], trow(R_T2), True, True, inc=True)
            stt["bc"] = (st, st.val)

        def ln_d(k):
            xo = xo_of(k)
            XN = XNP[k % 2]
            stt = ln_st[k]
            B.wait("vector", st, stt["bc"][1])
            B.op("vector", lambda e, xo=xo, XN=XN: e.tensor_sub(
                out=r3(XN[:, :], 8, 512), in0=r3(XR[:, xo:xo + 4096], 8, 512),
                in1=bcast3(pb[6].ap(), 8, 512)), sv, 1)
            xn = B.op("vector", lambda e, XN=XN: e.tensor_tensor(
                out=r3(XN[:, :], 8, 512), in0=r3(XN[:, :], 8, 512),
                in1=bcast3(pb[7].ap(), 8, 512), op=ALU.mult), sv, 1)
            bank_ticket[6] = xn
            bank_ticket[7] = xn
            return xn

        ln_a(0, 0)
        ln_a(0, 1)
        ln_b(0)
        ln_c(0)
        xn_tks[0] = ln_d(0)
        for k in range(NTILES):
            XN = XNP[k % 2]
            B.wait("tensor", sv, xn_tks[k][1])
            if k == 0:
                B.wait("tensor", wqkv_tk[0], wqkv_tk[1])
            for oj in range(8):
                bk = oj % 4
                use_bank(bk)
                for ci in range(8):
                    mm(pb[bk][:, :],
                       WQKV[:, ci * 3072 + oj * 128: ci * 3072 + (oj + 1) * 128],
                       XN[:, ci * 512:(ci + 1) * 512], ci == 0, ci == 7, inc=(ci == 7))
                mtk = (st, st.val)
                pa, pbb = PHP[oj % 2]
                B.wait("scalar", st, mtk[1])
                B.wait("scalar", sv, ph_war[oj % 2])
                B.op("scalar", lambda e, bk=bk, pa=pa: e.activation(
                    out=pa[:, :], in_=pb[bk][:, :], func=AF.Relu, scale=-1.0), ss, 1)
                e_tk = B.op("scalar", lambda e, pa=pa, pbb=pbb: e.activation(
                    out=pbb[:, :], in_=pa[:, :], func=AF.Exp, scale=-1.0), ss, 1)
                B.wait("vector", st, mtk[1])
                B.wait("vector", ss, e_tk[1])
                r1 = B.op("vector", lambda e, bk=bk, pa=pa: e.tensor_scalar(
                    out=pa[:, :], in0=pb[bk][:, :], scalar1=0.0, scalar2=None,
                    op0=ALU.max), sv, 1)
                bank_ticket[bk] = r1
                ad = B.op("vector", lambda e, oj=oj, k=k, pa=pa, pbb=pbb: e.tensor_add(
                    out=QPHI[:, k * 4096 + oj * 512: k * 4096 + (oj + 1) * 512],
                    in0=pbb[:, :], in1=pa[:, :]), sv, 1)
                ph_war[oj % 2] = ad[1]
            if k == 0:
                dbg("qphi_a", QPHI[:, 0:4096])

            for sub in range(4):
                for bk in range(4):
                    use_bank(bk)
                if kv_first[0]:
                    use_bank(4)
                    use_bank(5)
                for sl in range(4):
                    eoff = 1024 + sl * 512
                    for ci in range(8):
                        mm(pb[sl][:, :],
                           XN[:, ci * 512 + sub * 128: ci * 512 + (sub + 1) * 128],
                           WQKV[:, ci * 3072 + eoff: ci * 3072 + eoff + 512],
                           ci == 0, ci == 7, inc=(ci == 7))
                kv_tk = (st, st.val)
                for sl in range(2):
                    pa, pbb = PHP[sl]
                    B.wait("scalar", st, kv_tk[1] - 3 + sl)
                    B.wait("scalar", sv, ph_war[sl])
                    B.op("scalar", lambda e, sl=sl, pa=pa: e.activation(
                        out=pa[:, :], in_=pb[sl][:, :], func=AF.Relu, scale=-1.0), ss, 1)
                    e_tk = B.op("scalar", lambda e, pa=pa, pbb=pbb: e.activation(
                        out=pbb[:, :], in_=pa[:, :], func=AF.Exp, scale=-1.0), ss, 1)
                    B.wait("vector", st, kv_tk[1] - 3 + sl)
                    B.wait("vector", ss, e_tk[1])
                    r1 = B.op("vector", lambda e, sl=sl, pa=pa: e.tensor_scalar(
                        out=pa[:, :], in0=pb[sl][:, :], scalar1=0.0, scalar2=None,
                        op0=ALU.max), sv, 1)
                    bank_ticket[sl] = r1
                    ad = B.op("vector", lambda e, sl=sl, pa=pa, pbb=pbb: e.tensor_add(
                        out=KTM[:, sl * 512:(sl + 1) * 512], in0=pbb[:, :], in1=pa[:, :]), sv, 1)
                    ph_war[sl] = ad[1]
                for sl in range(2):
                    B.wait("vector", st, kv_tk[1] - 1 + sl)
                    vtk2 = B.op("vector", lambda e, sl=sl: e.tensor_copy(
                        out=AP(VEXT, sl * 8 * 66, [[1056, 128], [66, 8], [1, 64]]),
                        in_=r3(pb[2 + sl][:, :], 8, 64)), sv, 1)
                    bank_ticket[2 + sl] = vtk2
                kv_ready = (sv, sv.val)
                if k + 1 < NTILES:
                    if sub == 0:
                        ln_a(k + 1, 0)
                    elif sub == 1:
                        ln_a(k + 1, 1)
                    elif sub == 2:
                        ln_b(k + 1)
                    else:
                        ln_c(k + 1)
                        xn_tks[k + 1] = ln_d(k + 1)
                B.wait("tensor", sv, kv_ready[1])
                first = kv_first[0]
                kv_first[0] = False
                last = (k == NTILES - 1 and sub == 3)
                for h in range(H):
                    p2 = h // 2
                    bk = 4 + p2 // 4
                    cb = (p2 % 4) * 65
                    pbase = (h % 2) * 64
                    mm(pb[bk][pbase:pbase + 64, cb:cb + 65],
                       KTM[:, h * 64:(h + 1) * 64],
                       VEXT[:, h * 66:h * 66 + 65],
                       first and (h % 8 == 0), last, inc=(h == H - 1))
        kvdone_tk = (st, st.val)

        B.wait("sync", st, kvdone_tk[1])
        B.op("sync", lambda e: e.dma_start(out=EALL[:, :], in_=eall_d), sd, 16)
        eall_tk = (sd, sd.val)
        B.wait("vector", st, kvdone_tk[1])
        for b in range(2):
            B.op("vector", lambda e, b=b: e.tensor_copy(
                out=KVSB[:, b * 260:(b + 1) * 260], in_=pb[4 + b][:, 0:260]), sv, 1)
        kvsb_tk = (sv, sv.val)
        bank_ticket[4] = kvsb_tk
        bank_ticket[5] = kvsb_tk
        B.wait("sync", sv, kvsb_tk[1])
        B.op("sync", lambda e: e.dma_start(out=cck_in.ap(), in_=KVSB[:, :]), sd, 16)
        B.wait("gpsimd", sd, sd.val)
        B.op("gpsimd", lambda e: e.collective_compute(
            "AllReduce", ALU.add, replica_groups=[list(range(NCORES))],
            ins=[cck_in.ap()], outs=[cck_out.ap()]), scc, 1)
        kv_ar = (scc, scc.val)
        B.wait("gpsimd", st, kvdone_tk[1])
        B.op("gpsimd", lambda e: e.dma_start(out=WPROJ[:, :], in_=wproj_d), sg, 16)
        wproj_tk = (sg, sg.val)
        B.wait("gpsimd", kv_ar[0], kv_ar[1])
        B.wait("gpsimd", ksf_zero_tk[0], ksf_zero_tk[1])
        B.op("gpsimd", lambda e: e.dma_start(
            out=AP(KSF, 0, [[128, 64], [18, 8], [1, 1]]),
            in_=AP(cck_out, 64, [[520, 64], [65, 8], [1, 1]]),
            allow_slow_non_contiguous=True), sg, 16)
        B.op("gpsimd", lambda e: e.dma_start(
            out=AP(KSF, 64 * 128 + 1, [[128, 64], [18, 8], [1, 1]]),
            in_=AP(cck_out, 64 * 520 + 64, [[520, 64], [65, 8], [1, 1]]),
            allow_slow_non_contiguous=True), sg, 16)
        ksf_tk = (sg, sg.val)
        B.op("gpsimd", lambda e: e.dma_start(out=KVBF[:, :], in_=cck_out.ap()), sg, 16)
        kvpost_tk = (sg, sg.val)
        dbg("kvsb_a", KVSB[:, :])

        B.wait("tensor", ksf_tk[0], ksf_tk[1])
        B.wait("tensor", eall_tk[0], eall_tk[1])
        kv_waited = [False]
        for k in range(NTILES):
            xo = xo_of(k)
            use_bank(6)
            for ci in range(8):
                mm(pb[6][0:16, :], KSF[:, ci * 16:(ci + 1) * 16],
                   QPHI[:, k * 4096 + ci * 512: k * 4096 + (ci + 1) * 512],
                   ci == 0, ci == 7, inc=(ci == 7))
            n_tk = (st, st.val)
            B.wait("vector", st, n_tk[1])
            nv = B.op("vector", lambda e: e.tensor_scalar(
                out=NRS[:, :], in0=pb[6][0:16, :], scalar1=EPS_AT, scalar2=None,
                op0=ALU.add), sv, 1)
            bank_ticket[6] = nv
            B.wait("scalar", sv, nv[1])
            B.op("scalar", lambda e: e.activation(
                out=REC16[:, :], in_=NRS[:, :], func=AF.Ln), ss, 1)
            rec_tk = B.op("scalar", lambda e: e.activation(
                out=REC16[:, :], in_=REC16[:, :], func=AF.Exp, scale=-1.0), ss, 1)
            B.wait("tensor", ss, rec_tk[1])
            if not kv_waited[0]:
                kv_waited[0] = True
                B.wait("tensor", kvpost_tk[0], kvpost_tk[1])
                B.wait("tensor", wproj_tk[0], wproj_tk[1])
            for j in range(8):
                bko = j % 2
                bkr = 2 + (j % 2)
                use_bank(bko)
                use_bank(bkr)
                mm(pb[bko][0:64, :], KVBF[0:64, j * 65:j * 65 + 64],
                   QPHI[0:64, k * 4096 + j * 512: k * 4096 + (j + 1) * 512], True, True)
                mm(pb[bko][64:128, :], KVBF[64:128, j * 65:j * 65 + 64],
                   QPHI[64:128, k * 4096 + j * 512: k * 4096 + (j + 1) * 512],
                   True, True, inc=True)
                ro = mm(pb[bkr][:, :], EALL[:, j * 128:(j + 1) * 128], REC16[:, :],
                        True, True, inc=True)
                rb = PHA if (j % 2 == 0) else PHB
                B.wait("scalar", st, ro[1])
                B.wait("scalar", sv, sv.val)
                rc = B.op("scalar", lambda e, bkr=bkr, rb=rb: e.copy(
                    out=rb[:, :], in_=pb[bkr][:, :]), ss, 1)
                bank_ticket[bkr] = rc
                B.wait("vector", st, ro[1])
                B.wait("vector", ss, rc[1])
                dv = B.op("vector", lambda e, j=j, bko=bko, rb=rb: e.tensor_mul(
                    out=OUTPRE[:, j * 512:(j + 1) * 512], in0=pb[bko][:, :],
                    in1=rb[:, :]), sv, 1)
                bank_ticket[bko] = dv
            op_tk = (sv, sv.val)
            if k == 0:
                dbg("outpre_a", OUTPRE[:, :])
            B.wait("tensor", sv, op_tk[1])
            for oj in range(8):
                bk = 5 + (oj % 2)
                use_bank(bk)
                for ci in range(8):
                    mm(pb[bk][:, :],
                       WPROJ[:, ci * 1024 + oj * 128: ci * 1024 + (oj + 1) * 128],
                       OUTPRE[:, ci * 512:(ci + 1) * 512], ci == 0, ci == 7, inc=(ci == 7))
                mtk = (st, st.val)
                B.wait("vector", st, mtk[1])
                ev = B.op("vector", lambda e, oj=oj, bk=bk, xo=xo: e.scalar_tensor_tensor(
                    out=XR[:, xo + oj * 512: xo + (oj + 1) * 512], in0=pb[bk][:, :],
                    scalar=BIASC[:, oj:oj + 1],
                    in1=XR[:, xo + oj * 512: xo + (oj + 1) * 512],
                    op0=ALU.add, op1=ALU.add), sv, 1)
                bank_ticket[bk] = ev

    B.wait("vector", sd, x_tk[1])
    B.wait("vector", sv, cells_tk[1])
    B.wait("tensor", sg, wffn_tk[1])
    emit_ffn(0)
    if stage == 1:
        emit_store_xr()
        return end_block()

    B.wait("gpsimd", st, st.val)
    B.wait("gpsimd", sv, sv.val)
    B.op("gpsimd", lambda e: e.dma_start(out=WQKV[:, :], in_=wqkv_d), sg, 16)
    wqkv_tk = (sg, sg.val)
    for eng in Builder.ENGS:
        for sem in (st, sv, ss):
            B.wait(eng, sem)
    emit_attn(wqkv_tk)
    if stage == 2:
        emit_store_xr()
        return end_block()

    B.wait("gpsimd", st, st.val)
    B.wait("gpsimd", sv, sv.val)
    B.op("gpsimd", lambda e: e.dma_start(out=W1SB[:, :], in_=w1_d), sg, 16)
    w1_tk2 = (sg, sg.val)
    B.op("gpsimd", lambda e: e.dma_start(out=W2SB[:, :], in_=w2_d), sg, 16)
    B.op("gpsimd", lambda e: e.dma_start(out=W3SB[:, :], in_=w3_d), sg, 16)
    B.op("gpsimd", lambda e: e.dma_start(out=W4SB[:, :], in_=w4_d), sg, 16)
    B.op("gpsimd", lambda e: e.dma_start(out=BROWS[:, :], in_=brows_d), sg, 16)
    B.op("gpsimd", lambda e: e.dma_start(out=GW1SB[:, :], in_=gw1_d), sg, 16)
    B.op("gpsimd", lambda e: e.dma_start(out=GW2SB[:, :], in_=gw2_d), sg, 16)
    B.op("gpsimd", lambda e: e.dma_start(out=GW3SB[:, :], in_=gw3_d), sg, 16)
    gw_tk[1] = (sg, sg.val)
    for eng in Builder.ENGS:
        for sem in (st, sv, ss):
            B.wait(eng, sem)
    emit_ffn(1, w1_tk2=w1_tk2)
    return end_block()


def _images(inputs):
    f32 = np.float32
    x = np.asarray(inputs["L0_attn_x"], f32).reshape(TFULL, C)
    Wqkv = np.asarray(inputs["L0_attn_Wqkv"], f32)
    Wproj = np.asarray(inputs["L0_attn_Wproj"], f32)
    bproj = np.asarray(inputs["L0_attn_bproj"], f32)
    w = {i: np.asarray(inputs["L0_ffn_cn_w%d" % i], f32) for i in (1, 2, 3, 4)}
    b = {i: np.asarray(inputs["L0_ffn_cn_b%d" % i], f32) for i in (1, 2, 3, 4)}
    gw = {i: np.asarray(inputs["L0_ffn_gc_w%d" % i], f32) for i in (1, 2, 3)}
    gb = {i: np.asarray(inputs["L0_ffn_gc_b%d" % i], f32) for i in (1, 2, 3)}
    ar = np.asarray(inputs["L0_ffn_adaptive_range"], f32)

    def img(wm, kch):
        K, O = wm.shape
        return np.ascontiguousarray(
            wm.reshape(kch, 128, O).transpose(1, 0, 2).reshape(128, kch * O))

    com = {
        "w1": img(w[1], 8), "w2": img(w[2], 8), "w3": img(w[3], 4),
        "w4": np.ascontiguousarray(w[4].reshape(2, 128).T),
        "gw1": img(gw[1], 8), "gw2": img(gw[2], 4),
        "gw3": np.ascontiguousarray(gw[3].reshape(2, 128).T),
        "wqkv": img(np.ascontiguousarray(Wqkv.T), 8),
        "wproj": img(np.ascontiguousarray(Wproj.T), 8),
    }
    biasc = np.zeros((128, 16), f32)
    biasc[:, 0:8] = bproj.reshape(8, 128).T
    biasc[:, 8:16] = b[1].reshape(8, 128).T
    com["biasc"] = biasc
    brows = np.zeros((1, 2560), f32)
    brows[0, 0:1024] = b[1]
    brows[0, 1024:1536] = b[2]
    brows[0, 1536:1792] = b[3]
    brows[0, 1792:2304] = gb[1]
    brows[0, 2304:2560] = gb[2]
    com["brows"] = brows
    com["scal"] = np.array([[float(ar[0]), float(b[4][0]), float(gb[3][0]), 0.0]], f32)
    eall = np.zeros((16, 1024), f32)
    for j in range(8):
        eall[2 * j, j * 128: j * 128 + 64] = 1.0
        eall[2 * j + 1, j * 128 + 64: (j + 1) * 128] = 1.0
    com["eall"] = eall

    per_core = []
    for i in range(NCORES):
        sl = x[i * T:(i + 1) * T]
        xi = np.ascontiguousarray(
            sl.reshape(NTILES, TT, NCH, 128).transpose(3, 0, 2, 1).reshape(128, NCH * T))
        m = dict(com)
        m["x"] = xi
        per_core.append(m)
    return per_core


def kernel(**inputs):
    nc = build_program()
    in_maps = _images(inputs)
    res = run_bass_kernel_spmd(nc, in_maps, core_ids=list(range(NCORES)))
    out = np.empty((TFULL, C), np.float32)
    for i in range(NCORES):
        oi = np.asarray(res.results[i]["out"])
        out[i * T:(i + 1) * T] = (
            oi.reshape(128, NTILES, NCH, TT).transpose(1, 3, 2, 0).reshape(T, C))
    return out.reshape(1, TFULL, C)


# version 25
# speedup vs baseline: 1.0775x; 1.0232x over previous
import numpy as np
import concourse.bass as bass
import concourse.mybir as mybir
from concourse.ap import AP
from concourse.bass_utils import run_bass_kernel_spmd

F32 = mybir.dt.float32
BF16 = mybir.dt.bfloat16
AF = mybir.ActivationFunctionType
ALU = mybir.AluOpType

NCORES = 8
TFULL = 16384
C = 1024
NCH = 8
T = TFULL // NCORES
TT = 512
NTILES = T // TT
H = 16
EPS_LN = 1e-5
EPS_AT = 1e-6

SGN = [1.0, 1.0, -1.0, -1.0, -1.0, -1.0, 1.0, 1.0]
SIG = lambda j: (j + 4) % 8
XBFN_SLOT = {6: 0, 7: 1, 0: 2, 1: 3}


class Sem:
    def __init__(self, nc, name):
        self.h = nc.alloc_semaphore(name)
        self.val = 0


class Builder:
    ENGS = ("sync", "gpsimd", "vector", "scalar", "tensor")

    def __init__(self):
        self.ops = {e: [] for e in self.ENGS}

    def op(self, eng, fn, inc=None, n=1):
        if inc is not None:
            inc.val += n
            def run(e, fn=fn, sem=inc.h, n=n):
                fn(e).then_inc(sem, n)
            self.ops[eng].append(run)
            return (inc, inc.val)
        self.ops[eng].append(lambda e, fn=fn: fn(e))
        return None

    def wait(self, eng, sem, val=None):
        v = sem.val if val is None else val
        if v <= 0:
            return
        self.ops[eng].append(lambda e, s=sem.h, v=v: e.wait_ge(s, v))

    def wait_t(self, eng, ticket):
        if ticket is not None:
            self.wait(eng, ticket[0], ticket[1])


def bcast3(tile_ap, reps, n):
    return AP(tile_ap.tensor, tile_ap.offset,
              [[tile_ap.ap[0][0], tile_ap.ap[0][1]], [0, reps], [1, n]])


def r3(ap2d, c, n):
    return ap2d.rearrange("p (c n) -> p c n", c=c, n=n)


def build_program(stage=3, dump=()):
    nc = bass.Bass("TRN2", target_bir_lowering=False, detect_race_conditions=False)

    x_d = nc.dram_tensor("x", [128, NCH * T], F32, kind="ExternalInput").ap()
    out_d = nc.dram_tensor("out", [128, NCH * T], F32, kind="ExternalOutput").ap()
    w1_d = nc.dram_tensor("w1", [128, 8192], F32, kind="ExternalInput").ap()
    w2_d = nc.dram_tensor("w2", [128, 4096], F32, kind="ExternalInput").ap()
    w3_d = nc.dram_tensor("w3", [128, 1024], F32, kind="ExternalInput").ap()
    w4_d = nc.dram_tensor("w4", [128, 2], F32, kind="ExternalInput").ap()
    gw1_d = nc.dram_tensor("gw1", [128, 4096], F32, kind="ExternalInput").ap()
    gw2_d = nc.dram_tensor("gw2", [128, 1024], F32, kind="ExternalInput").ap()
    gw3_d = nc.dram_tensor("gw3", [128, 2], F32, kind="ExternalInput").ap()
    wqkv_d = nc.dram_tensor("wqkv", [128, 24576], F32, kind="ExternalInput").ap()
    wproj_d = nc.dram_tensor("wproj", [128, 8192], F32, kind="ExternalInput").ap()
    biasc_d = nc.dram_tensor("biasc", [128, 16], F32, kind="ExternalInput").ap()
    brows_d = nc.dram_tensor("brows", [1, 2560], F32, kind="ExternalInput").ap()
    scal_d = nc.dram_tensor("scal", [1, 4], F32, kind="ExternalInput").ap()
    eall_d = nc.dram_tensor("eall", [16, 1024], F32, kind="ExternalInput").ap()
    ccg_in = nc.dram_tensor("ccg_in", [128, 8], F32, kind="Internal")
    ccg_out = nc.dram_tensor("ccg_out", [128, 8], F32, kind="Internal")
    cck_in = nc.dram_tensor("cck_in", [128, 520], F32, kind="Internal")
    cck_out = nc.dram_tensor("cck_out", [128, 520], F32, kind="Internal")
    ccs_in = nc.dram_tensor("ccs_in", [128, 8], F32, kind="Internal")
    ccs_out = nc.dram_tensor("ccs_out", [128, 8], F32, kind="Internal")

    base = nc.sbuf_base
    cur = [base]
    offs = {}

    def alloc(name, shape, dtype, at=None):
        nbytes = int(np.prod(shape[1:])) * (4 if dtype == F32 else 2)
        if at is None:
            o = (cur[0] + 63) & ~63
            cur[0] = o + nbytes
        else:
            o = at
        t = nc.alloc_sbuf_tensor_at(name, shape, dtype, offset=o)
        offs[name] = o
        return t

    XR = alloc("XR", [128, NCH * T], F32)
    TROW = alloc("TROW", [1, 3072], F32)
    SPROW = alloc("SPROW", [1, 2048], F32)
    CELLS = alloc("CELLS", [128, 16], F32)
    BIASC = alloc("BIASC", [128, 16], F32)
    ONESB = alloc("ONESB", [1, 512], BF16)
    ONESF = alloc("ONESF", [1, 128], F32)
    ONE1B = alloc("ONE1B", [1, 32], BF16)
    ONECB = alloc("ONECB", [128, 1], BF16)
    PERS_END = cur[0]

    cur[0] = PERS_END
    W1SB = alloc("W1SB", [128, 8192], BF16)
    W2SB = alloc("W2SB", [128, 4096], BF16)
    W3SB = alloc("W3SB", [128, 1024], BF16)
    W4SB = alloc("W4SB", [128, 2], BF16)
    BROWS = alloc("BROWS", [1, 2560], BF16)
    CBROW = alloc("CBROW", [1, 512], BF16)
    CBROW2 = alloc("CBROW2", [1, 512], BF16)
    GCP = alloc("GCP", [128, 32], F32)
    GCN = alloc("GCN", [128, 8], F32)
    GCNB = alloc("GCNB", [128, 8], BF16)
    GH1 = alloc("GH1", [128, 4], BF16)
    GH2 = alloc("GH2", [128, 2], BF16)
    MROW = alloc("MROW", [1, 2048], F32)
    RROW = alloc("RROW", [1, 2048], F32)
    XBF = alloc("XBF", [128, 4096], BF16)
    XBFN = alloc("XBFN", [128, 2048], BF16)
    GPAIR = [alloc("G_a", [128, 4096], BF16), alloc("G_b", [128, 4096], BF16)]
    HPAIR = [alloc("H_a", [128, 4096], BF16), alloc("H_b", [128, 4096], BF16)]
    SCR = []
    for pi, par in enumerate(("a", "b")):
        s0 = (cur[0] + 63) & ~63
        d = {
            "H1G": alloc("H1G_" + par, [128, 4096], BF16, at=s0),
            "H2G": alloc("H2G_" + par, [128, 2048], BF16, at=s0 + 8192),
            "H3G": alloc("H3G_" + par, [128, 1024], BF16, at=s0 + 12288),
            "CBC": alloc("CBC_" + par, [128, 512], BF16, at=offs["XBFN"] + pi * 1024),
        }
        cur[0] = s0 + 14336
        SCR.append(d)
    GW1SB = alloc("GW1SB", [128, 4096], BF16, at=offs["H1G_a"])
    GW2SB = alloc("GW2SB", [128, 1024], BF16, at=offs["H2G_a"])
    GW3SB = alloc("GW3SB", [128, 2], BF16, at=offs["H3G_a"])
    QFIN = alloc("QFIN", [128, 4096], F32, at=offs["H1G_a"])
    TMP1 = alloc("TMP1", [128, 4096], BF16, at=offs["XBF"])
    FFN_END = cur[0]

    cur[0] = PERS_END
    WQKV = alloc("WQKV", [128, 24576], BF16)
    WPROJ = alloc("WPROJ", [128, 8192], BF16, at=offs["WQKV"])
    QPHI = alloc("QPHI", [128, NCH * T], BF16)
    XNBF = alloc("XNBF", [128, 4096], BF16)
    XNBF2 = alloc("XNBF2", [128, 4096], BF16)
    KTM = alloc("KTM", [128, 1024], BF16)
    VEXT = alloc("VEXT", [128, 1056], BF16)
    PHA = alloc("PHA", [128, 512], BF16)
    PHB = alloc("PHB", [128, 512], BF16)
    PHA2 = alloc("PHA2", [128, 512], BF16)
    PHB2 = alloc("PHB2", [128, 512], BF16)
    OUTPRE = alloc("OUTPRE", [128, 4096], BF16)
    KVSB = alloc("KVSB", [128, 520], F32)
    KVBF = alloc("KVBF", [128, 520], BF16)
    NRS = alloc("NRS", [16, 512], F32)
    REC16 = alloc("REC16", [16, 512], F32)
    EALL = alloc("EALL", [16, 1024], F32, at=offs["KTM"])
    KS8 = alloc("KS8", [128, 8], F32)
    KSF = alloc("KSF", [128, 128], BF16)
    ATTN_END = cur[0]

    cap = nc.sbuf_top
    assert FFN_END <= cap, (FFN_END, cap)
    assert ATTN_END <= cap, (ATTN_END, cap)

    pb = [nc.alloc_psum_tensor("pb%d" % i, [128, 512], F32) for i in range(8)]

    sd = Sem(nc, "sd")
    sg = Sem(nc, "sg")
    sv = Sem(nc, "sv")
    ss = Sem(nc, "ss")
    st = Sem(nc, "st")
    scc = Sem(nc, "scc")
    so = Sem(nc, "so")

    B = Builder()
    bank_ticket = [None] * 8

    def use_bank(i):
        B.wait_t("tensor", bank_ticket[i])

    def dbg(tag, ap, dtype=None):
        if tag not in dump:
            return
        dt = ap.tensor.dtype if dtype is None else dtype
        dd = nc.dram_tensor("dbg_" + tag, list(ap.shape), dt, kind="ExternalOutput").ap()
        for sem in (st, sv, ss):
            B.wait("sync", sem)
        B.op("sync", lambda e, dd=dd, ap=ap: e.dma_start(out=dd, in_=ap), sd, 16)
        for eng in ("vector", "scalar", "tensor"):
            B.wait(eng, sd)

    C_AR, C_B4, C_GB3, C_EPS, C_HB4, C_HGB3, C_K1, C_K2, C_C3, C_GTH, C_GF, C_C0 = range(12)

    def cell(i):
        return CELLS[0:1, i:i + 1]

    R_C, R_A, R_B, R_T1, R_T2, R_T3 = (0, 512, 1024, 1536, 2048, 2560)

    def trow(slot):
        return TROW[0:1, slot:slot + 512]

    def mm(out, lhsT, rhs, start, stop, inc=False):
        if inc:
            return B.op("tensor", lambda e: e.matmul(out=out, lhsT=lhsT, rhs=rhs,
                                                     start=start, stop=stop), st, 1)
        return B.op("tensor", lambda e: e.matmul(out=out, lhsT=lhsT, rhs=rhs,
                                                 start=start, stop=stop))

    def end_block():
        B.wait("gpsimd", so, so.val)
        B.wait("sync", so, so.val)
        with nc.Block() as block:
            for eng in Builder.ENGS:
                fns = B.ops[eng]
                def body(e, fns=fns):
                    for f in fns:
                        f(e)
                getattr(block, eng)(body)
        return nc

    def xo_of(k):
        return k * 4096

    def emit_store_xr():
        for k in range(NTILES):
            xo = xo_of(k)
            B.wait("sync", sv, sv.val)
            B.wait("sync", ss, ss.val)
            B.op("sync", lambda e, xo=xo: e.dma_start(
                out=AP(out_d.tensor, xo, [[NCH * T, 128], [1, 4096]]),
                in_=XR[:, xo:xo + 4096]), so, 16)

    B.op("sync", lambda e: e.dma_start(out=XR[:, :], in_=x_d), sd, 16)
    x_tk = (sd, sd.val)
    B.op("sync", lambda e: e.dma_start(out=BIASC[:, :], in_=biasc_d), sd, 16)
    B.op("sync", lambda e: e.dma_start(out=CELLS[0:1, 0:4], in_=scal_d), sd, 16)
    init_d = (sd, sd.val)
    B.op("gpsimd", lambda e: e.dma_start(out=W1SB[:, :], in_=w1_d), sg, 16)
    B.op("gpsimd", lambda e: e.dma_start(out=W2SB[:, :], in_=w2_d), sg, 16)
    B.op("gpsimd", lambda e: e.dma_start(out=W3SB[:, :], in_=w3_d), sg, 16)
    B.op("gpsimd", lambda e: e.dma_start(out=W4SB[:, :], in_=w4_d), sg, 16)
    B.op("gpsimd", lambda e: e.dma_start(out=BROWS[:, :], in_=brows_d), sg, 16)
    wffn_tk = (sg, sg.val)
    B.op("gpsimd", lambda e: e.dma_start(out=GW1SB[:, :], in_=gw1_d), sg, 16)
    B.op("gpsimd", lambda e: e.dma_start(out=GW2SB[:, :], in_=gw2_d), sg, 16)
    B.op("gpsimd", lambda e: e.dma_start(out=GW3SB[:, :], in_=gw3_d), sg, 16)
    gw_tk = {0: (sg, sg.val)}
    B.op("vector", lambda e: e.memset(ONESB[:, :], 1.0))
    B.op("vector", lambda e: e.memset(ONESF[:, :], 1.0))
    B.op("vector", lambda e: e.memset(ONE1B[:, :], 1.0))
    B.op("vector", lambda e: e.memset(ONECB[:, :], 1.0))
    B.op("vector", lambda e: e.memset(CELLS[:, C_EPS:C_EPS + 1], EPS_LN), sv, 1)
    B.wait("vector", sd, init_d[1])
    B.op("vector", lambda e: e.tensor_scalar(out=cell(C_HB4), in0=cell(C_B4),
                                             scalar1=0.5, scalar2=None, op0=ALU.mult))
    B.op("vector", lambda e: e.tensor_scalar(out=cell(C_HGB3), in0=cell(C_GB3),
                                             scalar1=0.5, scalar2=None, op0=ALU.mult))
    B.op("vector", lambda e: e.tensor_scalar(out=cell(C_K1), in0=cell(C_AR),
                                             scalar1=0.3, scalar2=None, op0=ALU.mult))
    B.op("vector", lambda e: e.tensor_scalar(out=cell(C_K2), in0=cell(C_AR),
                                             scalar1=1.4, scalar2=None, op0=ALU.mult))
    B.op("vector", lambda e: e.tensor_scalar(out=cell(C_C3), in0=cell(C_AR),
                                             scalar1=-0.7, scalar2=0.1,
                                             op0=ALU.mult, op1=ALU.add), sv, 1)
    cells_tk = (sv, sv.val)

    def emit_ffn(phase, w1_tk2=None):
        SQB = GPAIR[0]
        cp = [None] * (NTILES + 1)
        cp[0] = B.op("vector", lambda e: e.tensor_copy(
            out=XBF[:, :], in_=XR[:, 0:4096]), sv, 1)
        for k in range(NTILES):
            xo = xo_of(k)
            use_bank(6)
            B.wait("tensor", cp[k][0], cp[k][1])
            for ci in range(8):
                mm(pb[6][0:1, :], ONECB[:, :], XBF[:, ci * 512:(ci + 1) * 512],
                   ci == 0, ci == 7, inc=(ci == 7))
            s1_tk = (st, st.val)
            B.wait("scalar", st, s1_tk[1])
            B.wait("scalar", sv, sv.val)
            stk = B.op("scalar", lambda e: e.activation(
                out=SQB[:, :], in_=XBF[:, :], func=AF.Square), ss, 1)
            use_bank(7)
            B.wait("tensor", ss, stk[1])
            for ci in range(8):
                mm(pb[7][0:1, :], ONECB[:, :], SQB[:, ci * 512:(ci + 1) * 512],
                   ci == 0, ci == 7, inc=(ci == 7))
            s2_tk = (st, st.val)
            if k + 1 < NTILES:
                xo2 = xo_of(k + 1)
                B.wait("vector", st, s1_tk[1])
                B.wait("vector", ss, stk[1])
                cp[k + 1] = B.op("vector", lambda e, xo2=xo2: e.tensor_copy(
                    out=XBF[:, :], in_=XR[:, xo2:xo2 + 4096]), sv, 1)
            B.op("scalar", lambda e, k=k: e.mul(
                out=MROW[0:1, k * 512:(k + 1) * 512], in_=pb[6][0:1, :], mul=1.0 / C), ss, 1)
            B.op("scalar", lambda e: e.activation(
                out=trow(R_T1), in_=pb[6][0:1, :], func=AF.Square, scale=1.0 / C), ss, 1)
            msq_tk = (ss, ss.val)
            bank_ticket[6] = msq_tk
            B.wait("vector", ss, msq_tk[1])
            B.wait("vector", st, s2_tk[1])
            vv = B.op("vector", lambda e: e.scalar_tensor_tensor(
                out=trow(R_T2), in0=pb[7][0:1, :], scalar=1.0 / C, in1=trow(R_T1),
                op0=ALU.mult, op1=ALU.subtract), sv, 1)
            bank_ticket[7] = vv
            B.wait("scalar", sv, vv[1])
            B.op("scalar", lambda e: e.activation(
                out=trow(R_T3), in_=trow(R_T2), func=AF.Ln, bias=cell(C_EPS)), ss, 1)
            rtk = B.op("scalar", lambda e, k=k: e.activation(
                out=RROW[0:1, k * 512:(k + 1) * 512], in_=trow(R_T3), func=AF.Exp,
                scale=-0.5), ss, 1)
            use_bank(0)
            use_bank(1)
            B.wait("tensor", ss, rtk[1])
            mm(pb[0][:, :], ONESF[:, :], MROW[0:1, k * 512:(k + 1) * 512], True, True, inc=True)
            bc = mm(pb[1][:, :], ONESF[:, :], RROW[0:1, k * 512:(k + 1) * 512], True, True, inc=True)
            B.wait("vector", st, bc[1])
            B.op("vector", lambda e, xo=xo: e.tensor_sub(
                out=SQB[:, :], in0=XR[:, xo:xo + 4096], in1=bcast3(pb[0].ap(), 8, 512)), sv, 1)
            for ci in range(8):
                B.op("vector", lambda e, ci=ci, k=k: e.scalar_tensor_tensor(
                    out=SQB[:, ci * 512:(ci + 1) * 512],
                    in0=SQB[:, ci * 512:(ci + 1) * 512], scalar=1.0,
                    in1=pb[1][:, :], op0=ALU.bypass, op1=ALU.mult,
                    accum_out=GCP[:, k * 8 + ci:k * 8 + ci + 1]), sv, 1)
            gc_tk = (sv, sv.val)
            bank_ticket[0] = gc_tk
            bank_ticket[1] = gc_tk

        B.op("vector", lambda e: e.tensor_add(out=GCN[:, :], in0=GCP[:, 0:8], in1=GCP[:, 8:16]))
        B.op("vector", lambda e: e.tensor_add(out=GCN[:, :], in0=GCN[:, :], in1=GCP[:, 16:24]))
        B.op("vector", lambda e: e.tensor_add(out=GCN[:, :], in0=GCN[:, :], in1=GCP[:, 24:32]), sv, 1)
        B.wait("sync", sv, sv.val)
        B.op("sync", lambda e: e.dma_start(out=ccg_in.ap(), in_=GCN[:, :]), sd, 16)
        B.wait("gpsimd", sd, sd.val)
        B.op("gpsimd", lambda e: e.collective_compute(
            "AllReduce", ALU.add, replica_groups=[list(range(NCORES))],
            ins=[ccg_in.ap()], outs=[ccg_out.ap()]), scc, 1)
        B.wait("sync", scc, scc.val)
        B.op("sync", lambda e: e.dma_start(out=GCN[:, :], in_=ccg_out.ap()), sd, 16)
        ar_done = (sd, sd.val)

        def emit_prep(k, par):
            xo = xo_of(k)
            B.wait("vector", st, st.val)
            use_bank(0)
            use_bank(1)
            mm(pb[0][:, :], ONESF[:, :], MROW[0:1, k * 512:(k + 1) * 512], True, True, inc=True)
            bc = mm(pb[1][:, :], ONESF[:, :], RROW[0:1, k * 512:(k + 1) * 512], True, True, inc=True)
            B.wait("vector", st, bc[1])
            B.op("vector", lambda e, xo=xo: e.tensor_sub(
                out=XBF[:, :], in0=XR[:, xo:xo + 4096], in1=bcast3(pb[0].ap(), 8, 512)), sv, 1)
            o0 = B.op("vector", lambda e: e.tensor_tensor(
                out=r3(XBF[:, :], 8, 512), in0=r3(XBF[:, :], 8, 512),
                in1=bcast3(pb[1].ap(), 8, 512), op=ALU.mult), sv, 1)
            bank_ticket[0] = o0
            bank_ticket[1] = o0
            B.op("vector", lambda e: e.tensor_scalar(
                out=XBFN[:, 0:1024], in0=XBF[:, 6 * 512:8 * 512], scalar1=-1.0,
                scalar2=None, op0=ALU.mult))
            on = B.op("vector", lambda e: e.tensor_scalar(
                out=XBFN[:, 1024:2048], in0=XBF[:, 0:1024], scalar1=-1.0,
                scalar2=None, op0=ALU.mult), sv, 1)
            if k == 0:
                dbg("out0_p%d" % phase, XBF[:, :])
            return o0, on

        def emit_g0h0(k, par, o0, on):
            GT, HT = GPAIR[par], HPAIR[par]
            B.wait("tensor", sv, o0[1])
            if phase == 1 and k == 0:
                B.wait("tensor", w1_tk2[0], w1_tk2[1])
            for oj in range(8):
                bk = 2 + (oj % 2)
                use_bank(bk)
                for ci in range(8):
                    mm(pb[bk][:, :],
                       W1SB[:, ci * 1024 + oj * 128: ci * 1024 + (oj + 1) * 128],
                       XBF[:, ci * 512:(ci + 1) * 512], ci == 0, ci == 7, inc=(ci == 7))
                mtk = (st, st.val)
                B.wait("scalar", st, mtk[1])
                etk = B.op("scalar", lambda e, oj=oj, bk=bk, GT=GT: e.copy(
                    out=GT[:, oj * 512:(oj + 1) * 512], in_=pb[bk][:, :]), ss, 1)
                bank_ticket[bk] = etk
            B.wait("tensor", sv, on[1])
            for oj in range(8):
                bk = 2 + (oj % 2)
                use_bank(bk)
                for ci in range(8):
                    if SGN[ci] > 0:
                        rhs = XBF[:, SIG(ci) * 512:(SIG(ci) + 1) * 512]
                    else:
                        sl = XBFN_SLOT[SIG(ci)]
                        rhs = XBFN[:, sl * 512:(sl + 1) * 512]
                    mm(pb[bk][:, :],
                       W1SB[:, ci * 1024 + oj * 128: ci * 1024 + (oj + 1) * 128],
                       rhs, ci == 0, ci == 7, inc=(ci == 7))
                mtk = (st, st.val)
                B.wait("vector", st, mtk[1])
                etk = B.op("vector", lambda e, oj=oj, bk=bk, HT=HT: e.tensor_copy(
                    out=HT[:, oj * 512:(oj + 1) * 512], in_=pb[bk][:, :]), sv, 1)
                bank_ticket[bk] = etk
            if k == 0:
                dbg("G0_p%d" % phase, GT[:, :])
                dbg("H0_p%d" % phase, HT[:, :])

        def emit_gf_net():
            B.wait("vector", ar_done[0], ar_done[1])
            gcn_tk = B.op("vector", lambda e: e.tensor_scalar(
                out=GCNB[:, :], in0=GCN[:, :], scalar1=1.0 / TFULL, scalar2=None,
                op0=ALU.mult), sv, 1)
            use_bank(6)
            B.wait("tensor", sv, gcn_tk[1])
            B.wait("tensor", gw_tk[phase][0], gw_tk[phase][1])
            for oj in range(4):
                for ci in range(8):
                    mm(pb[6][:, oj:oj + 1],
                       GW1SB[:, ci * 512 + oj * 128: ci * 512 + (oj + 1) * 128],
                       GCNB[:, ci:ci + 1], ci == 0, False)
                mm(pb[6][:, oj:oj + 1], BROWS[0:1, 1792 + oj * 128:1792 + (oj + 1) * 128],
                   ONE1B[0:1, 0:1], False, True, inc=True)
            g1_tk = (st, st.val)
            B.wait("scalar", st, g1_tk[1])
            e1 = B.op("scalar", lambda e: e.activation(out=GH1[:, :], in_=pb[6][:, 0:4],
                                                       func=AF.Gelu), ss, 1)
            bank_ticket[6] = e1
            use_bank(7)
            B.wait("tensor", ss, e1[1])
            for oj in range(2):
                for ci in range(4):
                    mm(pb[7][:, oj:oj + 1],
                       GW2SB[:, ci * 256 + oj * 128: ci * 256 + (oj + 1) * 128],
                       GH1[:, ci:ci + 1], ci == 0, False)
                mm(pb[7][:, oj:oj + 1], BROWS[0:1, 2304 + oj * 128:2304 + (oj + 1) * 128],
                   ONE1B[0:1, 0:1], False, True, inc=True)
            g2_tk = (st, st.val)
            B.wait("scalar", st, g2_tk[1])
            e2 = B.op("scalar", lambda e: e.activation(out=GH2[:, :], in_=pb[7][:, 0:2],
                                                       func=AF.Gelu), ss, 1)
            bank_ticket[7] = e2
            use_bank(4)
            B.wait("tensor", ss, e2[1])
            for ci in range(2):
                mm(pb[4][0:1, 0:1], GW3SB[:, ci:ci + 1], GH2[:, ci:ci + 1],
                   ci == 0, ci == 1, inc=(ci == 1))
            g3_tk = (st, st.val)
            B.wait("scalar", st, g3_tk[1])
            e3 = B.op("scalar", lambda e: e.activation(
                out=cell(C_GTH), in_=pb[4][0:1, 0:1], func=AF.Tanh, scale=0.5,
                bias=cell(C_HGB3)), ss, 1)
            bank_ticket[4] = e3
            B.wait("vector", ss, e3[1])
            B.op("vector", lambda e: e.tensor_scalar(
                out=cell(C_GF), in0=cell(C_GTH), scalar1=0.5, scalar2=0.5,
                op0=ALU.mult, op1=ALU.add))
            cz = B.op("vector", lambda e: e.tensor_scalar(
                out=cell(C_C0), in0=cell(C_GF), scalar1=cell(C_K2), scalar2=cell(C_C3),
                op0=ALU.mult, op1=ALU.add), sv, 1)
            return cz

        RS = [{"c": R_C, "A": R_A, "B": R_B}, {"c": R_T1, "A": R_T2, "B": R_T3}]

        def rowp(par, nm):
            return trow(RS[par][nm])

        last_fin = [None]

        upd_st = [[None] * 4, [None] * 4]
        CBROWP = [CBROW, CBROW2]

        def emit_cycle(k, par, cyc, cz_tk, mid=None):
            GT, HT = GPAIR[par], HPAIR[par]
            scr = SCR[par]
            bb = 0 if par == 0 else 4
            if cyc == 0:
                upd_st[par] = [None] * 4
            upd_tk = upd_st[par]
            if cyc == 0 and cz_tk is not None:
                B.wait("scalar", cz_tk[0], cz_tk[1])
            for g2 in range(4):
                c0_, c1_ = 2 * g2, 2 * g2 + 1
                if cyc == 0 and g2 == 0 and last_fin[0] is not None:
                    fh = last_fin[0][0] if par == 0 else last_fin[0][1]
                    B.wait("scalar", fh[0], fh[1])
                if upd_tk[g2] is not None:
                    B.wait("scalar", sv, upd_tk[g2][1])
                for ci in (c0_, c1_):
                    B.op("scalar", lambda e, ci=ci, GT=GT, scr=scr: e.activation(
                        out=scr["H1G"][:, ci * 512:(ci + 1) * 512],
                        in_=GT[:, ci * 512:(ci + 1) * 512], func=AF.Gelu,
                        bias=BIASC[:, 8 + ci:9 + ci]),
                        ss if ci == c1_ else None, 1)
                g1tk = (ss, ss.val)
                B.wait("tensor", ss, g1tk[1])
                if g2 == 0:
                    for bk in range(bb, bb + 4):
                        use_bank(bk)
                for ci in (c0_, c1_):
                    for oj in range(4):
                        mm(pb[bb + oj][:, :],
                           W2SB[:, ci * 512 + oj * 128: ci * 512 + (oj + 1) * 128],
                           scr["H1G"][:, ci * 512:(ci + 1) * 512],
                           ci == 0, ci == 7 and oj == 3, inc=(ci == 7 and oj == 3))
                if g2 == 1 and mid is not None:
                    mid()
            w2tk = (st, st.val)
            B.wait("scalar", st, w2tk[1])
            if cyc == 0 and par == 0 and last_fin[0] is not None:
                B.wait("scalar", last_fin[0][1][0], last_fin[0][1][1])
            for oj in range(4):
                etk = B.op("scalar", lambda e, oj=oj, bb=bb, scr=scr: e.activation(
                    out=scr["H2G"][:, oj * 512:(oj + 1) * 512], in_=pb[bb + oj][:, :],
                    func=AF.Gelu), ss, 1)
                bank_ticket[bb + oj] = etk
            h2tk = (ss, ss.val)
            use_bank(bb)
            use_bank(bb + 1)
            B.wait("tensor", ss, h2tk[1])
            for ci in range(4):
                for oj in range(2):
                    mm(pb[bb + oj][:, :],
                       W3SB[:, ci * 256 + oj * 128: ci * 256 + (oj + 1) * 128],
                       scr["H2G"][:, ci * 512:(ci + 1) * 512],
                       ci == 0, ci == 3 and oj == 1, inc=(ci == 3 and oj == 1))
            w3tk = (st, st.val)
            B.wait("scalar", st, w3tk[1])
            for oj in range(2):
                etk = B.op("scalar", lambda e, oj=oj, bb=bb, scr=scr: e.activation(
                    out=scr["H3G"][:, oj * 512:(oj + 1) * 512], in_=pb[bb + oj][:, :],
                    func=AF.Gelu), ss, 1)
                bank_ticket[bb + oj] = etk
            h3tk = (ss, ss.val)
            use_bank(bb + 2)
            B.wait("tensor", ss, h3tk[1])
            for ci in range(2):
                mm(pb[bb + 2][0:1, :], W4SB[:, ci:ci + 1],
                   scr["H3G"][:, ci * 512:(ci + 1) * 512], ci == 0, ci == 1,
                   inc=(ci == 1))
            ztk = (st, st.val)
            B.wait("scalar", st, ztk[1])
            thtk = B.op("scalar", lambda e, par=par, bb=bb: e.activation(
                out=rowp(par, "c"), in_=pb[bb + 2][0:1, :], func=AF.Tanh, scale=0.5,
                bias=cell(C_HB4)), ss, 1)
            bank_ticket[bb + 2] = thtk
            B.wait("vector", ss, thtk[1])
            if cyc == 0 and cz_tk is not None:
                B.wait("vector", cz_tk[0], cz_tk[1])
            B.op("vector", lambda e, par=par: e.tensor_scalar(
                out=rowp(par, "c"), in0=rowp(par, "c"), scalar1=cell(C_K1),
                scalar2=cell(C_C0), op0=ALU.mult, op1=ALU.add), sv, 1)
            if cyc == 0:
                B.op("vector", lambda e, par=par: e.tensor_copy(
                    out=rowp(par, "B"), in_=rowp(par, "c")))
                B.op("vector", lambda e, par=par: e.memset(rowp(par, "A"), 1.0), sv, 1)
            else:
                B.op("vector", lambda e, par=par: e.tensor_mul(
                    out=SPROW[0:1, par * 512:(par + 1) * 512], in0=rowp(par, "c"),
                    in1=rowp(par, "B")))
                B.op("vector", lambda e, par=par: e.tensor_mul(
                    out=SPROW[0:1, 1024 + par * 512:1024 + (par + 1) * 512],
                    in0=rowp(par, "c"), in1=rowp(par, "A")))
                B.op("vector", lambda e, par=par: e.tensor_sub(
                    out=rowp(par, "A"), in0=rowp(par, "A"),
                    in1=SPROW[0:1, par * 512:(par + 1) * 512]))
                B.op("vector", lambda e, par=par: e.tensor_add(
                    out=rowp(par, "B"), in0=rowp(par, "B"),
                    in1=SPROW[0:1, 1024 + par * 512:1024 + (par + 1) * 512]), sv, 1)
            if cyc >= 2:
                return None

            def emit_cbc():
                cbtk = B.op("vector", lambda e, par=par: e.tensor_copy(
                    out=CBROWP[par][:, :], in_=rowp(par, "c")), sv, 1)
                cbb = bb + 3
                use_bank(cbb)
                B.wait("tensor", sv, cbtk[1])
                bmk = mm(pb[cbb][:, :], ONESB[0:1, 0:128], CBROWP[par][:, :],
                         True, True, inc=True)
                B.wait("scalar", st, bmk[1])
                cbe = B.op("scalar", lambda e, cbb=cbb, scr=scr: e.copy(
                    out=scr["CBC"][:, :], in_=pb[cbb][:, :]), ss, 1)
                bank_ticket[cbb] = cbe
                B.wait("vector", ss, cbe[1])
                for g2 in range(4):
                    sl = slice(g2 * 1024, (g2 + 1) * 1024)
                    B.op("vector", lambda e, sl=sl, scr=scr, HT=HT: e.tensor_tensor(
                        out=r3(TMP1[:, sl], 2, 512), in0=r3(HT[:, sl], 2, 512),
                        in1=bcast3(scr["CBC"].ap(), 2, 512), op=ALU.mult), sv, 1)
                    if cyc == 0:
                        B.op("vector", lambda e, sl=sl, scr=scr, GT=GT: e.tensor_tensor(
                            out=r3(scr["H1G"][:, sl], 2, 512), in0=r3(GT[:, sl], 2, 512),
                            in1=bcast3(scr["CBC"].ap(), 2, 512), op=ALU.mult))
                        B.op("vector", lambda e, sl=sl, scr=scr, HT=HT: e.tensor_sub(
                            out=HT[:, sl], in0=HT[:, sl], in1=scr["H1G"][:, sl]))
                    gadd = B.op("vector", lambda e, sl=sl, GT=GT: e.tensor_add(
                        out=GT[:, sl], in0=GT[:, sl], in1=TMP1[:, sl]), sv, 1)
                    upd_tk[g2] = gadd

            return emit_cbc

        def emit_final_rows(k, par):
            sp = par * 1024
            if k == 0:
                dbg("mr_p%d" % phase, MROW[0:1, :])
                dbg("rr_p%d" % phase, RROW[0:1, :])
            rr_ = RROW[0:1, k * 512:(k + 1) * 512]
            mr_ = MROW[0:1, k * 512:(k + 1) * 512]
            B.op("vector", lambda e, par=par, rr_=rr_: e.scalar_tensor_tensor(
                out=rowp(par, "A"), in0=rowp(par, "A"), scalar=0.5, in1=rr_,
                op0=ALU.mult, op1=ALU.mult))
            B.op("vector", lambda e, par=par, rr_=rr_: e.scalar_tensor_tensor(
                out=rowp(par, "B"), in0=rowp(par, "B"), scalar=0.5, in1=rr_,
                op0=ALU.mult, op1=ALU.mult))
            B.op("vector", lambda e, par=par, mr_=mr_: e.tensor_mul(
                out=rowp(par, "c"), in0=rowp(par, "A"), in1=mr_))
            B.op("vector", lambda e, par=par, mr_=mr_, sp=sp: e.tensor_mul(
                out=SPROW[0:1, sp:sp + 512], in0=rowp(par, "B"), in1=mr_))
            B.op("vector", lambda e, par=par, sp=sp: e.tensor_add(
                out=SPROW[0:1, sp + 512:sp + 1024], in0=rowp(par, "c"),
                in1=SPROW[0:1, sp:sp + 512]))
            B.op("vector", lambda e, par=par, sp=sp: e.tensor_sub(
                out=rowp(par, "c"), in0=rowp(par, "c"), in1=SPROW[0:1, sp:sp + 512]))
            rows_tk = B.op("vector", lambda e, par=par: e.tensor_scalar(
                out=rowp(par, "A"), in0=rowp(par, "A"), scalar1=1.0, scalar2=None,
                op0=ALU.add), sv, 1)
            return rows_tk

        def emit_final_apply(k, par, rows_tk):
            xo = xo_of(k)
            sp = par * 1024
            bA, bB, bP, bM = (6, 7, 4, 5) if par == 0 else (2, 3, 0, 1)
            for bknum in (bA, bB, bP, bM):
                use_bank(bknum)
            B.wait("tensor", sv, rows_tk[1])
            mm(pb[bA][:, :], ONESF[:, :], rowp(par, "A"), True, True, inc=True)
            mm(pb[bB][:, :], ONESF[:, :], rowp(par, "B"), True, True, inc=True)
            mm(pb[bP][:, :], ONESF[:, :], SPROW[0:1, sp + 512:sp + 1024], True, True, inc=True)
            fb = mm(pb[bM][:, :], ONESF[:, :], rowp(par, "c"), True, True, inc=True)
            B.wait("vector", st, fb[1])
            B.op("vector", lambda e, xo=xo, bB=bB: e.tensor_tensor(
                out=r3(QFIN[:, 0:2048], 4, 512), in0=r3(XR[:, xo + 2048:xo + 4096], 4, 512),
                in1=bcast3(pb[bB].ap(), 4, 512), op=ALU.mult))
            B.op("vector", lambda e, xo=xo, bB=bB: e.tensor_tensor(
                out=r3(QFIN[:, 2048:4096], 4, 512), in0=r3(XR[:, xo:xo + 2048], 4, 512),
                in1=bcast3(pb[bB].ap(), 4, 512), op=ALU.mult), sv, 1)
            B.op("vector", lambda e, xo=xo, bA=bA: e.tensor_tensor(
                out=r3(XR[:, xo:xo + 4096], 8, 512), in0=r3(XR[:, xo:xo + 4096], 8, 512),
                in1=bcast3(pb[bA].ap(), 8, 512), op=ALU.mult), sv, 1)
            qhead = None
            for rq in range(4):
                opx = ALU.add if SGN[2 * rq] > 0 else ALU.subtract
                B.op("vector", lambda e, rq=rq, opx=opx, xo=xo: e.tensor_tensor(
                    out=XR[:, xo + rq * 1024: xo + (rq + 1) * 1024],
                    in0=XR[:, xo + rq * 1024: xo + (rq + 1) * 1024],
                    in1=QFIN[:, rq * 1024:(rq + 1) * 1024], op=opx), sv, 1)
                if rq == 1:
                    qhead = (sv, sv.val)
            qfull = (sv, sv.val)
            last_fin[0] = (qhead, qfull)
            for rq in range(4):
                rbank = pb[bP] if SGN[2 * rq] > 0 else pb[bM]
                B.op("vector", lambda e, rq=rq, rbank=rbank, xo=xo: e.tensor_sub(
                    out=XR[:, xo + rq * 1024: xo + (rq + 1) * 1024],
                    in0=XR[:, xo + rq * 1024: xo + (rq + 1) * 1024],
                    in1=bcast3(rbank.ap(), 2, 512)), sv, 1)
            fin_tk = (sv, sv.val)
            for bknum in (bA, bB, bP, bM):
                bank_ticket[bknum] = fin_tk
            if phase == 1:
                B.wait("sync", sv, fin_tk[1])
                B.op("sync", lambda e, xo=xo: e.dma_start(
                    out=AP(out_d.tensor, xo, [[NCH * T, 128], [1, 4096]]),
                    in_=XR[:, xo:xo + 4096]), so, 16)

        o0a = emit_prep(0, 0)
        emit_g0h0(0, 0, *o0a)
        o0b = emit_prep(1, 1)
        emit_g0h0(1, 1, *o0b)
        cz_tk = emit_gf_net()
        for kp in (0, 2):
            pend = None
            for cyc in range(3):
                pend = emit_cycle(kp, 0, cyc, cz_tk, mid=pend)
                if cyc == 2 and kp == 0:
                    o2 = emit_prep(2, 0)
                pend = emit_cycle(kp + 1, 1, cyc, cz_tk, mid=pend)
            if kp == 0:
                emit_g0h0(2, 0, *o2)
                o3 = emit_prep(3, 1)
                emit_g0h0(3, 1, *o3)
            ra = emit_final_rows(kp, 0)
            rb = emit_final_rows(kp + 1, 1)
            emit_final_apply(kp, 0, ra)
            emit_final_apply(kp + 1, 1, rb)

    def emit_attn(wqkv_tk):
        B.op("vector", lambda e: e.memset(VEXT[:, :], 0.0))
        for h in range(H):
            B.op("vector", lambda e, h=h: e.memset(VEXT[:, h * 66 + 64:h * 66 + 65], 1.0))
        B.op("vector", lambda e: e.memset(KSF[:, :], 0.0), sv, 1)
        vext_tk = (sv, sv.val)
        ksf_zero_tk = vext_tk

        kv_first = [True]
        XNP = [XNBF, XNBF2]
        PHP = [(PHA, PHB), (PHA2, PHB2)]
        xn_tks = [None] * NTILES
        ph_war = [0, 0]
        ln_st = {}

        def ln_a(k, half):
            xo = xo_of(k)
            XN = XNP[k % 2]
            B.wait("vector", st, st.val)
            cp = B.op("vector", lambda e, xo=xo, XN=XN, half=half: e.tensor_copy(
                out=XN[:, half * 2048:(half + 1) * 2048],
                in_=XR[:, xo + half * 2048: xo + (half + 1) * 2048]), sv, 1)
            if half == 0:
                ln_st[k] = {}
            ln_st[k]["cp%d" % half] = cp

        def ln_b(k):
            XN = XNP[k % 2]
            stt = ln_st[k]
            use_bank(6)
            B.wait("tensor", sv, stt["cp1"][1])
            for ci in range(8):
                mm(pb[6][0:1, :], ONECB[:, :], XN[:, ci * 512:(ci + 1) * 512],
                   ci == 0, ci == 7, inc=(ci == 7))
            stt["s1"] = (st, st.val)
            B.wait("scalar", sv, stt["cp1"][1])
            B.wait("scalar", st, stt["s1"][1])
            stt["sq"] = B.op("scalar", lambda e, XN=XN: e.activation(
                out=OUTPRE[:, :], in_=XN[:, :], func=AF.Square), ss, 1)

        def ln_c(k):
            stt = ln_st[k]
            use_bank(7)
            B.wait("tensor", ss, stt["sq"][1])
            for ci in range(8):
                mm(pb[7][0:1, :], ONECB[:, :], OUTPRE[:, ci * 512:(ci + 1) * 512],
                   ci == 0, ci == 7, inc=(ci == 7))
            s2 = (st, st.val)
            B.op("scalar", lambda e: e.mul(out=trow(R_T1), in_=pb[6][0:1, :], mul=1.0 / C), ss, 1)
            msq = B.op("scalar", lambda e: e.activation(
                out=trow(R_T3), in_=pb[6][0:1, :], func=AF.Square, scale=1.0 / C), ss, 1)
            bank_ticket[6] = msq
            B.wait("vector", ss, msq[1])
            B.wait("vector", st, s2[1])
            vv = B.op("vector", lambda e: e.scalar_tensor_tensor(
                out=trow(R_C), in0=pb[7][0:1, :], scalar=1.0 / C, in1=trow(R_T3),
                op0=ALU.mult, op1=ALU.subtract), sv, 1)
            bank_ticket[7] = vv
            B.wait("scalar", sv, vv[1])
            B.op("scalar", lambda e: e.activation(
                out=trow(R_C), in_=trow(R_C), func=AF.Ln, bias=cell(C_EPS)), ss, 1)
            rtk = B.op("scalar", lambda e: e.activation(
                out=trow(R_T2), in_=trow(R_C), func=AF.Exp, scale=-0.5), ss, 1)
            use_bank(6)
            use_bank(7)
            B.wait("tensor", ss, rtk[1])
            mm(pb[6][:, :], ONESF[:, :], trow(R_T1), True, True, inc=True)
            bc = mm(pb[7][:, :], ONESF[:, :], trow(R_T2), True, True, inc=True)
            stt["bc"] = (st, st.val)

        def ln_d(k):
            xo = xo_of(k)
            XN = XNP[k % 2]
            stt = ln_st[k]
            B.wait("vector", st, stt["bc"][1])
            B.op("vector", lambda e, xo=xo, XN=XN: e.tensor_sub(
                out=r3(XN[:, :], 8, 512), in0=r3(XR[:, xo:xo + 4096], 8, 512),
                in1=bcast3(pb[6].ap(), 8, 512)), sv, 1)
            xn = B.op("vector", lambda e, XN=XN: e.tensor_tensor(
                out=r3(XN[:, :], 8, 512), in0=r3(XN[:, :], 8, 512),
                in1=bcast3(pb[7].ap(), 8, 512), op=ALU.mult), sv, 1)
            bank_ticket[6] = xn
            bank_ticket[7] = xn
            return xn

        ln_a(0, 0)
        ln_a(0, 1)
        ln_b(0)
        ln_c(0)
        xn_tks[0] = ln_d(0)
        for k in range(NTILES):
            XN = XNP[k % 2]
            B.wait("tensor", sv, xn_tks[k][1])
            if k == 0:
                B.wait("tensor", wqkv_tk[0], wqkv_tk[1])
            for oj in range(8):
                bk = oj % 4
                use_bank(bk)
                for ci in range(8):
                    mm(pb[bk][:, :],
                       WQKV[:, ci * 3072 + oj * 128: ci * 3072 + (oj + 1) * 128],
                       XN[:, ci * 512:(ci + 1) * 512], ci == 0, ci == 7, inc=(ci == 7))
                mtk = (st, st.val)
                pa, pbb = PHP[oj % 2]
                B.wait("scalar", st, mtk[1])
                B.wait("scalar", sv, ph_war[oj % 2])
                B.op("scalar", lambda e, bk=bk, pa=pa: e.activation(
                    out=pa[:, :], in_=pb[bk][:, :], func=AF.Relu, scale=-1.0), ss, 1)
                e_tk = B.op("scalar", lambda e, pa=pa, pbb=pbb: e.activation(
                    out=pbb[:, :], in_=pa[:, :], func=AF.Exp, scale=-1.0), ss, 1)
                B.wait("vector", st, mtk[1])
                B.wait("vector", ss, e_tk[1])
                r1 = B.op("vector", lambda e, bk=bk, pa=pa: e.tensor_scalar(
                    out=pa[:, :], in0=pb[bk][:, :], scalar1=0.0, scalar2=None,
                    op0=ALU.max), sv, 1)
                bank_ticket[bk] = r1
                ad = B.op("vector", lambda e, oj=oj, k=k, pa=pa, pbb=pbb: e.tensor_add(
                    out=QPHI[:, k * 4096 + oj * 512: k * 4096 + (oj + 1) * 512],
                    in0=pbb[:, :], in1=pa[:, :]), sv, 1)
                ph_war[oj % 2] = ad[1]
            if k == 0:
                dbg("qphi_a", QPHI[:, 0:4096])

            for sub in range(4):
                for bk in range(4):
                    use_bank(bk)
                if kv_first[0]:
                    use_bank(4)
                    use_bank(5)
                for sl in range(4):
                    eoff = 1024 + sl * 512
                    for ci in range(8):
                        mm(pb[sl][:, :],
                           XN[:, ci * 512 + sub * 128: ci * 512 + (sub + 1) * 128],
                           WQKV[:, ci * 3072 + eoff: ci * 3072 + eoff + 512],
                           ci == 0, ci == 7, inc=(ci == 7))
                kv_tk = (st, st.val)
                for sl in range(2):
                    pa, pbb = PHP[sl]
                    B.wait("scalar", st, kv_tk[1] - 3 + sl)
                    B.wait("scalar", sv, ph_war[sl])
                    B.op("scalar", lambda e, sl=sl, pa=pa: e.activation(
                        out=pa[:, :], in_=pb[sl][:, :], func=AF.Relu, scale=-1.0), ss, 1)
                    e_tk = B.op("scalar", lambda e, pa=pa, pbb=pbb: e.activation(
                        out=pbb[:, :], in_=pa[:, :], func=AF.Exp, scale=-1.0), ss, 1)
                    B.wait("vector", st, kv_tk[1] - 3 + sl)
                    B.wait("vector", ss, e_tk[1])
                    r1 = B.op("vector", lambda e, sl=sl, pa=pa: e.tensor_scalar(
                        out=pa[:, :], in0=pb[sl][:, :], scalar1=0.0, scalar2=None,
                        op0=ALU.max), sv, 1)
                    bank_ticket[sl] = r1
                    ad = B.op("vector", lambda e, sl=sl, pa=pa, pbb=pbb: e.tensor_add(
                        out=KTM[:, sl * 512:(sl + 1) * 512], in0=pbb[:, :], in1=pa[:, :]), sv, 1)
                    ph_war[sl] = ad[1]
                for sl in range(2):
                    B.wait("vector", st, kv_tk[1] - 1 + sl)
                    vtk2 = B.op("vector", lambda e, sl=sl: e.tensor_copy(
                        out=AP(VEXT, sl * 8 * 66, [[1056, 128], [66, 8], [1, 64]]),
                        in_=r3(pb[2 + sl][:, :], 8, 64)), sv, 1)
                    bank_ticket[2 + sl] = vtk2
                kv_ready = (sv, sv.val)
                if k + 1 < NTILES:
                    if sub == 0:
                        ln_a(k + 1, 0)
                    elif sub == 1:
                        ln_a(k + 1, 1)
                    elif sub == 2:
                        ln_b(k + 1)
                    else:
                        ln_c(k + 1)
                        xn_tks[k + 1] = ln_d(k + 1)
                B.wait("tensor", sv, kv_ready[1])
                first = kv_first[0]
                kv_first[0] = False
                last = (k == NTILES - 1 and sub == 3)
                for h in range(H):
                    p2 = h // 2
                    bk = 4 + p2 // 4
                    cb = (p2 % 4) * 65
                    pbase = (h % 2) * 64
                    mm(pb[bk][pbase:pbase + 64, cb:cb + 65],
                       KTM[:, h * 64:(h + 1) * 64],
                       VEXT[:, h * 66:h * 66 + 65],
                       first and (h % 8 == 0), last, inc=(h == H - 1))
        kvdone_tk = (st, st.val)

        B.wait("sync", st, kvdone_tk[1])
        B.op("sync", lambda e: e.dma_start(out=EALL[:, :], in_=eall_d), sd, 16)
        eall_tk = (sd, sd.val)
        B.wait("vector", st, kvdone_tk[1])
        for b in range(2):
            B.op("vector", lambda e, b=b: e.tensor_copy(
                out=KVSB[:, b * 260:(b + 1) * 260], in_=pb[4 + b][:, 0:260]), sv, 1)
        kvsb_tk = (sv, sv.val)
        bank_ticket[4] = kvsb_tk
        bank_ticket[5] = kvsb_tk
        B.wait("sync", sv, kvsb_tk[1])
        B.op("sync", lambda e: e.dma_start(out=cck_in.ap(), in_=KVSB[:, :]), sd, 16)
        B.wait("gpsimd", sd, sd.val)
        B.op("gpsimd", lambda e: e.collective_compute(
            "AllReduce", ALU.add, replica_groups=[list(range(NCORES))],
            ins=[cck_in.ap()], outs=[cck_out.ap()]), scc, 1)
        kv_ar = (scc, scc.val)
        B.wait("gpsimd", st, kvdone_tk[1])
        B.op("gpsimd", lambda e: e.dma_start(out=WPROJ[:, :], in_=wproj_d), sg, 16)
        wproj_tk = (sg, sg.val)
        B.wait("gpsimd", kv_ar[0], kv_ar[1])
        B.wait("gpsimd", ksf_zero_tk[0], ksf_zero_tk[1])
        B.op("gpsimd", lambda e: e.dma_start(
            out=AP(KSF, 0, [[128, 64], [18, 8], [1, 1]]),
            in_=AP(cck_out, 64, [[520, 64], [65, 8], [1, 1]]),
            allow_slow_non_contiguous=True), sg, 16)
        B.op("gpsimd", lambda e: e.dma_start(
            out=AP(KSF, 64 * 128 + 1, [[128, 64], [18, 8], [1, 1]]),
            in_=AP(cck_out, 64 * 520 + 64, [[520, 64], [65, 8], [1, 1]]),
            allow_slow_non_contiguous=True), sg, 16)
        ksf_tk = (sg, sg.val)
        B.op("gpsimd", lambda e: e.dma_start(out=KVBF[:, :], in_=cck_out.ap()), sg, 16)
        kvpost_tk = (sg, sg.val)
        dbg("kvsb_a", KVSB[:, :])

        B.wait("tensor", ksf_tk[0], ksf_tk[1])
        B.wait("tensor", eall_tk[0], eall_tk[1])
        kv_waited = [False]
        for k in range(NTILES):
            xo = xo_of(k)
            use_bank(6)
            for ci in range(8):
                mm(pb[6][0:16, :], KSF[:, ci * 16:(ci + 1) * 16],
                   QPHI[:, k * 4096 + ci * 512: k * 4096 + (ci + 1) * 512],
                   ci == 0, ci == 7, inc=(ci == 7))
            n_tk = (st, st.val)
            B.wait("vector", st, n_tk[1])
            nv = B.op("vector", lambda e: e.tensor_scalar(
                out=NRS[:, :], in0=pb[6][0:16, :], scalar1=EPS_AT, scalar2=None,
                op0=ALU.add), sv, 1)
            bank_ticket[6] = nv
            B.wait("scalar", sv, nv[1])
            B.op("scalar", lambda e: e.activation(
                out=REC16[:, :], in_=NRS[:, :], func=AF.Ln), ss, 1)
            rec_tk = B.op("scalar", lambda e: e.activation(
                out=REC16[:, :], in_=REC16[:, :], func=AF.Exp, scale=-1.0), ss, 1)
            B.wait("tensor", ss, rec_tk[1])
            if not kv_waited[0]:
                kv_waited[0] = True
                B.wait("tensor", kvpost_tk[0], kvpost_tk[1])
                B.wait("tensor", wproj_tk[0], wproj_tk[1])
            for j in range(8):
                bko = j % 2
                bkr = 2 + (j % 2)
                use_bank(bko)
                use_bank(bkr)
                mm(pb[bko][0:64, :], KVBF[0:64, j * 65:j * 65 + 64],
                   QPHI[0:64, k * 4096 + j * 512: k * 4096 + (j + 1) * 512], True, True)
                mm(pb[bko][64:128, :], KVBF[64:128, j * 65:j * 65 + 64],
                   QPHI[64:128, k * 4096 + j * 512: k * 4096 + (j + 1) * 512],
                   True, True, inc=True)
                ro = mm(pb[bkr][:, :], EALL[:, j * 128:(j + 1) * 128], REC16[:, :],
                        True, True, inc=True)
                rb = PHA if (j % 2 == 0) else PHB
                B.wait("scalar", st, ro[1])
                B.wait("scalar", sv, sv.val)
                rc = B.op("scalar", lambda e, bkr=bkr, rb=rb: e.copy(
                    out=rb[:, :], in_=pb[bkr][:, :]), ss, 1)
                bank_ticket[bkr] = rc
                B.wait("vector", st, ro[1])
                B.wait("vector", ss, rc[1])
                dv = B.op("vector", lambda e, j=j, bko=bko, rb=rb: e.tensor_mul(
                    out=OUTPRE[:, j * 512:(j + 1) * 512], in0=pb[bko][:, :],
                    in1=rb[:, :]), sv, 1)
                bank_ticket[bko] = dv
            op_tk = (sv, sv.val)
            if k == 0:
                dbg("outpre_a", OUTPRE[:, :])
            B.wait("tensor", sv, op_tk[1])
            for oj in range(8):
                bk = 5 + (oj % 2)
                use_bank(bk)
                for ci in range(8):
                    mm(pb[bk][:, :],
                       WPROJ[:, ci * 1024 + oj * 128: ci * 1024 + (oj + 1) * 128],
                       OUTPRE[:, ci * 512:(ci + 1) * 512], ci == 0, ci == 7, inc=(ci == 7))
                mtk = (st, st.val)
                B.wait("vector", st, mtk[1])
                ev = B.op("vector", lambda e, oj=oj, bk=bk, xo=xo: e.scalar_tensor_tensor(
                    out=XR[:, xo + oj * 512: xo + (oj + 1) * 512], in0=pb[bk][:, :],
                    scalar=BIASC[:, oj:oj + 1],
                    in1=XR[:, xo + oj * 512: xo + (oj + 1) * 512],
                    op0=ALU.add, op1=ALU.add), sv, 1)
                bank_ticket[bk] = ev

    B.wait("vector", sd, x_tk[1])
    B.wait("vector", sv, cells_tk[1])
    B.wait("tensor", sg, wffn_tk[1])
    emit_ffn(0)
    if stage == 1:
        emit_store_xr()
        return end_block()

    B.wait("gpsimd", st, st.val)
    B.wait("gpsimd", sv, sv.val)
    B.op("gpsimd", lambda e: e.dma_start(out=WQKV[:, :], in_=wqkv_d), sg, 16)
    wqkv_tk = (sg, sg.val)
    for eng in Builder.ENGS:
        for sem in (st, sv, ss):
            B.wait(eng, sem)
    emit_attn(wqkv_tk)
    if stage == 2:
        emit_store_xr()
        return end_block()

    B.wait("gpsimd", st, st.val)
    B.wait("gpsimd", sv, sv.val)
    B.op("gpsimd", lambda e: e.dma_start(out=W1SB[:, :], in_=w1_d), sg, 16)
    w1_tk2 = (sg, sg.val)
    B.op("gpsimd", lambda e: e.dma_start(out=W2SB[:, :], in_=w2_d), sg, 16)
    B.op("gpsimd", lambda e: e.dma_start(out=W3SB[:, :], in_=w3_d), sg, 16)
    B.op("gpsimd", lambda e: e.dma_start(out=W4SB[:, :], in_=w4_d), sg, 16)
    B.op("gpsimd", lambda e: e.dma_start(out=BROWS[:, :], in_=brows_d), sg, 16)
    B.op("gpsimd", lambda e: e.dma_start(out=GW1SB[:, :], in_=gw1_d), sg, 16)
    B.op("gpsimd", lambda e: e.dma_start(out=GW2SB[:, :], in_=gw2_d), sg, 16)
    B.op("gpsimd", lambda e: e.dma_start(out=GW3SB[:, :], in_=gw3_d), sg, 16)
    gw_tk[1] = (sg, sg.val)
    for eng in Builder.ENGS:
        for sem in (st, sv, ss):
            B.wait(eng, sem)
    emit_ffn(1, w1_tk2=w1_tk2)
    return end_block()


def _images(inputs):
    f32 = np.float32
    x = np.asarray(inputs["L0_attn_x"], f32).reshape(TFULL, C)
    Wqkv = np.asarray(inputs["L0_attn_Wqkv"], f32)
    Wproj = np.asarray(inputs["L0_attn_Wproj"], f32)
    bproj = np.asarray(inputs["L0_attn_bproj"], f32)
    w = {i: np.asarray(inputs["L0_ffn_cn_w%d" % i], f32) for i in (1, 2, 3, 4)}
    b = {i: np.asarray(inputs["L0_ffn_cn_b%d" % i], f32) for i in (1, 2, 3, 4)}
    gw = {i: np.asarray(inputs["L0_ffn_gc_w%d" % i], f32) for i in (1, 2, 3)}
    gb = {i: np.asarray(inputs["L0_ffn_gc_b%d" % i], f32) for i in (1, 2, 3)}
    ar = np.asarray(inputs["L0_ffn_adaptive_range"], f32)

    def img(wm, kch):
        K, O = wm.shape
        return np.ascontiguousarray(
            wm.reshape(kch, 128, O).transpose(1, 0, 2).reshape(128, kch * O))

    com = {
        "w1": img(w[1], 8), "w2": img(w[2], 8), "w3": img(w[3], 4),
        "w4": np.ascontiguousarray(w[4].reshape(2, 128).T),
        "gw1": img(gw[1], 8), "gw2": img(gw[2], 4),
        "gw3": np.ascontiguousarray(gw[3].reshape(2, 128).T),
        "wqkv": img(np.ascontiguousarray(Wqkv.T), 8),
        "wproj": img(np.ascontiguousarray(Wproj.T), 8),
    }
    biasc = np.zeros((128, 16), f32)
    biasc[:, 0:8] = bproj.reshape(8, 128).T
    biasc[:, 8:16] = b[1].reshape(8, 128).T
    com["biasc"] = biasc
    brows = np.zeros((1, 2560), f32)
    brows[0, 0:1024] = b[1]
    brows[0, 1024:1536] = b[2]
    brows[0, 1536:1792] = b[3]
    brows[0, 1792:2304] = gb[1]
    brows[0, 2304:2560] = gb[2]
    com["brows"] = brows
    com["scal"] = np.array([[float(ar[0]), float(b[4][0]), float(gb[3][0]), 0.0]], f32)
    eall = np.zeros((16, 1024), f32)
    for j in range(8):
        eall[2 * j, j * 128: j * 128 + 64] = 1.0
        eall[2 * j + 1, j * 128 + 64: (j + 1) * 128] = 1.0
    com["eall"] = eall

    per_core = []
    for i in range(NCORES):
        sl = x[i * T:(i + 1) * T]
        xi = np.ascontiguousarray(
            sl.reshape(NTILES, TT, NCH, 128).transpose(3, 0, 2, 1).reshape(128, NCH * T))
        m = dict(com)
        m["x"] = xi
        per_core.append(m)
    return per_core


def kernel(**inputs):
    nc = build_program()
    in_maps = _images(inputs)
    res = run_bass_kernel_spmd(nc, in_maps, core_ids=list(range(NCORES)))
    out = np.empty((TFULL, C), np.float32)
    for i in range(NCORES):
        oi = np.asarray(res.results[i]["out"])
        out[i * T:(i + 1) * T] = (
            oi.reshape(128, NTILES, NCH, TT).transpose(1, 3, 2, 0).reshape(T, C))
    return out.reshape(1, TFULL, C)


# version 26
# speedup vs baseline: 1.1331x; 1.0517x over previous
import numpy as np
import concourse.bass as bass
import concourse.mybir as mybir
from concourse.ap import AP
from concourse.bass_utils import run_bass_kernel_spmd

F32 = mybir.dt.float32
BF16 = mybir.dt.bfloat16
AF = mybir.ActivationFunctionType
ALU = mybir.AluOpType

NCORES = 8
TFULL = 16384
C = 1024
NCH = 8
T = TFULL // NCORES
TT = 512
NTILES = T // TT
H = 16
EPS_LN = 1e-5
EPS_AT = 1e-6

SGN = [1.0, 1.0, -1.0, -1.0, -1.0, -1.0, 1.0, 1.0]
SIG = lambda j: (j + 4) % 8
XBFN_SLOT = {6: 0, 7: 1, 0: 2, 1: 3}


class Sem:
    def __init__(self, nc, name):
        self.h = nc.alloc_semaphore(name)
        self.val = 0


class Builder:
    ENGS = ("sync", "gpsimd", "vector", "scalar", "tensor")

    def __init__(self):
        self.ops = {e: [] for e in self.ENGS}

    def op(self, eng, fn, inc=None, n=1):
        if inc is not None:
            inc.val += n
            def run(e, fn=fn, sem=inc.h, n=n):
                fn(e).then_inc(sem, n)
            self.ops[eng].append(run)
            return (inc, inc.val)
        self.ops[eng].append(lambda e, fn=fn: fn(e))
        return None

    def wait(self, eng, sem, val=None):
        v = sem.val if val is None else val
        if v <= 0:
            return
        self.ops[eng].append(lambda e, s=sem.h, v=v: e.wait_ge(s, v))

    def wait_t(self, eng, ticket):
        if ticket is not None:
            self.wait(eng, ticket[0], ticket[1])


def bcast3(tile_ap, reps, n):
    return AP(tile_ap.tensor, tile_ap.offset,
              [[tile_ap.ap[0][0], tile_ap.ap[0][1]], [0, reps], [1, n]])


def r3(ap2d, c, n):
    return ap2d.rearrange("p (c n) -> p c n", c=c, n=n)


def build_program(stage=3, dump=()):
    nc = bass.Bass("TRN2", target_bir_lowering=False, detect_race_conditions=False)

    x_d = nc.dram_tensor("x", [128, NCH * T], F32, kind="ExternalInput").ap()
    out_d = nc.dram_tensor("out", [128, NCH * T], F32, kind="ExternalOutput").ap()
    w1_d = nc.dram_tensor("w1", [128, 8192], F32, kind="ExternalInput").ap()
    w2_d = nc.dram_tensor("w2", [128, 4096], F32, kind="ExternalInput").ap()
    w3_d = nc.dram_tensor("w3", [128, 1024], F32, kind="ExternalInput").ap()
    w4_d = nc.dram_tensor("w4", [128, 2], F32, kind="ExternalInput").ap()
    gw1_d = nc.dram_tensor("gw1", [128, 4096], F32, kind="ExternalInput").ap()
    gw2_d = nc.dram_tensor("gw2", [128, 1024], F32, kind="ExternalInput").ap()
    gw3_d = nc.dram_tensor("gw3", [128, 2], F32, kind="ExternalInput").ap()
    wqkv_d = nc.dram_tensor("wqkv", [128, 24576], F32, kind="ExternalInput").ap()
    wproj_d = nc.dram_tensor("wproj", [128, 8192], F32, kind="ExternalInput").ap()
    biasc_d = nc.dram_tensor("biasc", [128, 16], F32, kind="ExternalInput").ap()
    brows_d = nc.dram_tensor("brows", [1, 2560], F32, kind="ExternalInput").ap()
    scal_d = nc.dram_tensor("scal", [1, 4], F32, kind="ExternalInput").ap()
    eall_d = nc.dram_tensor("eall", [16, 1024], F32, kind="ExternalInput").ap()
    ccg_in = nc.dram_tensor("ccg_in", [128, 8], F32, kind="Internal")
    ccg_out = nc.dram_tensor("ccg_out", [128, 8], F32, kind="Internal")
    cck_in = nc.dram_tensor("cck_in", [128, 520], F32, kind="Internal")
    cck_out = nc.dram_tensor("cck_out", [128, 520], F32, kind="Internal")
    ccs_in = nc.dram_tensor("ccs_in", [128, 8], F32, kind="Internal")
    ccs_out = nc.dram_tensor("ccs_out", [128, 8], F32, kind="Internal")

    base = nc.sbuf_base
    cur = [base]
    offs = {}

    def alloc(name, shape, dtype, at=None):
        nbytes = int(np.prod(shape[1:])) * (4 if dtype == F32 else 2)
        if at is None:
            o = (cur[0] + 63) & ~63
            cur[0] = o + nbytes
        else:
            o = at
        t = nc.alloc_sbuf_tensor_at(name, shape, dtype, offset=o)
        offs[name] = o
        return t

    XR = alloc("XR", [128, NCH * T], F32)
    TROW = alloc("TROW", [1, 3072], F32)
    SPROW = alloc("SPROW", [1, 2048], F32)
    CELLS = alloc("CELLS", [128, 16], F32)
    BIASC = alloc("BIASC", [128, 16], F32)
    ONESB = alloc("ONESB", [1, 512], BF16)
    ONESF = alloc("ONESF", [1, 128], F32)
    ONE1B = alloc("ONE1B", [1, 32], BF16)
    ONECB = alloc("ONECB", [128, 1], BF16)
    PERS_END = cur[0]

    cur[0] = PERS_END
    W1SB = alloc("W1SB", [128, 8192], BF16)
    W2SB = alloc("W2SB", [128, 4096], BF16)
    W3SB = alloc("W3SB", [128, 1024], BF16)
    W4SB = alloc("W4SB", [128, 2], BF16)
    BROWS = alloc("BROWS", [1, 2560], BF16)
    CBROW = alloc("CBROW", [1, 512], BF16)
    CBROW2 = alloc("CBROW2", [1, 512], BF16)
    GCP = alloc("GCP", [128, 32], F32)
    GCN = alloc("GCN", [128, 8], F32)
    GCNB = alloc("GCNB", [128, 8], BF16)
    GH1 = alloc("GH1", [128, 4], BF16)
    GH2 = alloc("GH2", [128, 2], BF16)
    MROW = alloc("MROW", [1, 2048], F32)
    RROW = alloc("RROW", [1, 2048], F32)
    XBF = alloc("XBF", [128, 4096], BF16)
    XBFN = alloc("XBFN", [128, 2048], BF16)
    GPAIR = [alloc("G_a", [128, 4096], BF16), alloc("G_b", [128, 4096], BF16)]
    HPAIR = [alloc("H_a", [128, 4096], BF16), alloc("H_b", [128, 4096], BF16)]
    SCR = []
    for pi, par in enumerate(("a", "b")):
        s0 = (cur[0] + 63) & ~63
        d = {
            "H1G": alloc("H1G_" + par, [128, 4096], BF16, at=s0),
            "H2G": alloc("H2G_" + par, [128, 2048], BF16, at=s0 + 8192),
            "H3G": alloc("H3G_" + par, [128, 1024], BF16, at=s0 + 12288),
            "CBC": alloc("CBC_" + par, [128, 512], BF16, at=offs["XBFN"] + pi * 1024),
        }
        cur[0] = s0 + 14336
        SCR.append(d)
    GW1SB = alloc("GW1SB", [128, 4096], BF16, at=offs["H1G_a"])
    GW2SB = alloc("GW2SB", [128, 1024], BF16, at=offs["H2G_a"])
    GW3SB = alloc("GW3SB", [128, 2], BF16, at=offs["H3G_a"])
    QFIN = alloc("QFIN", [128, 4096], F32, at=offs["H1G_a"])
    TMP1 = alloc("TMP1", [128, 4096], BF16, at=offs["XBF"])
    FFN_END = cur[0]

    cur[0] = PERS_END
    WQKV = alloc("WQKV", [128, 24576], BF16)
    WPROJ = alloc("WPROJ", [128, 8192], BF16, at=offs["WQKV"])
    QPHI = alloc("QPHI", [128, NCH * T], BF16)
    XNBF = alloc("XNBF", [128, 4096], BF16)
    XNBF2 = alloc("XNBF2", [128, 4096], BF16)
    KTM = alloc("KTM", [128, 1024], BF16)
    VEXT = alloc("VEXT", [128, 1056], BF16)
    PHA = alloc("PHA", [128, 512], BF16)
    PHB = alloc("PHB", [128, 512], BF16)
    PHA2 = alloc("PHA2", [128, 512], BF16)
    PHB2 = alloc("PHB2", [128, 512], BF16)
    OUTPRE = alloc("OUTPRE", [128, 4096], BF16)
    KVSB = alloc("KVSB", [128, 520], F32)
    KVBF = alloc("KVBF", [128, 520], BF16)
    NRS = alloc("NRS", [16, 512], F32)
    REC16 = alloc("REC16", [16, 512], F32)
    EALL = alloc("EALL", [16, 1024], F32, at=offs["KTM"])
    KS8 = alloc("KS8", [128, 8], F32)
    KSF = alloc("KSF", [128, 128], BF16)
    ATTN_END = cur[0]

    cap = nc.sbuf_top
    assert FFN_END <= cap, (FFN_END, cap)
    assert ATTN_END <= cap, (ATTN_END, cap)

    pb = [nc.alloc_psum_tensor("pb%d" % i, [128, 512], F32) for i in range(8)]

    sd = Sem(nc, "sd")
    sg = Sem(nc, "sg")
    sv = Sem(nc, "sv")
    ss = Sem(nc, "ss")
    st = Sem(nc, "st")
    scc = Sem(nc, "scc")
    so = Sem(nc, "so")

    B = Builder()
    bank_ticket = [None] * 8

    def use_bank(i):
        B.wait_t("tensor", bank_ticket[i])

    def dbg(tag, ap, dtype=None):
        if tag not in dump:
            return
        dt = ap.tensor.dtype if dtype is None else dtype
        dd = nc.dram_tensor("dbg_" + tag, list(ap.shape), dt, kind="ExternalOutput").ap()
        for sem in (st, sv, ss):
            B.wait("sync", sem)
        B.op("sync", lambda e, dd=dd, ap=ap: e.dma_start(out=dd, in_=ap), sd, 16)
        for eng in ("vector", "scalar", "tensor"):
            B.wait(eng, sd)

    C_AR, C_B4, C_GB3, C_EPS, C_HB4, C_HGB3, C_K1, C_K2, C_C3, C_GTH, C_GF, C_C0 = range(12)

    def cell(i):
        return CELLS[0:1, i:i + 1]

    R_C, R_A, R_B, R_T1, R_T2, R_T3 = (0, 512, 1024, 1536, 2048, 2560)

    def trow(slot):
        return TROW[0:1, slot:slot + 512]

    def mm(out, lhsT, rhs, start, stop, inc=False):
        if inc:
            return B.op("tensor", lambda e: e.matmul(out=out, lhsT=lhsT, rhs=rhs,
                                                     start=start, stop=stop), st, 1)
        return B.op("tensor", lambda e: e.matmul(out=out, lhsT=lhsT, rhs=rhs,
                                                 start=start, stop=stop))

    def end_block():
        B.wait("gpsimd", so, so.val)
        B.wait("sync", so, so.val)
        with nc.Block() as block:
            for eng in Builder.ENGS:
                fns = B.ops[eng]
                def body(e, fns=fns):
                    for f in fns:
                        f(e)
                getattr(block, eng)(body)
        return nc

    def xo_of(k):
        return k * 4096

    def emit_store_xr():
        for k in range(NTILES):
            xo = xo_of(k)
            B.wait("sync", sv, sv.val)
            B.wait("sync", ss, ss.val)
            B.op("sync", lambda e, xo=xo: e.dma_start(
                out=AP(out_d.tensor, xo, [[NCH * T, 128], [1, 4096]]),
                in_=XR[:, xo:xo + 4096]), so, 16)

    B.op("sync", lambda e: e.dma_start(out=XR[:, :], in_=x_d), sd, 16)
    x_tk = (sd, sd.val)
    B.op("sync", lambda e: e.dma_start(out=BIASC[:, :], in_=biasc_d), sd, 16)
    B.op("sync", lambda e: e.dma_start(out=CELLS[0:1, 0:4], in_=scal_d), sd, 16)
    init_d = (sd, sd.val)
    B.op("gpsimd", lambda e: e.dma_start(out=W1SB[:, :], in_=w1_d), sg, 16)
    B.op("gpsimd", lambda e: e.dma_start(out=W2SB[:, :], in_=w2_d), sg, 16)
    B.op("gpsimd", lambda e: e.dma_start(out=W3SB[:, :], in_=w3_d), sg, 16)
    B.op("gpsimd", lambda e: e.dma_start(out=W4SB[:, :], in_=w4_d), sg, 16)
    B.op("gpsimd", lambda e: e.dma_start(out=BROWS[:, :], in_=brows_d), sg, 16)
    wffn_tk = (sg, sg.val)
    B.op("gpsimd", lambda e: e.dma_start(out=GW1SB[:, :], in_=gw1_d), sg, 16)
    B.op("gpsimd", lambda e: e.dma_start(out=GW2SB[:, :], in_=gw2_d), sg, 16)
    B.op("gpsimd", lambda e: e.dma_start(out=GW3SB[:, :], in_=gw3_d), sg, 16)
    gw_tk = {0: (sg, sg.val)}
    B.op("vector", lambda e: e.memset(ONESB[:, :], 1.0))
    B.op("vector", lambda e: e.memset(ONESF[:, :], 1.0))
    B.op("vector", lambda e: e.memset(ONE1B[:, :], 1.0))
    B.op("vector", lambda e: e.memset(ONECB[:, :], 1.0))
    B.op("vector", lambda e: e.memset(CELLS[:, C_EPS:C_EPS + 1], EPS_LN), sv, 1)
    B.wait("vector", sd, init_d[1])
    B.op("vector", lambda e: e.tensor_scalar(out=cell(C_HB4), in0=cell(C_B4),
                                             scalar1=0.5, scalar2=None, op0=ALU.mult))
    B.op("vector", lambda e: e.tensor_scalar(out=cell(C_HGB3), in0=cell(C_GB3),
                                             scalar1=0.5, scalar2=None, op0=ALU.mult))
    B.op("vector", lambda e: e.tensor_scalar(out=cell(C_K1), in0=cell(C_AR),
                                             scalar1=0.3, scalar2=None, op0=ALU.mult))
    B.op("vector", lambda e: e.tensor_scalar(out=cell(C_K2), in0=cell(C_AR),
                                             scalar1=1.4, scalar2=None, op0=ALU.mult))
    B.op("vector", lambda e: e.tensor_scalar(out=cell(C_C3), in0=cell(C_AR),
                                             scalar1=-0.7, scalar2=0.1,
                                             op0=ALU.mult, op1=ALU.add), sv, 1)
    cells_tk = (sv, sv.val)

    def emit_ffn(phase, w1_tk2=None):
        SQB = GPAIR[0]
        cp = [None] * (NTILES + 1)
        cp[0] = B.op("vector", lambda e: e.tensor_copy(
            out=XBF[:, :], in_=XR[:, 0:4096]), sv, 1)
        for k in range(NTILES):
            xo = xo_of(k)
            use_bank(6)
            B.wait("tensor", cp[k][0], cp[k][1])
            for ci in range(8):
                mm(pb[6][0:1, :], ONECB[:, :], XBF[:, ci * 512:(ci + 1) * 512],
                   ci == 0, ci == 7, inc=(ci == 7))
            s1_tk = (st, st.val)
            B.wait("scalar", st, s1_tk[1])
            B.wait("scalar", sv, sv.val)
            stk = B.op("scalar", lambda e: e.activation(
                out=SQB[:, :], in_=XBF[:, :], func=AF.Square), ss, 1)
            use_bank(7)
            B.wait("tensor", ss, stk[1])
            for ci in range(8):
                mm(pb[7][0:1, :], ONECB[:, :], SQB[:, ci * 512:(ci + 1) * 512],
                   ci == 0, ci == 7, inc=(ci == 7))
            s2_tk = (st, st.val)
            if k + 1 < NTILES:
                xo2 = xo_of(k + 1)
                B.wait("vector", st, s1_tk[1])
                B.wait("vector", ss, stk[1])
                cp[k + 1] = B.op("vector", lambda e, xo2=xo2: e.tensor_copy(
                    out=XBF[:, :], in_=XR[:, xo2:xo2 + 4096]), sv, 1)
            B.op("scalar", lambda e, k=k: e.mul(
                out=MROW[0:1, k * 512:(k + 1) * 512], in_=pb[6][0:1, :], mul=1.0 / C), ss, 1)
            B.op("scalar", lambda e: e.activation(
                out=trow(R_T1), in_=pb[6][0:1, :], func=AF.Square, scale=1.0 / C), ss, 1)
            msq_tk = (ss, ss.val)
            bank_ticket[6] = msq_tk
            B.wait("vector", ss, msq_tk[1])
            B.wait("vector", st, s2_tk[1])
            vv = B.op("vector", lambda e: e.scalar_tensor_tensor(
                out=trow(R_T2), in0=pb[7][0:1, :], scalar=1.0 / C, in1=trow(R_T1),
                op0=ALU.mult, op1=ALU.subtract), sv, 1)
            bank_ticket[7] = vv
            B.wait("scalar", sv, vv[1])
            B.op("scalar", lambda e: e.activation(
                out=trow(R_T3), in_=trow(R_T2), func=AF.Ln, bias=cell(C_EPS)), ss, 1)
            rtk = B.op("scalar", lambda e, k=k: e.activation(
                out=RROW[0:1, k * 512:(k + 1) * 512], in_=trow(R_T3), func=AF.Exp,
                scale=-0.5), ss, 1)
            use_bank(0)
            use_bank(1)
            B.wait("tensor", ss, rtk[1])
            mm(pb[0][:, :], ONESF[:, :], MROW[0:1, k * 512:(k + 1) * 512], True, True, inc=True)
            bc = mm(pb[1][:, :], ONESF[:, :], RROW[0:1, k * 512:(k + 1) * 512], True, True, inc=True)
            B.wait("vector", st, bc[1])
            B.op("vector", lambda e, xo=xo: e.tensor_sub(
                out=SQB[:, :], in0=XR[:, xo:xo + 4096], in1=bcast3(pb[0].ap(), 8, 512)), sv, 1)
            for ci in range(8):
                B.op("vector", lambda e, ci=ci, k=k: e.scalar_tensor_tensor(
                    out=SQB[:, ci * 512:(ci + 1) * 512],
                    in0=SQB[:, ci * 512:(ci + 1) * 512], scalar=1.0,
                    in1=pb[1][:, :], op0=ALU.bypass, op1=ALU.mult,
                    accum_out=GCP[:, k * 8 + ci:k * 8 + ci + 1]), sv, 1)
            gc_tk = (sv, sv.val)
            bank_ticket[0] = gc_tk
            bank_ticket[1] = gc_tk

        B.op("vector", lambda e: e.tensor_add(out=GCN[:, :], in0=GCP[:, 0:8], in1=GCP[:, 8:16]))
        B.op("vector", lambda e: e.tensor_add(out=GCN[:, :], in0=GCN[:, :], in1=GCP[:, 16:24]))
        B.op("vector", lambda e: e.tensor_add(out=GCN[:, :], in0=GCN[:, :], in1=GCP[:, 24:32]), sv, 1)
        B.wait("sync", sv, sv.val)
        B.op("sync", lambda e: e.dma_start(out=ccg_in.ap(), in_=GCN[:, :]), sd, 16)
        B.wait("gpsimd", sd, sd.val)
        B.op("gpsimd", lambda e: e.collective_compute(
            "AllReduce", ALU.add, replica_groups=[list(range(NCORES))],
            ins=[ccg_in.ap()], outs=[ccg_out.ap()]), scc, 1)
        B.wait("sync", scc, scc.val)
        B.op("sync", lambda e: e.dma_start(out=GCN[:, :], in_=ccg_out.ap()), sd, 16)
        ar_done = (sd, sd.val)

        def emit_prep(k, par):
            xo = xo_of(k)
            B.wait("vector", st, st.val)
            use_bank(0)
            use_bank(1)
            mm(pb[0][:, :], ONESF[:, :], MROW[0:1, k * 512:(k + 1) * 512], True, True, inc=True)
            bc = mm(pb[1][:, :], ONESF[:, :], RROW[0:1, k * 512:(k + 1) * 512], True, True, inc=True)
            B.wait("vector", st, bc[1])
            B.op("vector", lambda e, xo=xo: e.tensor_sub(
                out=XBF[:, :], in0=XR[:, xo:xo + 4096], in1=bcast3(pb[0].ap(), 8, 512)), sv, 1)
            o0 = B.op("vector", lambda e: e.tensor_tensor(
                out=r3(XBF[:, :], 8, 512), in0=r3(XBF[:, :], 8, 512),
                in1=bcast3(pb[1].ap(), 8, 512), op=ALU.mult), sv, 1)
            bank_ticket[0] = o0
            bank_ticket[1] = o0
            B.op("vector", lambda e: e.tensor_scalar(
                out=XBFN[:, 0:1024], in0=XBF[:, 6 * 512:8 * 512], scalar1=-1.0,
                scalar2=None, op0=ALU.mult))
            on = B.op("vector", lambda e: e.tensor_scalar(
                out=XBFN[:, 1024:2048], in0=XBF[:, 0:1024], scalar1=-1.0,
                scalar2=None, op0=ALU.mult), sv, 1)
            if k == 0:
                dbg("out0_p%d" % phase, XBF[:, :])
            return o0, on

        def emit_g0h0(k, par, o0, on):
            GT, HT = GPAIR[par], HPAIR[par]
            B.wait("tensor", sv, o0[1])
            if phase == 1 and k == 0:
                B.wait("tensor", w1_tk2[0], w1_tk2[1])
            for oj in range(8):
                bk = 2 + (oj % 2)
                use_bank(bk)
                for ci in range(8):
                    mm(pb[bk][:, :],
                       W1SB[:, ci * 1024 + oj * 128: ci * 1024 + (oj + 1) * 128],
                       XBF[:, ci * 512:(ci + 1) * 512], ci == 0, ci == 7, inc=(ci == 7))
                mtk = (st, st.val)
                B.wait("scalar", st, mtk[1])
                etk = B.op("scalar", lambda e, oj=oj, bk=bk, GT=GT: e.copy(
                    out=GT[:, oj * 512:(oj + 1) * 512], in_=pb[bk][:, :]), ss, 1)
                bank_ticket[bk] = etk
            B.wait("tensor", sv, on[1])
            for oj in range(8):
                bk = 2 + (oj % 2)
                use_bank(bk)
                for ci in range(8):
                    if SGN[ci] > 0:
                        rhs = XBF[:, SIG(ci) * 512:(SIG(ci) + 1) * 512]
                    else:
                        sl = XBFN_SLOT[SIG(ci)]
                        rhs = XBFN[:, sl * 512:(sl + 1) * 512]
                    mm(pb[bk][:, :],
                       W1SB[:, ci * 1024 + oj * 128: ci * 1024 + (oj + 1) * 128],
                       rhs, ci == 0, ci == 7, inc=(ci == 7))
                mtk = (st, st.val)
                B.wait("vector", st, mtk[1])
                etk = B.op("vector", lambda e, oj=oj, bk=bk, HT=HT: e.tensor_copy(
                    out=HT[:, oj * 512:(oj + 1) * 512], in_=pb[bk][:, :]), sv, 1)
                bank_ticket[bk] = etk
            if k == 0:
                dbg("G0_p%d" % phase, GT[:, :])
                dbg("H0_p%d" % phase, HT[:, :])

        def emit_gf_net():
            B.wait("vector", ar_done[0], ar_done[1])
            gcn_tk = B.op("vector", lambda e: e.tensor_scalar(
                out=GCNB[:, :], in0=GCN[:, :], scalar1=1.0 / TFULL, scalar2=None,
                op0=ALU.mult), sv, 1)
            use_bank(6)
            B.wait("tensor", sv, gcn_tk[1])
            B.wait("tensor", gw_tk[phase][0], gw_tk[phase][1])
            for oj in range(4):
                for ci in range(8):
                    mm(pb[6][:, oj:oj + 1],
                       GW1SB[:, ci * 512 + oj * 128: ci * 512 + (oj + 1) * 128],
                       GCNB[:, ci:ci + 1], ci == 0, False)
                mm(pb[6][:, oj:oj + 1], BROWS[0:1, 1792 + oj * 128:1792 + (oj + 1) * 128],
                   ONE1B[0:1, 0:1], False, True, inc=True)
            g1_tk = (st, st.val)
            B.wait("scalar", st, g1_tk[1])
            e1 = B.op("scalar", lambda e: e.activation(out=GH1[:, :], in_=pb[6][:, 0:4],
                                                       func=AF.Gelu), ss, 1)
            bank_ticket[6] = e1
            use_bank(7)
            B.wait("tensor", ss, e1[1])
            for oj in range(2):
                for ci in range(4):
                    mm(pb[7][:, oj:oj + 1],
                       GW2SB[:, ci * 256 + oj * 128: ci * 256 + (oj + 1) * 128],
                       GH1[:, ci:ci + 1], ci == 0, False)
                mm(pb[7][:, oj:oj + 1], BROWS[0:1, 2304 + oj * 128:2304 + (oj + 1) * 128],
                   ONE1B[0:1, 0:1], False, True, inc=True)
            g2_tk = (st, st.val)
            B.wait("scalar", st, g2_tk[1])
            e2 = B.op("scalar", lambda e: e.activation(out=GH2[:, :], in_=pb[7][:, 0:2],
                                                       func=AF.Gelu), ss, 1)
            bank_ticket[7] = e2
            use_bank(4)
            B.wait("tensor", ss, e2[1])
            for ci in range(2):
                mm(pb[4][0:1, 0:1], GW3SB[:, ci:ci + 1], GH2[:, ci:ci + 1],
                   ci == 0, ci == 1, inc=(ci == 1))
            g3_tk = (st, st.val)
            B.wait("scalar", st, g3_tk[1])
            e3 = B.op("scalar", lambda e: e.activation(
                out=cell(C_GTH), in_=pb[4][0:1, 0:1], func=AF.Tanh, scale=0.5,
                bias=cell(C_HGB3)), ss, 1)
            bank_ticket[4] = e3
            B.wait("vector", ss, e3[1])
            B.op("vector", lambda e: e.tensor_scalar(
                out=cell(C_GF), in0=cell(C_GTH), scalar1=0.5, scalar2=0.5,
                op0=ALU.mult, op1=ALU.add))
            cz = B.op("vector", lambda e: e.tensor_scalar(
                out=cell(C_C0), in0=cell(C_GF), scalar1=cell(C_K2), scalar2=cell(C_C3),
                op0=ALU.mult, op1=ALU.add), sv, 1)
            return cz

        RS = [{"c": R_C, "A": R_A, "B": R_B}, {"c": R_T1, "A": R_T2, "B": R_T3}]

        def rowp(par, nm):
            return trow(RS[par][nm])

        last_fin = [None]

        upd_st = [[None] * 4, [None] * 4]
        CBROWP = [CBROW, CBROW2]

        def emit_cycle(k, par, cyc, cz_tk, mid=None):
            GT, HT = GPAIR[par], HPAIR[par]
            scr = SCR[par]
            bb = 0 if par == 0 else 4
            if cyc == 0:
                upd_st[par] = [None] * 4
            upd_tk = upd_st[par]
            if cyc == 0 and cz_tk is not None:
                B.wait("scalar", cz_tk[0], cz_tk[1])
            for g2 in range(4):
                c0_, c1_ = 2 * g2, 2 * g2 + 1
                if cyc == 0 and g2 == 0 and last_fin[0] is not None:
                    fh = last_fin[0][0] if par == 0 else last_fin[0][1]
                    B.wait("scalar", fh[0], fh[1])
                if upd_tk[g2] is not None:
                    B.wait("scalar", sv, upd_tk[g2][1])
                for ci in (c0_, c1_):
                    B.op("scalar", lambda e, ci=ci, GT=GT, scr=scr: e.activation(
                        out=scr["H1G"][:, ci * 512:(ci + 1) * 512],
                        in_=GT[:, ci * 512:(ci + 1) * 512], func=AF.Gelu,
                        bias=BIASC[:, 8 + ci:9 + ci]),
                        ss if ci == c1_ else None, 1)
                g1tk = (ss, ss.val)
                B.wait("tensor", ss, g1tk[1])
                if g2 == 0:
                    for bk in range(bb, bb + 4):
                        use_bank(bk)
                for ci in (c0_, c1_):
                    for oj in range(4):
                        mm(pb[bb + oj][:, :],
                           W2SB[:, ci * 512 + oj * 128: ci * 512 + (oj + 1) * 128],
                           scr["H1G"][:, ci * 512:(ci + 1) * 512],
                           ci == 0, ci == 7 and oj == 3, inc=(ci == 7 and oj == 3))
                if g2 == 1 and mid is not None:
                    mid()
            w2tk = (st, st.val)
            B.wait("scalar", st, w2tk[1])
            if cyc == 0 and par == 0 and last_fin[0] is not None:
                B.wait("scalar", last_fin[0][1][0], last_fin[0][1][1])
            for oj in range(4):
                etk = B.op("scalar", lambda e, oj=oj, bb=bb, scr=scr: e.activation(
                    out=scr["H2G"][:, oj * 512:(oj + 1) * 512], in_=pb[bb + oj][:, :],
                    func=AF.Gelu), ss, 1)
                bank_ticket[bb + oj] = etk
            h2tk = (ss, ss.val)
            use_bank(bb)
            use_bank(bb + 1)
            B.wait("tensor", ss, h2tk[1])
            for ci in range(4):
                for oj in range(2):
                    mm(pb[bb + oj][:, :],
                       W3SB[:, ci * 256 + oj * 128: ci * 256 + (oj + 1) * 128],
                       scr["H2G"][:, ci * 512:(ci + 1) * 512],
                       ci == 0, ci == 3 and oj == 1, inc=(ci == 3 and oj == 1))
            w3tk = (st, st.val)
            B.wait("scalar", st, w3tk[1])
            for oj in range(2):
                etk = B.op("scalar", lambda e, oj=oj, bb=bb, scr=scr: e.activation(
                    out=scr["H3G"][:, oj * 512:(oj + 1) * 512], in_=pb[bb + oj][:, :],
                    func=AF.Gelu), ss, 1)
                bank_ticket[bb + oj] = etk
            h3tk = (ss, ss.val)
            use_bank(bb + 2)
            B.wait("tensor", ss, h3tk[1])
            for ci in range(2):
                mm(pb[bb + 2][0:1, :], W4SB[:, ci:ci + 1],
                   scr["H3G"][:, ci * 512:(ci + 1) * 512], ci == 0, ci == 1,
                   inc=(ci == 1))
            ztk = (st, st.val)
            B.wait("scalar", st, ztk[1])
            thtk = B.op("scalar", lambda e, par=par, bb=bb: e.activation(
                out=rowp(par, "c"), in_=pb[bb + 2][0:1, :], func=AF.Tanh, scale=0.5,
                bias=cell(C_HB4)), ss, 1)
            bank_ticket[bb + 2] = thtk
            B.wait("vector", ss, thtk[1])
            if cyc == 0 and cz_tk is not None:
                B.wait("vector", cz_tk[0], cz_tk[1])
            B.op("vector", lambda e, par=par: e.tensor_scalar(
                out=rowp(par, "c"), in0=rowp(par, "c"), scalar1=cell(C_K1),
                scalar2=cell(C_C0), op0=ALU.mult, op1=ALU.add), sv, 1)
            if cyc == 0:
                B.op("vector", lambda e, par=par: e.tensor_copy(
                    out=rowp(par, "B"), in_=rowp(par, "c")))
                B.op("vector", lambda e, par=par: e.memset(rowp(par, "A"), 1.0), sv, 1)
            else:
                B.op("vector", lambda e, par=par: e.tensor_mul(
                    out=SPROW[0:1, par * 512:(par + 1) * 512], in0=rowp(par, "c"),
                    in1=rowp(par, "B")))
                B.op("vector", lambda e, par=par: e.tensor_mul(
                    out=SPROW[0:1, 1024 + par * 512:1024 + (par + 1) * 512],
                    in0=rowp(par, "c"), in1=rowp(par, "A")))
                B.op("vector", lambda e, par=par: e.tensor_sub(
                    out=rowp(par, "A"), in0=rowp(par, "A"),
                    in1=SPROW[0:1, par * 512:(par + 1) * 512]))
                B.op("vector", lambda e, par=par: e.tensor_add(
                    out=rowp(par, "B"), in0=rowp(par, "B"),
                    in1=SPROW[0:1, 1024 + par * 512:1024 + (par + 1) * 512]), sv, 1)
            if cyc >= 2:
                return None

            def emit_cbc():
                cbtk = B.op("vector", lambda e, par=par: e.tensor_copy(
                    out=CBROWP[par][:, :], in_=rowp(par, "c")), sv, 1)
                cbb = bb + 3
                use_bank(cbb)
                B.wait("tensor", sv, cbtk[1])
                bmk = mm(pb[cbb][:, :], ONESB[0:1, 0:128], CBROWP[par][:, :],
                         True, True, inc=True)
                B.wait("scalar", st, bmk[1])
                cbe = B.op("scalar", lambda e, cbb=cbb, scr=scr: e.copy(
                    out=scr["CBC"][:, :], in_=pb[cbb][:, :]), ss, 1)
                bank_ticket[cbb] = cbe
                B.wait("vector", ss, cbe[1])
                for g2 in range(4):
                    sl = slice(g2 * 1024, (g2 + 1) * 1024)
                    B.op("vector", lambda e, sl=sl, scr=scr, HT=HT: e.tensor_tensor(
                        out=r3(TMP1[:, sl], 2, 512), in0=r3(HT[:, sl], 2, 512),
                        in1=bcast3(scr["CBC"].ap(), 2, 512), op=ALU.mult), sv, 1)
                    if cyc == 0:
                        B.op("vector", lambda e, sl=sl, scr=scr, GT=GT: e.tensor_tensor(
                            out=r3(scr["H1G"][:, sl], 2, 512), in0=r3(GT[:, sl], 2, 512),
                            in1=bcast3(scr["CBC"].ap(), 2, 512), op=ALU.mult))
                        B.op("vector", lambda e, sl=sl, scr=scr, HT=HT: e.tensor_sub(
                            out=HT[:, sl], in0=HT[:, sl], in1=scr["H1G"][:, sl]))
                    gadd = B.op("vector", lambda e, sl=sl, GT=GT: e.tensor_add(
                        out=GT[:, sl], in0=GT[:, sl], in1=TMP1[:, sl]), sv, 1)
                    upd_tk[g2] = gadd

            return emit_cbc

        def emit_final_rows(k, par):
            sp = par * 1024
            if k == 0:
                dbg("mr_p%d" % phase, MROW[0:1, :])
                dbg("rr_p%d" % phase, RROW[0:1, :])
            rr_ = RROW[0:1, k * 512:(k + 1) * 512]
            mr_ = MROW[0:1, k * 512:(k + 1) * 512]
            B.op("vector", lambda e, par=par, rr_=rr_: e.scalar_tensor_tensor(
                out=rowp(par, "A"), in0=rowp(par, "A"), scalar=0.5, in1=rr_,
                op0=ALU.mult, op1=ALU.mult))
            B.op("vector", lambda e, par=par, rr_=rr_: e.scalar_tensor_tensor(
                out=rowp(par, "B"), in0=rowp(par, "B"), scalar=0.5, in1=rr_,
                op0=ALU.mult, op1=ALU.mult))
            B.op("vector", lambda e, par=par, mr_=mr_: e.tensor_mul(
                out=rowp(par, "c"), in0=rowp(par, "A"), in1=mr_))
            B.op("vector", lambda e, par=par, mr_=mr_, sp=sp: e.tensor_mul(
                out=SPROW[0:1, sp:sp + 512], in0=rowp(par, "B"), in1=mr_))
            B.op("vector", lambda e, par=par, sp=sp: e.tensor_add(
                out=SPROW[0:1, sp + 512:sp + 1024], in0=rowp(par, "c"),
                in1=SPROW[0:1, sp:sp + 512]))
            B.op("vector", lambda e, par=par, sp=sp: e.tensor_sub(
                out=rowp(par, "c"), in0=rowp(par, "c"), in1=SPROW[0:1, sp:sp + 512]))
            rows_tk = B.op("vector", lambda e, par=par: e.tensor_scalar(
                out=rowp(par, "A"), in0=rowp(par, "A"), scalar1=1.0, scalar2=None,
                op0=ALU.add), sv, 1)
            return rows_tk

        def emit_final_apply(k, par, rows_tk):
            xo = xo_of(k)
            sp = par * 1024
            bA, bB, bP, bM = (6, 7, 4, 5) if par == 0 else (2, 3, 0, 1)
            for bknum in (bA, bB, bP, bM):
                use_bank(bknum)
            B.wait("tensor", sv, rows_tk[1])
            mm(pb[bA][:, :], ONESF[:, :], rowp(par, "A"), True, True, inc=True)
            mm(pb[bB][:, :], ONESF[:, :], rowp(par, "B"), True, True, inc=True)
            mm(pb[bP][:, :], ONESF[:, :], SPROW[0:1, sp + 512:sp + 1024], True, True, inc=True)
            fb = mm(pb[bM][:, :], ONESF[:, :], rowp(par, "c"), True, True, inc=True)
            B.wait("vector", st, fb[1])
            B.op("vector", lambda e, xo=xo, bB=bB: e.tensor_tensor(
                out=r3(QFIN[:, 0:2048], 4, 512), in0=r3(XR[:, xo + 2048:xo + 4096], 4, 512),
                in1=bcast3(pb[bB].ap(), 4, 512), op=ALU.mult))
            B.op("vector", lambda e, xo=xo, bB=bB: e.tensor_tensor(
                out=r3(QFIN[:, 2048:4096], 4, 512), in0=r3(XR[:, xo:xo + 2048], 4, 512),
                in1=bcast3(pb[bB].ap(), 4, 512), op=ALU.mult), sv, 1)
            B.op("vector", lambda e, xo=xo, bA=bA: e.tensor_tensor(
                out=r3(XR[:, xo:xo + 4096], 8, 512), in0=r3(XR[:, xo:xo + 4096], 8, 512),
                in1=bcast3(pb[bA].ap(), 8, 512), op=ALU.mult), sv, 1)
            qhead = None
            for rq in range(4):
                opx = ALU.add if SGN[2 * rq] > 0 else ALU.subtract
                B.op("vector", lambda e, rq=rq, opx=opx, xo=xo: e.tensor_tensor(
                    out=XR[:, xo + rq * 1024: xo + (rq + 1) * 1024],
                    in0=XR[:, xo + rq * 1024: xo + (rq + 1) * 1024],
                    in1=QFIN[:, rq * 1024:(rq + 1) * 1024], op=opx), sv, 1)
                if rq == 1:
                    qhead = (sv, sv.val)
            qfull = (sv, sv.val)
            last_fin[0] = (qhead, qfull)
            for rq in range(4):
                rbank = pb[bP] if SGN[2 * rq] > 0 else pb[bM]
                B.op("vector", lambda e, rq=rq, rbank=rbank, xo=xo: e.tensor_sub(
                    out=XR[:, xo + rq * 1024: xo + (rq + 1) * 1024],
                    in0=XR[:, xo + rq * 1024: xo + (rq + 1) * 1024],
                    in1=bcast3(rbank.ap(), 2, 512)), sv, 1)
            fin_tk = (sv, sv.val)
            for bknum in (bA, bB, bP, bM):
                bank_ticket[bknum] = fin_tk
            if phase == 1:
                B.wait("sync", sv, fin_tk[1])
                B.op("sync", lambda e, xo=xo: e.dma_start(
                    out=AP(out_d.tensor, xo, [[NCH * T, 128], [1, 4096]]),
                    in_=XR[:, xo:xo + 4096]), so, 16)

        o0a = emit_prep(0, 0)
        emit_g0h0(0, 0, *o0a)
        o0b = emit_prep(1, 1)
        emit_g0h0(1, 1, *o0b)
        cz_tk = emit_gf_net()
        for kp in (0, 2):
            pend = None
            for cyc in range(3):
                pend = emit_cycle(kp, 0, cyc, cz_tk, mid=pend)
                if cyc == 2 and kp == 0:
                    o2 = emit_prep(2, 0)
                pend = emit_cycle(kp + 1, 1, cyc, cz_tk, mid=pend)
            if kp == 0:
                emit_g0h0(2, 0, *o2)
                o3 = emit_prep(3, 1)
                emit_g0h0(3, 1, *o3)
            ra = emit_final_rows(kp, 0)
            rb = emit_final_rows(kp + 1, 1)
            emit_final_apply(kp, 0, ra)
            emit_final_apply(kp + 1, 1, rb)

    def emit_attn(wqkv_tk):
        B.op("vector", lambda e: e.memset(VEXT[:, :], 0.0))
        for h in range(H):
            B.op("vector", lambda e, h=h: e.memset(VEXT[:, h * 66 + 64:h * 66 + 65], 1.0))
        B.op("vector", lambda e: e.memset(KSF[:, :], 0.0), sv, 1)
        vext_tk = (sv, sv.val)
        ksf_zero_tk = vext_tk

        kv_first = [True]
        XNP = [XNBF, XNBF2]
        PHP = [(PHA, PHB), (PHA2, PHB2)]
        xn_tks = [None] * NTILES
        ph_war = [0, 0]
        ln_st = {}

        def ln_a(k, half):
            xo = xo_of(k)
            XN = XNP[k % 2]
            B.wait("vector", st, st.val)
            cp = B.op("vector", lambda e, xo=xo, XN=XN, half=half: e.tensor_copy(
                out=XN[:, half * 2048:(half + 1) * 2048],
                in_=XR[:, xo + half * 2048: xo + (half + 1) * 2048]), sv, 1)
            if half == 0:
                ln_st[k] = {}
            ln_st[k]["cp%d" % half] = cp

        def ln_b(k):
            XN = XNP[k % 2]
            stt = ln_st[k]
            use_bank(6)
            B.wait("tensor", sv, stt["cp1"][1])
            for ci in range(8):
                mm(pb[6][0:1, :], ONECB[:, :], XN[:, ci * 512:(ci + 1) * 512],
                   ci == 0, ci == 7, inc=(ci == 7))
            stt["s1"] = (st, st.val)
            B.wait("scalar", sv, stt["cp1"][1])
            B.wait("scalar", st, stt["s1"][1])
            stt["sq"] = B.op("scalar", lambda e, XN=XN: e.activation(
                out=OUTPRE[:, :], in_=XN[:, :], func=AF.Square), ss, 1)

        def ln_c(k):
            stt = ln_st[k]
            use_bank(7)
            B.wait("tensor", ss, stt["sq"][1])
            for ci in range(8):
                mm(pb[7][0:1, :], ONECB[:, :], OUTPRE[:, ci * 512:(ci + 1) * 512],
                   ci == 0, ci == 7, inc=(ci == 7))
            s2 = (st, st.val)
            B.op("scalar", lambda e: e.mul(out=trow(R_T1), in_=pb[6][0:1, :], mul=1.0 / C), ss, 1)
            msq = B.op("scalar", lambda e: e.activation(
                out=trow(R_T3), in_=pb[6][0:1, :], func=AF.Square, scale=1.0 / C), ss, 1)
            bank_ticket[6] = msq
            B.wait("vector", ss, msq[1])
            B.wait("vector", st, s2[1])
            vv = B.op("vector", lambda e: e.scalar_tensor_tensor(
                out=trow(R_C), in0=pb[7][0:1, :], scalar=1.0 / C, in1=trow(R_T3),
                op0=ALU.mult, op1=ALU.subtract), sv, 1)
            bank_ticket[7] = vv
            B.wait("scalar", sv, vv[1])
            B.op("scalar", lambda e: e.activation(
                out=trow(R_C), in_=trow(R_C), func=AF.Ln, bias=cell(C_EPS)), ss, 1)
            rtk = B.op("scalar", lambda e: e.activation(
                out=trow(R_T2), in_=trow(R_C), func=AF.Exp, scale=-0.5), ss, 1)
            use_bank(6)
            use_bank(7)
            B.wait("tensor", ss, rtk[1])
            mm(pb[6][:, :], ONESF[:, :], trow(R_T1), True, True, inc=True)
            bc = mm(pb[7][:, :], ONESF[:, :], trow(R_T2), True, True, inc=True)
            stt["bc"] = (st, st.val)

        def ln_d(k):
            xo = xo_of(k)
            XN = XNP[k % 2]
            stt = ln_st[k]
            B.wait("vector", st, stt["bc"][1])
            B.op("vector", lambda e, xo=xo, XN=XN: e.tensor_sub(
                out=r3(XN[:, :], 8, 512), in0=r3(XR[:, xo:xo + 4096], 8, 512),
                in1=bcast3(pb[6].ap(), 8, 512)), sv, 1)
            xn = B.op("vector", lambda e, XN=XN: e.tensor_tensor(
                out=r3(XN[:, :], 8, 512), in0=r3(XN[:, :], 8, 512),
                in1=bcast3(pb[7].ap(), 8, 512), op=ALU.mult), sv, 1)
            bank_ticket[6] = xn
            bank_ticket[7] = xn
            return xn

        ln_a(0, 0)
        ln_a(0, 1)
        ln_b(0)
        ln_c(0)
        xn_tks[0] = ln_d(0)
        def emit_q(k):
            XN = XNP[k % 2]
            B.wait("tensor", sv, xn_tks[k][1])
            for oj in range(8):
                bk = oj % 4
                use_bank(bk)
                for ci in range(8):
                    mm(pb[bk][:, :],
                       WQKV[:, ci * 3072 + oj * 128: ci * 3072 + (oj + 1) * 128],
                       XN[:, ci * 512:(ci + 1) * 512], ci == 0, ci == 7, inc=(ci == 7))
                mtk = (st, st.val)
                pa, pbb = PHP[oj % 2]
                B.wait("scalar", st, mtk[1])
                B.wait("scalar", sv, ph_war[oj % 2])
                B.op("scalar", lambda e, bk=bk, pa=pa: e.activation(
                    out=pa[:, :], in_=pb[bk][:, :], func=AF.Relu, scale=-1.0), ss, 1)
                e_tk = B.op("scalar", lambda e, pa=pa, pbb=pbb: e.activation(
                    out=pbb[:, :], in_=pa[:, :], func=AF.Exp, scale=-1.0), ss, 1)
                B.wait("vector", st, mtk[1])
                B.wait("vector", ss, e_tk[1])
                r1 = B.op("vector", lambda e, bk=bk, pa=pa: e.tensor_scalar(
                    out=pa[:, :], in0=pb[bk][:, :], scalar1=0.0, scalar2=None,
                    op0=ALU.max), sv, 1)
                bank_ticket[bk] = r1
                ad = B.op("vector", lambda e, oj=oj, k=k, pa=pa, pbb=pbb: e.tensor_add(
                    out=QPHI[:, k * 4096 + oj * 512: k * 4096 + (oj + 1) * 512],
                    in0=pbb[:, :], in1=pa[:, :]), sv, 1)
                ph_war[oj % 2] = ad[1]
            if k == 0:
                dbg("qphi_a", QPHI[:, 0:4096])

        for k in range(NTILES):
            XN = XNP[k % 2]
            if k == 0:
                B.wait("tensor", wqkv_tk[0], wqkv_tk[1])
                B.wait("tensor", sv, xn_tks[0][1])
            for sub in range(4):
                for bk in range(4):
                    use_bank(bk)
                if kv_first[0]:
                    use_bank(4)
                    use_bank(5)
                for sl in range(4):
                    eoff = 1024 + sl * 512
                    for ci in range(8):
                        mm(pb[sl][:, :],
                           XN[:, ci * 512 + sub * 128: ci * 512 + (sub + 1) * 128],
                           WQKV[:, ci * 3072 + eoff: ci * 3072 + eoff + 512],
                           ci == 0, ci == 7, inc=(ci == 7))
                kv_tk = (st, st.val)
                for sl in range(2):
                    pa, pbb = PHP[sl]
                    B.wait("scalar", st, kv_tk[1] - 3 + sl)
                    B.wait("scalar", sv, ph_war[sl])
                    B.op("scalar", lambda e, sl=sl, pa=pa: e.activation(
                        out=pa[:, :], in_=pb[sl][:, :], func=AF.Relu, scale=-1.0), ss, 1)
                    e_tk = B.op("scalar", lambda e, pa=pa, pbb=pbb: e.activation(
                        out=pbb[:, :], in_=pa[:, :], func=AF.Exp, scale=-1.0), ss, 1)
                    B.wait("vector", st, kv_tk[1] - 3 + sl)
                    B.wait("vector", ss, e_tk[1])
                    r1 = B.op("vector", lambda e, sl=sl, pa=pa: e.tensor_scalar(
                        out=pa[:, :], in0=pb[sl][:, :], scalar1=0.0, scalar2=None,
                        op0=ALU.max), sv, 1)
                    bank_ticket[sl] = r1
                    ad = B.op("vector", lambda e, sl=sl, pa=pa, pbb=pbb: e.tensor_add(
                        out=KTM[:, sl * 512:(sl + 1) * 512], in0=pbb[:, :], in1=pa[:, :]), sv, 1)
                    ph_war[sl] = ad[1]
                for sl in range(2):
                    B.wait("vector", st, kv_tk[1] - 1 + sl)
                    vtk2 = B.op("vector", lambda e, sl=sl: e.tensor_copy(
                        out=AP(VEXT, sl * 8 * 66, [[1056, 128], [66, 8], [1, 64]]),
                        in_=r3(pb[2 + sl][:, :], 8, 64)), sv, 1)
                    bank_ticket[2 + sl] = vtk2
                kv_ready = (sv, sv.val)
                if k + 1 < NTILES:
                    if sub == 0:
                        ln_a(k + 1, 0)
                    elif sub == 1:
                        ln_a(k + 1, 1)
                    elif sub == 2:
                        ln_b(k + 1)
                    else:
                        ln_c(k + 1)
                        xn_tks[k + 1] = ln_d(k + 1)
                B.wait("tensor", sv, kv_ready[1])
                first = kv_first[0]
                kv_first[0] = False
                last = (k == NTILES - 1 and sub == 3)
                for h in range(H):
                    p2 = h // 2
                    bk = 4 + p2 // 4
                    cb = (p2 % 4) * 65
                    pbase = (h % 2) * 64
                    mm(pb[bk][pbase:pbase + 64, cb:cb + 65],
                       KTM[:, h * 64:(h + 1) * 64],
                       VEXT[:, h * 66:h * 66 + 65],
                       first and (h % 8 == 0), last, inc=(h == H - 1))
            if k < NTILES - 1:
                emit_q(k)
        kvdone_tk = (st, st.val)

        B.wait("sync", st, kvdone_tk[1])
        B.op("sync", lambda e: e.dma_start(out=EALL[:, :], in_=eall_d), sd, 16)
        eall_tk = (sd, sd.val)
        B.wait("vector", st, kvdone_tk[1])
        for b in range(2):
            B.op("vector", lambda e, b=b: e.tensor_copy(
                out=KVSB[:, b * 260:(b + 1) * 260], in_=pb[4 + b][:, 0:260]), sv, 1)
        kvsb_tk = (sv, sv.val)
        bank_ticket[4] = kvsb_tk
        bank_ticket[5] = kvsb_tk
        B.wait("sync", sv, kvsb_tk[1])
        B.op("sync", lambda e: e.dma_start(out=cck_in.ap(), in_=KVSB[:, :]), sd, 16)
        B.wait("gpsimd", sd, sd.val)
        B.op("gpsimd", lambda e: e.collective_compute(
            "AllReduce", ALU.add, replica_groups=[list(range(NCORES))],
            ins=[cck_in.ap()], outs=[cck_out.ap()]), scc, 1)
        kv_ar = (scc, scc.val)
        B.wait("gpsimd", st, kvdone_tk[1])
        B.op("gpsimd", lambda e: e.dma_start(out=WPROJ[:, :], in_=wproj_d), sg, 16)
        wproj_tk = (sg, sg.val)
        B.wait("gpsimd", kv_ar[0], kv_ar[1])
        B.wait("gpsimd", ksf_zero_tk[0], ksf_zero_tk[1])
        B.op("gpsimd", lambda e: e.dma_start(
            out=AP(KSF, 0, [[128, 64], [18, 8], [1, 1]]),
            in_=AP(cck_out, 64, [[520, 64], [65, 8], [1, 1]]),
            allow_slow_non_contiguous=True), sg, 16)
        B.op("gpsimd", lambda e: e.dma_start(
            out=AP(KSF, 64 * 128 + 1, [[128, 64], [18, 8], [1, 1]]),
            in_=AP(cck_out, 64 * 520 + 64, [[520, 64], [65, 8], [1, 1]]),
            allow_slow_non_contiguous=True), sg, 16)
        ksf_tk = (sg, sg.val)
        B.op("gpsimd", lambda e: e.dma_start(out=KVBF[:, :], in_=cck_out.ap()), sg, 16)
        kvpost_tk = (sg, sg.val)
        dbg("kvsb_a", KVSB[:, :])
        emit_q(NTILES - 1)

        B.wait("tensor", ksf_tk[0], ksf_tk[1])
        B.wait("tensor", eall_tk[0], eall_tk[1])
        kv_waited = [False]
        for k in range(NTILES):
            xo = xo_of(k)
            use_bank(6)
            for ci in range(8):
                mm(pb[6][0:16, :], KSF[:, ci * 16:(ci + 1) * 16],
                   QPHI[:, k * 4096 + ci * 512: k * 4096 + (ci + 1) * 512],
                   ci == 0, ci == 7, inc=(ci == 7))
            n_tk = (st, st.val)
            B.wait("vector", st, n_tk[1])
            nv = B.op("vector", lambda e: e.tensor_scalar(
                out=NRS[:, :], in0=pb[6][0:16, :], scalar1=EPS_AT, scalar2=None,
                op0=ALU.add), sv, 1)
            bank_ticket[6] = nv
            B.wait("scalar", sv, nv[1])
            B.op("scalar", lambda e: e.activation(
                out=REC16[:, :], in_=NRS[:, :], func=AF.Ln), ss, 1)
            rec_tk = B.op("scalar", lambda e: e.activation(
                out=REC16[:, :], in_=REC16[:, :], func=AF.Exp, scale=-1.0), ss, 1)
            B.wait("tensor", ss, rec_tk[1])
            if not kv_waited[0]:
                kv_waited[0] = True
                B.wait("tensor", kvpost_tk[0], kvpost_tk[1])
                B.wait("tensor", wproj_tk[0], wproj_tk[1])
            for j in range(8):
                bko = j % 2
                bkr = 2 + (j % 2)
                use_bank(bko)
                use_bank(bkr)
                mm(pb[bko][0:64, :], KVBF[0:64, j * 65:j * 65 + 64],
                   QPHI[0:64, k * 4096 + j * 512: k * 4096 + (j + 1) * 512], True, True)
                mm(pb[bko][64:128, :], KVBF[64:128, j * 65:j * 65 + 64],
                   QPHI[64:128, k * 4096 + j * 512: k * 4096 + (j + 1) * 512],
                   True, True, inc=True)
                ro = mm(pb[bkr][:, :], EALL[:, j * 128:(j + 1) * 128], REC16[:, :],
                        True, True, inc=True)
                rb = PHA if (j % 2 == 0) else PHB
                B.wait("scalar", st, ro[1])
                B.wait("scalar", sv, sv.val)
                rc = B.op("scalar", lambda e, bkr=bkr, rb=rb: e.copy(
                    out=rb[:, :], in_=pb[bkr][:, :]), ss, 1)
                bank_ticket[bkr] = rc
                B.wait("vector", st, ro[1])
                B.wait("vector", ss, rc[1])
                dv = B.op("vector", lambda e, j=j, bko=bko, rb=rb: e.tensor_mul(
                    out=OUTPRE[:, j * 512:(j + 1) * 512], in0=pb[bko][:, :],
                    in1=rb[:, :]), sv, 1)
                bank_ticket[bko] = dv
            op_tk = (sv, sv.val)
            if k == 0:
                dbg("outpre_a", OUTPRE[:, :])
            B.wait("tensor", sv, op_tk[1])
            for oj in range(8):
                bk = 5 + (oj % 2)
                use_bank(bk)
                for ci in range(8):
                    mm(pb[bk][:, :],
                       WPROJ[:, ci * 1024 + oj * 128: ci * 1024 + (oj + 1) * 128],
                       OUTPRE[:, ci * 512:(ci + 1) * 512], ci == 0, ci == 7, inc=(ci == 7))
                mtk = (st, st.val)
                B.wait("vector", st, mtk[1])
                ev = B.op("vector", lambda e, oj=oj, bk=bk, xo=xo: e.scalar_tensor_tensor(
                    out=XR[:, xo + oj * 512: xo + (oj + 1) * 512], in0=pb[bk][:, :],
                    scalar=BIASC[:, oj:oj + 1],
                    in1=XR[:, xo + oj * 512: xo + (oj + 1) * 512],
                    op0=ALU.add, op1=ALU.add), sv, 1)
                bank_ticket[bk] = ev

    B.wait("vector", sd, x_tk[1])
    B.wait("vector", sv, cells_tk[1])
    B.wait("tensor", sg, wffn_tk[1])
    emit_ffn(0)
    if stage == 1:
        emit_store_xr()
        return end_block()

    B.wait("gpsimd", st, st.val)
    B.wait("gpsimd", sv, sv.val)
    B.op("gpsimd", lambda e: e.dma_start(out=WQKV[:, :], in_=wqkv_d), sg, 16)
    wqkv_tk = (sg, sg.val)
    for eng in Builder.ENGS:
        for sem in (st, sv, ss):
            B.wait(eng, sem)
    emit_attn(wqkv_tk)
    if stage == 2:
        emit_store_xr()
        return end_block()

    B.wait("gpsimd", st, st.val)
    B.wait("gpsimd", sv, sv.val)
    B.op("gpsimd", lambda e: e.dma_start(out=W1SB[:, :], in_=w1_d), sg, 16)
    w1_tk2 = (sg, sg.val)
    B.op("gpsimd", lambda e: e.dma_start(out=W2SB[:, :], in_=w2_d), sg, 16)
    B.op("gpsimd", lambda e: e.dma_start(out=W3SB[:, :], in_=w3_d), sg, 16)
    B.op("gpsimd", lambda e: e.dma_start(out=W4SB[:, :], in_=w4_d), sg, 16)
    B.op("gpsimd", lambda e: e.dma_start(out=BROWS[:, :], in_=brows_d), sg, 16)
    B.op("gpsimd", lambda e: e.dma_start(out=GW1SB[:, :], in_=gw1_d), sg, 16)
    B.op("gpsimd", lambda e: e.dma_start(out=GW2SB[:, :], in_=gw2_d), sg, 16)
    B.op("gpsimd", lambda e: e.dma_start(out=GW3SB[:, :], in_=gw3_d), sg, 16)
    gw_tk[1] = (sg, sg.val)
    for eng in Builder.ENGS:
        for sem in (st, sv, ss):
            B.wait(eng, sem)
    emit_ffn(1, w1_tk2=w1_tk2)
    return end_block()


def _images(inputs):
    f32 = np.float32
    x = np.asarray(inputs["L0_attn_x"], f32).reshape(TFULL, C)
    Wqkv = np.asarray(inputs["L0_attn_Wqkv"], f32)
    Wproj = np.asarray(inputs["L0_attn_Wproj"], f32)
    bproj = np.asarray(inputs["L0_attn_bproj"], f32)
    w = {i: np.asarray(inputs["L0_ffn_cn_w%d" % i], f32) for i in (1, 2, 3, 4)}
    b = {i: np.asarray(inputs["L0_ffn_cn_b%d" % i], f32) for i in (1, 2, 3, 4)}
    gw = {i: np.asarray(inputs["L0_ffn_gc_w%d" % i], f32) for i in (1, 2, 3)}
    gb = {i: np.asarray(inputs["L0_ffn_gc_b%d" % i], f32) for i in (1, 2, 3)}
    ar = np.asarray(inputs["L0_ffn_adaptive_range"], f32)

    def img(wm, kch):
        K, O = wm.shape
        return np.ascontiguousarray(
            wm.reshape(kch, 128, O).transpose(1, 0, 2).reshape(128, kch * O))

    com = {
        "w1": img(w[1], 8), "w2": img(w[2], 8), "w3": img(w[3], 4),
        "w4": np.ascontiguousarray(w[4].reshape(2, 128).T),
        "gw1": img(gw[1], 8), "gw2": img(gw[2], 4),
        "gw3": np.ascontiguousarray(gw[3].reshape(2, 128).T),
        "wqkv": img(np.ascontiguousarray(Wqkv.T), 8),
        "wproj": img(np.ascontiguousarray(Wproj.T), 8),
    }
    biasc = np.zeros((128, 16), f32)
    biasc[:, 0:8] = bproj.reshape(8, 128).T
    biasc[:, 8:16] = b[1].reshape(8, 128).T
    com["biasc"] = biasc
    brows = np.zeros((1, 2560), f32)
    brows[0, 0:1024] = b[1]
    brows[0, 1024:1536] = b[2]
    brows[0, 1536:1792] = b[3]
    brows[0, 1792:2304] = gb[1]
    brows[0, 2304:2560] = gb[2]
    com["brows"] = brows
    com["scal"] = np.array([[float(ar[0]), float(b[4][0]), float(gb[3][0]), 0.0]], f32)
    eall = np.zeros((16, 1024), f32)
    for j in range(8):
        eall[2 * j, j * 128: j * 128 + 64] = 1.0
        eall[2 * j + 1, j * 128 + 64: (j + 1) * 128] = 1.0
    com["eall"] = eall

    per_core = []
    for i in range(NCORES):
        sl = x[i * T:(i + 1) * T]
        xi = np.ascontiguousarray(
            sl.reshape(NTILES, TT, NCH, 128).transpose(3, 0, 2, 1).reshape(128, NCH * T))
        m = dict(com)
        m["x"] = xi
        per_core.append(m)
    return per_core


def kernel(**inputs):
    nc = build_program()
    in_maps = _images(inputs)
    res = run_bass_kernel_spmd(nc, in_maps, core_ids=list(range(NCORES)))
    out = np.empty((TFULL, C), np.float32)
    for i in range(NCORES):
        oi = np.asarray(res.results[i]["out"])
        out[i * T:(i + 1) * T] = (
            oi.reshape(128, NTILES, NCH, TT).transpose(1, 3, 2, 0).reshape(T, C))
    return out.reshape(1, TFULL, C)
